# version 03
# baseline (speedup 1.0000x reference)
import contextlib
import math

import numpy as np

import concourse.bass as bass
import concourse.mybir as mybir
from concourse.bass_utils import run_bass_kernel_spmd

F32 = mybir.dt.float32
BF16 = mybir.dt.bfloat16
AF = mybir.ActivationFunctionType
ALU = mybir.AluOpType
AX = mybir.AxisListType

NEG_BIG = -3.0e38


ENGS = ("PE", "ACT", "DVE", "POOL", "SP")
_BLOCK_ATTR = {"PE": "tensor", "ACT": "scalar", "DVE": "vector",
               "POOL": "gpsimd", "SP": "sync"}


class Buf:
    __slots__ = ("name", "t", "last_w", "readers", "prow", "pcol", "ncol",
                 "space", "init_deps")

    def __init__(self, name, t, space, prow=0, pcol=0, ncol=None):
        self.name = name
        self.t = t
        self.space = space
        self.prow = prow
        self.pcol = pcol
        self.ncol = ncol
        self.last_w = {}
        self.readers = {}
        self.init_deps = {}

    def __getitem__(self, idx):
        if not isinstance(idx, tuple):
            idx = (idx, slice(None))
        pidx, cidx = idx
        p0, p1 = pidx.start or 0, pidx.stop if pidx.stop is not None else 128
        c0 = cidx.start or 0
        c1 = cidx.stop if cidx.stop is not None else self.ncol
        return self.t[self.prow + p0:self.prow + p1,
                      self.pcol + c0:self.pcol + c1]

    @property
    def ap(self):
        return self[:, :]


class Arena:

    def __init__(self, t, ncols, name):
        self.t = t
        self.ncols = ncols
        self.name = name
        self.free_at = 0
        self.live = []
        self.dead = []
        self.scopes = []

    def push(self):
        self.scopes.append(self.free_at)

    def pop(self):
        base = self.scopes.pop()
        still = []
        for (s, e, b) in self.live:
            if s >= base:
                deps = dict(b.last_w)
                _dmerge(deps, b.readers)
                self.dead.append((s, e, deps))
            else:
                still.append((s, e, b))
        self.live = still
        self.free_at = base

    def alloc(self, name, ncol, align=1):
        start = self.free_at
        if align > 1:
            start = (start + align - 1) // align * align
        end = start + ncol
        if end > self.ncols:
            raise MemoryError(
                f"arena {self.name} overflow: need {ncol} at {start}, "
                f"cap {self.ncols}")
        self.free_at = end
        b = Buf(name, self.t, "sbuf", prow=0, pcol=start, ncol=ncol)
        deps = {}
        for (s, e, d) in self.dead:
            if s < end and e > start:
                _dmerge(deps, d)
        b.init_deps = deps
        self.live.append((start, end, b))
        return b


def _dmerge(dst, src_):
    for s, c in src_.items():
        if dst.get(s, -1) < c:
            dst[s] = c


class Prog:
    def __init__(self, nc):
        self.nc = nc
        self.es = contextlib.ExitStack()
        self.cnt = {}
        self.know = {e: {} for e in ENGS}
        self.sems = {}
        self.q = {e: [] for e in ENGS}
        self.ninst = {e: 0 for e in ENGS}
        for e in ENGS:
            self._mk_sem("s" + e)
        self.NDSEM = 8
        for qn in ("qSP", "qPOOL", "qACT"):
            for i in range(self.NDSEM):
                self._mk_sem(qn + str(i))
        self._mk_sem("cc")
        self.dma_queue_eng = {"qSP": "SP", "qPOOL": "POOL", "qACT": "ACT"}
        self.dma_rr = {"qSP": 0, "qPOOL": 0, "qACT": 0}
        self.dma_inflight = {}
        self.dram_tensors = {}

    def _mk_sem(self, name):
        self.sems[name] = self.es.enter_context(self.nc.semaphore(name))
        self.cnt[name] = 0

    def sbuf(self, name, shape, dtype=F32):
        t = self.es.enter_context(self.nc.sbuf_tensor(name, list(shape), dtype))
        return Buf(name, t, "sbuf", ncol=shape[1])

    def arena(self, name, ncols, dtype=F32):
        t = self.es.enter_context(self.nc.sbuf_tensor(name, [128, ncols], dtype))
        return Arena(t, ncols, name)

    def psum(self, name):
        t = self.es.enter_context(self.nc.psum_tensor(name, [128, 512], F32))
        return Buf(name, t, "psum", ncol=512)

    def din(self, name, shape, dtype=F32):
        t = self.nc.dram_tensor(name, list(shape), dtype, kind="ExternalInput")
        b = Buf(name, t, "dram", ncol=shape[-1])
        self.dram_tensors[name] = b
        return b

    def dout(self, name, shape, dtype=F32):
        t = self.nc.dram_tensor(name, list(shape), dtype, kind="ExternalOutput")
        b = Buf(name, t, "dram", ncol=shape[-1])
        self.dram_tensors[name] = b
        return b

    def dram(self, name, shape, dtype=F32):
        t = self.nc.dram_tensor(name, list(shape), dtype)
        b = Buf(name, t, "dram", ncol=shape[-1])
        self.dram_tensors[name] = b
        return b

    def _waits_for(self, eng, reads, writes):
        need = {}
        for b in reads:
            _dmerge(need, b.last_w)
            if b.space == "psum":
                _dmerge(need, b.readers)
        for b in writes:
            _dmerge(need, b.last_w)
            _dmerge(need, b.readers)
            _dmerge(need, b.init_deps)
        out = []
        for s, c in need.items():
            if self.know[eng].get(s, 0) < c:
                out.append((s, c))
                self.know[eng][s] = c
        return out

    def _post(self, sem, reads, writes):
        n = self.cnt[sem]
        for b in writes:
            _dmerge(b.last_w, {sem: n})
        for b in reads:
            _dmerge(b.readers, {sem: n})

    def op(self, eng, fn, reads=(), writes=(), acc=False):
        reads = list(reads)
        writes = list(writes)
        if acc:
            waits = self._waits_for(eng, reads, [])
            extra = {}
            for b in writes:
                _dmerge(extra, b.readers)
                _dmerge(extra, b.init_deps)
            for s, c in extra.items():
                if self.know[eng].get(s, 0) < c:
                    waits.append((s, c))
                    self.know[eng][s] = c
        else:
            waits = self._waits_for(eng, reads, writes)
        sem = "s" + eng
        self.cnt[sem] += 1
        n = self.cnt[sem]
        semh = self.sems
        wl = list(waits)

        def emit(engobj, fn=fn, wl=wl, sem=sem, n=n):
            for (s, c) in wl:
                engobj.wait_ge(semh[s], c)
            ins = fn(engobj)
            ins.then_inc(semh[sem], 1)
        self.q[eng].append(emit)
        self.ninst[eng] += 1 + len(wl)
        self._post(sem, reads, writes)
        return n

    def dma(self, queue, out, out_ap, in_, in_ap, **kw):
        eng = self.dma_queue_eng[queue]
        i = self.dma_rr[queue]
        self.dma_rr[queue] = (i + 1) % self.NDSEM
        sname = queue + str(i)
        waits = self._waits_for(eng, [in_], [out])
        prev = self.dma_inflight.get(sname)
        if prev is not None and self.know[eng].get(sname, 0) < prev:
            waits.append((sname, prev))
            self.know[eng][sname] = prev
        self.cnt[sname] += 16
        n = self.cnt[sname]
        self.dma_inflight[sname] = n
        semh = self.sems

        def emit(engobj, out_ap=out_ap, in_ap=in_ap, wl=list(waits), kw=kw):
            for (s, c) in wl:
                engobj.wait_ge(semh[s], c)
            engobj.dma_start(out=out_ap, in_=in_ap, **kw).then_inc(
                semh[sname], 16)
        self.q[eng].append(emit)
        self.ninst[eng] += 1 + len(waits)
        _dmerge(out.last_w, {sname: n})
        _dmerge(in_.readers, {sname: n})
        return (sname, n)

    def wait_all(self, eng, sems=None):
        wl = []
        for s in (sems or list(self.cnt)):
            c = self.cnt[s]
            if c > 0 and self.know[eng].get(s, 0) < c:
                wl.append((s, c))
                self.know[eng][s] = c
        semh = self.sems

        def emit(engobj, wl=wl):
            for (s, c) in wl:
                engobj.wait_ge(semh[s], c)
        self.q[eng].append(emit)
        self.ninst[eng] += len(wl)

    def mm(self, psum, pap, lhsT, lap, rhs, rap, start, stop, extra_reads=()):
        return self.op(
            "PE", lambda t: t.matmul(pap, lap, rap, start=start, stop=stop),
            reads=[lhsT, rhs, *extra_reads], writes=[psum], acc=not start)

    def transpose(self, psum, pap, src, sap, ident):
        return self.op(
            "PE", lambda t: t.transpose(pap, sap, ident.ap),
            reads=[src, ident], writes=[psum])

    def copy(self, eng, dst, dap, src, sap):
        if eng == "ACT":
            return self.op(eng, lambda e: e.copy(dap, sap),
                           reads=[src], writes=[dst])
        return self.op(eng, lambda e: e.tensor_copy(dap, sap),
                       reads=[src], writes=[dst])

    def memset(self, eng, dst, dap, val=0.0):
        return self.op(eng, lambda e: e.memset(dap, val), writes=[dst])

    def tt(self, eng, aop, dst, dap, a, aap, b, bap):
        return self.op(eng, lambda e: e.tensor_tensor(dap, aap, bap, aop),
                       reads=[a, b], writes=[dst])

    def ts(self, eng, dst, dap, a, aap, s1, s2=None, op0=ALU.mult,
           op1=ALU.bypass, sbufs=()):
        reads = [a]
        v1, v2 = s1, s2
        if isinstance(s1, tuple):
            reads.append(s1[0]); v1 = s1[1]
        if isinstance(s2, tuple):
            reads.append(s2[0]); v2 = s2[1]
        return self.op(eng, lambda e: e.tensor_scalar(dap, aap, v1, v2, op0, op1),
                       reads=reads, writes=[dst])

    def stt(self, eng, dst, dap, a, aap, scal, b, bap, op0, op1):
        reads = [a, b]
        v = scal
        if isinstance(scal, tuple):
            reads.append(scal[0]); v = scal[1]
        return self.op(
            eng, lambda e: e.scalar_tensor_tensor(dap, aap, v, bap, op0, op1),
            reads=reads, writes=[dst])

    def act(self, dst, dap, src, sap, func, bias=0.0, scale=1.0, accum=None):
        reads = [src]
        writes = [dst]
        bv, sv = bias, scale
        if isinstance(bias, tuple):
            reads.append(bias[0]); bv = bias[1]
        if isinstance(scale, tuple):
            reads.append(scale[0]); sv = scale[1]
        aap = None
        if accum is not None:
            writes.append(accum[0]); aap = accum[1]
        return self.op(
            "ACT",
            lambda e: e.activation(dap, sap, func, bias=bv, scale=sv,
                                   accum_out=aap),
            reads=reads, writes=writes)

    def reduce(self, eng, dst, dap, src, sap, op=ALU.max, negate=False):
        return self.op(
            eng, lambda e: e.tensor_reduce(dap, sap, AX.X, op, negate=negate),
            reads=[src], writes=[dst])

    def recip(self, dst, dap, src, sap):
        return self.op("DVE", lambda e: e.reciprocal(dap, sap),
                       reads=[src], writes=[dst])

    def scan(self, eng, dst, dap, d0, d0ap, d1, d1ap, init=0.0):
        return self.op(
            eng, lambda e: e.tensor_tensor_scan(dap, d0ap, d1ap, init,
                                                ALU.mult, ALU.add),
            reads=[d0, d1], writes=[dst])

    def collective(self, kind, op, groups, in_b, out_b):
        waits = self._waits_for("POOL", [in_b], [out_b])
        self.cnt["cc"] += 1
        n = self.cnt["cc"]
        semh = self.sems

        def emit(engobj, wl=list(waits)):
            for (s, c) in wl:
                engobj.wait_ge(semh[s], c)
            engobj.collective_compute(
                kind, op, replica_groups=groups,
                ins=[in_b.t.ap().opt()], outs=[out_b.t.ap().opt()],
            ).then_inc(semh["cc"], 1)
        self.q["POOL"].append(emit)
        self.ninst["POOL"] += 1 + len(waits)
        _dmerge(out_b.last_w, {"cc": n})
        _dmerge(in_b.readers, {"cc": n})
        return n

    def finalize(self):
        with self.nc.Block() as block:
            for e in ENGS:
                fns = self.q[e]
                if not fns:
                    continue
                def body(engobj, fns=fns):
                    for f in fns:
                        f(engobj)
                getattr(block, _BLOCK_ATTR[e])(body)
        self.es.close()
        return self.nc


D = 1024
T = 2048
NB = 2
NQ = 4
DIN = 2048
NH = 8
HD = 256
NST = 64
QCH = 128
MTILE = 512
HALO = 644
KHALO = 640
WIN = 512
AHD = 64
AH = 8
HID = 2730
HIDP = 2816


def _chunked(w, kdim=128, mdim=128):
    K, M = w.shape
    nk = (K + kdim - 1) // kdim
    nm = (M + mdim - 1) // mdim
    wp = np.zeros((nk * kdim, nm * mdim), np.float32)
    wp[:K, :M] = w
    out = np.zeros((nm * 128, nk * mdim), np.float32)
    for m in range(nm):
        blk = wp[:, m * mdim:(m + 1) * mdim]
        out[m * 128:(m + 1) * 128] = (
            blk.reshape(nk, 128, mdim).transpose(1, 0, 2).reshape(128, nk * mdim))
    return np.ascontiguousarray(out)


def stage_inputs(inputs):
    x = np.asarray(inputs["L0_attn_x"], np.float32)
    maps = []
    ipw = np.asarray(inputs["L0_attn_in_proj_w"], np.float32)
    perm = np.concatenate([np.arange(0, 4096),
                           np.arange(4168, 4232),
                           np.arange(4104, 4168),
                           np.arange(4096, 4104)])
    wip = _chunked(np.ascontiguousarray(ipw.T[:, perm]))
    wop = _chunked(np.ascontiguousarray(
        np.asarray(inputs["L0_attn_out_proj_w"], np.float32).T))
    wmod = _chunked(np.asarray(inputs["L0_wmod"], np.float32))
    wqkv = [_chunked(np.asarray(inputs["L0_ffn_wqkv"], np.float32)[l])
            for l in range(2)]
    wo = [_chunked(np.asarray(inputs["L0_ffn_wo"], np.float32)[l])
          for l in range(2)]
    w1p, b1p = [], []
    for l in range(2):
        w1l = np.asarray(inputs["L0_ffn_w1"], np.float32)[l]
        b1l = np.asarray(inputs["L0_ffn_b1"], np.float32)[l]
        wpad = np.zeros((1024, 44 * 128), np.float32)
        bb = np.zeros((128, 44), np.float32)
        for j in range(22):
            a0, a1 = 128 * j, min(128 * (j + 1), HID)
            n = a1 - a0
            wpad[:, 256 * j:256 * j + n] = w1l[:, a0:a1]
            wpad[:, 256 * j + 128:256 * j + 128 + n] = w1l[:, HID + a0:HID + a1]
            bb[:n, 2 * j] = b1l[a0:a1]
            bb[:n, 2 * j + 1] = b1l[HID + a0:HID + a1]
        w1p.append(_chunked(wpad))
        b1p.append(bb)
    w2 = [_chunked(np.pad(np.asarray(inputs["L0_ffn_w2"], np.float32)[l],
                          ((0, HIDP - HID), (0, 0))))
          for l in range(2)]
    def fm(v, nch):
        return np.ascontiguousarray(
            np.asarray(v, np.float32).reshape(nch, 128).T)
    lg = [fm(np.asarray(inputs["L0_ffn_ln_g"], np.float32)[l], 8) for l in range(2)]
    lb = [fm(np.asarray(inputs["L0_ffn_ln_b"], np.float32)[l], 8) for l in range(2)]
    b2 = [fm(np.asarray(inputs["L0_ffn_b2"], np.float32)[l], 8) for l in range(2)]
    qsc = [fm(np.tile(np.asarray(inputs["L0_ffn_q_scale"], np.float32)[l], 8), 4)
           for l in range(2)]
    ksc = [fm(np.tile(np.asarray(inputs["L0_ffn_k_scale"], np.float32)[l], 8), 4)
           for l in range(2)]
    convw = np.ascontiguousarray(
        np.asarray(inputs["L0_attn_conv_w"], np.float32).reshape(16, 128, 4)
        .transpose(1, 0, 2).reshape(128, 64))
    convb = fm(inputs["L0_attn_conv_b"], 16)
    nw = fm(inputs["L0_attn_norm_w"], 16)
    Dp = np.tile(np.asarray(inputs["L0_attn_D_param"], np.float32)[None, :],
                 (128, 1))
    Alog = np.asarray(inputs["L0_attn_A_log"], np.float32).reshape(8, 1)
    dtb = np.asarray(inputs["L0_attn_dt_bias"], np.float32).reshape(8, 1)
    ident = np.eye(128, dtype=np.float32)
    sel = np.zeros((128, 8 * 128), np.float32)
    for r in range(8):
        sel[r, r * 128:(r + 1) * 128] = 1.0
    hsel2 = np.zeros((128, 128), np.float32)
    hsel2[0, :64] = 1.0
    hsel2[1, 64:] = 1.0
    hsel = np.zeros((128, 2), np.float32)
    hsel[:64, 0] = 1.0
    hsel[64:, 1] = 1.0
    rot = np.zeros((128, 128), np.float32)
    for b0 in (0, 64):
        for d_ in range(32):
            rot[b0 + d_ + 32, b0 + d_] = -1.0
            rot[b0 + d_, b0 + d_ + 32] = 1.0
    tri = np.tril(np.ones((128, 128), np.float32)).T
    band = np.full((128, 1408), NEG_BIG, np.float32)
    for p in range(128):
        v0, v1 = 384 + p, 384 + p + 512
        band[p, v0:v1 + 1] = 0.0
    krow = np.array([[1e-5, 1e-6, 1.0 / D, 1.0 / DIN, 0.25, 1.0, 0.0, 1e-12]],
                    np.float32)
    kcol = np.tile(krow, (128, 1)).astype(np.float32)
    mrow = np.full((1, 512), -1.0, np.float32)
    N_FULL = 8192
    cosf, sinf, scalef = _rope_tables(N_FULL)
    for c in range(8):
        b, q = c // 4, c % 4
        s = T * q
        m = {}
        xt = x[b].T
        xh3 = np.zeros((1024, T + 3), np.float32)
        lo = max(0, s - 3)
        xh3[:, 3 - (s - lo):] = xt[:, lo:s + T]
        m["xin"] = np.ascontiguousarray(xh3)
        m.update(wip=wip, wop=wop, wmod=wmod,
                 wqkv0=wqkv[0], wqkv1=wqkv[1], wo0=wo[0], wo1=wo[1],
                 w10=w1p[0], w11=w1p[1], w20=w2[0], w21=w2[1],
                 b10=b1p[0], b11=b1p[1], b20=b2[0], b21=b2[1],
                 lg0=lg[0], lg1=lg[1], lb0=lb[0], lb1=lb[1],
                 qsc0=qsc[0], qsc1=qsc[1], ksc0=ksc[0], ksc1=ksc[1],
                 convw=convw, convb=convb, nw=nw, Dp=Dp, Alog=Alog, dtb=dtb,
                 ident=ident, sel=sel, hsel2=hsel2, hsel=hsel, rot=rot,
                 tri=tri, band=band, krow=krow, kcol=kcol, mrow=mrow)
        m["cvec"] = fm(np.asarray(inputs["L0_c"], np.float32)[b], 8)
        def ftab(tab, pos):
            v = tab[np.clip(pos, 0, N_FULL - 1)]
            return np.ascontiguousarray(np.tile(v.T, (2, 1)).astype(np.float32))
        posq = np.arange(s - 128, s + T)
        posk = np.arange(s - KHALO, s + T)
        m["ropeqc"] = ftab(cosf * scalef * 8.0, posq)
        m["ropeqs"] = ftab(sinf * scalef * 8.0, posq)
        m["ropekc"] = ftab(cosf / scalef, posk)
        m["ropeks"] = ftab(sinf / scalef, posk)
        blk0 = np.zeros((4 * 128, 1024), np.float32)
        for st in range(4):
            sl = band[:, 384 - st * 128:384 - st * 128 + 1024].copy()
            if q == 0:
                sl[:, :512] = NEG_BIG
            blk0[st * 128:(st + 1) * 128] = sl
        m["blk0"] = blk0
        hm = np.full((128, KHALO), NEG_BIG, np.float32)
        if q > 0:
            for p in range(128):
                hm[p, p:p + 513] = 0.0
        m["hmask"] = hm
        m["hgate"] = np.full((128, 1), 0.0 if q == 0 else 1.0, np.float32)
        g = np.zeros((128, 4), np.float32)
        g1m = np.zeros((128, 4), np.float32)
        for j in range(3):
            g[:, j] = 1.0 if j < q else 0.0
            g1m[:, j] = 0.0 if j < q else 1.0
        m["e1g"] = g
        m["e1g1m"] = g1m
        es = np.zeros((128, 4), np.float32)
        if q > 0:
            es[:, q - 1] = 1.0
        m["eselg"] = es
        maps.append(m)
    return maps


def _rope_tables(n):
    half = np.arange(0, AHD, 2, dtype=np.float64)
    inv_freq = 1.0 / 10000.0 ** (half / AHD)
    t = np.arange(n, dtype=np.float64)
    freqs = t[:, None] * inv_freq[None, :]
    freqs = np.concatenate([freqs, freqs], axis=-1)
    base = (half + 0.4 * AHD) / (1.4 * AHD)
    power = (t - n // 2) / float(WIN // 2)
    scale = base[None, :] ** power[:, None]
    scale = np.concatenate([scale, scale], axis=-1)
    return np.cos(freqs), np.sin(freqs), scale


class Kern:
    def __init__(self, p, dbg=()):
        self.p = p
        self.dbg = set(dbg)
        self.A = p.arena("A", 46000)
        self.pb = [p.psum(f"pb{i}") for i in range(8)]
        self.dbg_outs = {}
        self.n_dbg = 0

    def dump(self, name, buf, prow=128, cols=None):
        if name not in self.dbg:
            return
        p = self.p
        cols = cols if cols is not None else buf.ncol
        d = p.dout("dbg_" + name, [prow, cols])
        p.dma("qPOOL", d, d.t[:, :], buf, buf[0:prow, 0:cols])
        self.dbg_outs[name] = (prow, cols)

    def bcast(self, bank, src, row, ncols, c0=0):
        p = self.p
        rhs_ap = src.t[0:128, src.pcol + c0:src.pcol + c0 + ncols]
        p.mm(bank, bank[0:128, 0:ncols], self.sel,
             self.sel[:, 128 * row:128 * (row + 1)], src, rhs_ap,
             start=True, stop=True)

    def ln_stats(self, chunks, caps, n, b_sum, b_sq, x2t, nch=8):
        p = self.p
        for j in range(nch):
            cb, cap = chunks[j], caps[j]
            p.act(x2t, x2t[0:128, 0:n], cb, cap, AF.Square)
            p.mm(b_sum, b_sum[0:1, 0:n], self.onescol, self.onescol[:, 0:1],
                 cb, cap, start=(j == 0), stop=(j == nch - 1))
            p.mm(b_sq, b_sq[0:1, 0:n], self.onescol, self.onescol[:, 0:1],
                 x2t, x2t[0:128, 0:n], start=(j == 0), stop=(j == nch - 1))

    def ln_rows(self, b_sum, b_sq, n, rinv, mur, eps_idx, dinv_idx):
        p = self.p
        kr = self.krow
        t0, t1, t2 = self.r0, self.r1, self.r2
        p.ts("DVE", t0, t0[0:1, 0:n], b_sum, b_sum[0:1, 0:n],
             (kr, kr[0:1, dinv_idx:dinv_idx + 1]), None, ALU.mult)
        p.ts("DVE", t1, t1[0:1, 0:n], b_sq, b_sq[0:1, 0:n],
             (kr, kr[0:1, dinv_idx:dinv_idx + 1]), None, ALU.mult)
        p.act(t2, t2[0:1, 0:n], t0, t0[0:1, 0:n], AF.Square)
        p.tt("DVE", ALU.subtract, t1, t1[0:1, 0:n], t1, t1[0:1, 0:n],
             t2, t2[0:1, 0:n])
        p.act(t2, t2[0:1, 0:n], t1, t1[0:1, 0:n], AF.Sqrt,
              bias=(kr, kr[0:1, eps_idx:eps_idx + 1]))
        p.recip(rinv, rinv[0:1, 0:n], t2, t2[0:1, 0:n])
        p.tt("DVE", ALU.mult, mur, mur[0:1, 0:n], t0, t0[0:1, 0:n],
             rinv, rinv[0:1, 0:n])

    def setup(self):
        p = self.p
        A = self.A
        zbuf = Buf("zall", A.t, "sbuf", ncol=A.ncols)
        p.memset("DVE", zbuf, A.t[0:128, 0:A.ncols], 0.0)
        A.dead.append((0, A.ncols, dict(zbuf.last_w)))
        def cload(name, ncol, prow=128, dname=None, q="qSP"):
            d = p.din(dname or name, [prow, ncol])
            b = A.alloc(name, ncol)
            p.dma(q, b, b[0:prow, 0:ncol], d, d.t[:, :])
            return b
        self.sel = cload("sel", 1024)
        self.ident = cload("ident", 128)
        self.hsel2 = cload("hsel2", 128)
        self.hsel = cload("hsel", 2)
        self.rot = cload("rot", 128)
        self.tri = cload("tri", 128)
        self.onescol = A.alloc("onescol", 1)
        p.ts("DVE", self.onescol, self.onescol[0:128, 0:1], self.hsel,
             self.hsel[0:128, 0:1], 0.0, 1.0, ALU.mult, ALU.add)
        self.krow = cload("krow", 8, prow=1)
        self.kcol = cload("kcol", 8)
        self.convw = cload("convw", 64)
        self.convb = cload("convb", 16)
        self.nw = cload("nw", 16)
        self.Dp = cload("Dp", 8)
        self.Alog = cload("Alog", 1, prow=8)
        self.dtb = cload("dtb", 1, prow=8)
        self.cvec = cload("cvec", 8)
        self.hgate = cload("hgate", 1)
        self.e1g = cload("e1g", 4)
        self.e1g1m = cload("e1g1m", 4)
        self.r0 = A.alloc("r0", 512)
        self.r1 = A.alloc("r1", 512)
        self.r2 = A.alloc("r2", 512)
        self.rinv = A.alloc("rinv", 512)
        self.mur = A.alloc("mur", 512)
        self.negA = A.alloc("negA", 1)
        p.act(self.negA, self.negA[0:8, 0:1], self.Alog, self.Alog[0:8, 0:1],
              AF.Exp)
        p.ts("DVE", self.negA, self.negA[0:8, 0:1], self.negA,
             self.negA[0:8, 0:1], -1.0, None, ALU.mult)
        self.wip = p.din("wip", [34 * 128, 8 * 128])
        self.wop = p.din("wop", [8 * 128, 16 * 128])
        self.wmod = p.din("wmod", [48 * 128, 8 * 128])
        self.wqkv = [p.din(f"wqkv{l}", [12 * 128, 8 * 128]) for l in range(2)]
        self.wo = [p.din(f"wo{l}", [8 * 128, 4 * 128]) for l in range(2)]
        self.w1 = [p.din(f"w1{l}", [44 * 128, 8 * 128]) for l in range(2)]
        self.w2 = [p.din(f"w2{l}", [8 * 128, 22 * 128]) for l in range(2)]
        self.b1 = [cload(f"b1_{l}", 44, dname=f"b1{l}") for l in range(2)]
        self.b2 = [cload(f"b2_{l}", 8, dname=f"b2{l}") for l in range(2)]
        self.lg = [cload(f"lg_{l}", 8, dname=f"lg{l}") for l in range(2)]
        self.lbv = [cload(f"lb_{l}", 8, dname=f"lb{l}") for l in range(2)]
        self.qsc = [cload(f"qsc_{l}", 4, dname=f"qsc{l}") for l in range(2)]
        self.ksc = [cload(f"ksc_{l}", 4, dname=f"ksc{l}") for l in range(2)]
        self.xin = p.din("xin", [D, T + 3])
        self.ropeqc = p.din("ropeqc", [128, 128 + T])
        self.ropeqs = p.din("ropeqs", [128, 128 + T])
        self.ropekc = p.din("ropekc", [128, KHALO + T])
        self.ropeks = p.din("ropeks", [128, KHALO + T])
        self.band = p.din("band", [128, 1408])
        self.blk0 = p.din("blk0", [4 * 128, 1024])
        self.hmask = p.din("hmask", [128, KHALO])
        def scr(name, shape):
            return (p.dout if name in self.dbg else p.dram)(name, shape)
        self.z_scr = scr("z_scr", [DIN, T])
        self.y_scr = scr("y_scr", [DIN, T])
        self.xmid = scr("xmid", [D, T])
        self.xffn = scr("xffn", [D, T])
        self.xffn_b = scr("xffn_b", [D, T])
        self.e1_in = p.dram("e1_in", [NH * NST + 8, HD])
        self.e1_out = p.dram("e1_out", [4 * (NH * NST + 8), HD])
        self.e2_in = p.dram("e2_in", [D, HALO])
        self.e2_out = p.dram("e2_out", [4 * D, HALO])
        self.e3_in = p.dram("e3_in", [D, HALO])
        self.e3_out = p.dram("e3_out", [4 * D, HALO])
        self.eselg = cload("eselg", 4)
        self.out_fm = p.dout("out_fm", [D, T])
        self.qpadE = p.sbuf("qpadE", [128, 512])
        self.qpadO = p.sbuf("qpadO", [128, 512])
        p.memset("DVE", self.qpadE, self.qpadE[64:128, 0:512], 0.0)
        p.memset("DVE", self.qpadO, self.qpadO[0:64, 0:512], 0.0)

    def mod_vector(self):
        p, A = self.p, self.A
        self.mod = A.alloc("mod", 48)
        self.modp1 = A.alloc("modp1", 16)
        A.push()
        silc = A.alloc("silc", 8)
        p.act(silc, silc[0:128, 0:8], self.cvec, self.cvec[0:128, 0:8],
              AF.Silu)
        mod = self.mod
        wbuf = [A.alloc(f"wmv{i}", 1024) for i in range(2)]
        for m in range(48):
            w = wbuf[m % 2]
            p.dma("qSP", w, w[0:128, 0:1024], self.wmod,
                  self.wmod.t[m * 128:(m + 1) * 128, :])
            bank = self.pb[m % 8]
            for kk in range(8):
                p.mm(bank, bank[0:128, 0:1], w, w[0:128, 128 * kk:128 * (kk + 1)],
                     silc, silc[0:128, kk:kk + 1], start=(kk == 0), stop=(kk == 7))
            p.copy("ACT", mod, mod[0:128, m:m + 1], bank, bank[0:128, 0:1])
        A.pop()
        p.ts("DVE", self.modp1, self.modp1[0:128, 0:8], self.mod,
             self.mod[0:128, 8:16], 1.0, None, ALU.add)
        p.ts("DVE", self.modp1, self.modp1[0:128, 8:16], self.mod,
             self.mod[0:128, 32:40], 1.0, None, ALU.add)
        t = self.A.alloc("gtmp", 16)
        for (c0, o) in ((16, 0), (40, 8)):
            p.ts("DVE", t, t[0:128, o:o + 8], self.mod,
                 self.mod[0:128, c0:c0 + 8], 0.0, None, ALU.is_ge)
            p.ts("DVE", t, t[0:128, o:o + 8], t, t[0:128, o:o + 8],
                 2.0, -1.0, ALU.mult, ALU.add)
            p.act(self.mod, self.mod[0:128, c0:c0 + 8], self.mod,
                  self.mod[0:128, c0:c0 + 8], AF.Abs)
            p.ts("DVE", self.mod, self.mod[0:128, c0:c0 + 8], self.mod,
                 self.mod[0:128, c0:c0 + 8], 0.25, None, ALU.max)
            p.tt("DVE", ALU.mult, self.mod, self.mod[0:128, c0:c0 + 8],
                 self.mod, self.mod[0:128, c0:c0 + 8], t, t[0:128, o:o + 8])
        self.dump("mod", self.mod)

    def _ln_mod_x(self, xsrc, xc0, n, modoff):
        p, A = self.p, self.A
        xt = [A.alloc(f"lx{j}", n) for j in range(8)]
        for j in range(8):
            p.dma("qACT", xt[j], xt[j][0:128, 0:n], xsrc,
                  xsrc.t[128 * j:128 * (j + 1), xc0:xc0 + n])
        x2t = A.alloc("lx2", n)
        bs, bq = self.pb[0], self.pb[1]
        self.ln_stats(xt, [b[0:128, 0:n] for b in xt], n, bs, bq, x2t)
        self.ln_rows(bs, bq, n, self.rinv, self.mur, eps_idx=0, dinv_idx=2)
        bmu, br = self.pb[2], self.pb[3]
        self.bcast(bmu, self.mur, 0, n)
        self.bcast(br, self.rinv, 0, n)
        po, mo = modoff
        h = xt
        for j in range(8):
            p.tt("DVE", ALU.mult, h[j], h[j][0:128, 0:n],
                 xt[j], xt[j][0:128, 0:n], br, br[0:128, 0:n])
            p.tt("DVE", ALU.subtract, h[j], h[j][0:128, 0:n],
                 h[j], h[j][0:128, 0:n], bmu, bmu[0:128, 0:n])
            p.ts("DVE", h[j], h[j][0:128, 0:n], h[j], h[j][0:128, 0:n],
                 (self.modp1, self.modp1[0:128, po + j:po + j + 1]),
                 (self.mod, self.mod[0:128, mo + j:mo + j + 1]),
                 ALU.mult, ALU.add)
        return h

    def _inproj(self, h, n, c0, mini=False):
        p, A = self.p, self.A
        wb = [A.alloc(f"wip{i}", 1024) for i in range(2)]
        zt = [A.alloc(f"zt{i}", n) for i in range(2)]
        mlist = range(16, 32) if mini else range(34)
        for i, m in enumerate(mlist):
            w = wb[i % 2]
            p.dma("qSP", w, w[0:128, 0:1024], self.wip,
                  self.wip.t[m * 128:(m + 1) * 128, :])
            bank = self.pb[4 + (i % 2)]
            M = 8 if m == 33 else 128
            for kk in range(8):
                p.mm(bank, bank[0:M, 0:n], w, w[0:128, 128 * kk:128 * kk + M],
                     h[kk], h[kk][0:128, 0:n], start=(kk == 0), stop=(kk == 7))
            if mini:
                ch = m - 16
                p.copy("DVE", self.xchalo, self.xchalo[0:128, 3 * ch:3 * ch + 3],
                       bank, bank[0:128, 0:n])
            elif m < 16:
                z = zt[i % 2]
                p.copy("DVE", z, z[0:128, 0:n], bank, bank[0:128, 0:n])
                p.dma("qPOOL", self.z_scr, self.z_scr.t[128 * m:128 * (m + 1),
                                                        c0:c0 + n],
                      z, z[0:128, 0:n])
            elif m < 32:
                ch = m - 16
                p.copy("DVE", self.xct[ch], self.xct[ch][0:128, 3:3 + n],
                       bank, bank[0:128, 0:n])
            elif m == 32:
                p.copy("DVE", self.chunk32, self.chunk32[0:128, c0:c0 + n],
                       bank, bank[0:128, 0:n])
            else:
                dtt = A.alloc("dtt", n)
                dtm_ = A.alloc("dtm", n)
                p.ts("DVE", dtt, dtt[0:8, 0:n], bank, bank[0:8, 0:n],
                     (self.dtb, self.dtb[0:8, 0:1]), None, ALU.add)
                p.ts("DVE", dtm_, dtm_[0:8, 0:n], dtt, dtt[0:8, 0:n],
                     0.0, None, ALU.max)
                p.act(dtt, dtt[0:8, 0:n], dtt, dtt[0:8, 0:n], AF.Abs)
                p.ts("DVE", dtt, dtt[0:8, 0:n], dtt, dtt[0:8, 0:n],
                     -1.0, None, ALU.mult)
                p.act(dtt, dtt[0:8, 0:n], dtt, dtt[0:8, 0:n], AF.Exp)
                p.act(dtt, dtt[0:8, 0:n], dtt, dtt[0:8, 0:n], AF.Ln,
                      bias=(self.kcol, self.kcol[0:8, 5:6]))
                p.tt("DVE", ALU.add, self.dt8, self.dt8[0:8, c0:c0 + n],
                     dtm_, dtm_[0:8, 0:n], dtt, dtt[0:8, 0:n])

    def mamba_a(self):
        p, A = self.p, self.A
        A.push()
        self.Lg = A.alloc("Lg", T + 1)
        self.chunk32 = A.alloc("chunk32", T)
        self.dt8 = A.alloc("dt8", T)
        self.state = [A.alloc(f"state{h}", HD) for h in range(NH)]
        self.xchalo = A.alloc("xchalo", 48)
        self.onesT = A.alloc("onesT", MTILE)
        p.memset("DVE", self.onesT, self.onesT[0:8, 0:MTILE], 1.0)
        for h in range(NH):
            p.memset("DVE", self.state[h], self.state[h][0:128, 0:HD], 0.0)
        A.push()
        self.xct = [A.alloc(f"xct{ch}", MTILE + 3) for ch in range(16)]
        A.push()
        hmini = self._ln_mod_x(self.xin, 0, 3, (0, 0))
        self._inproj(hmini, 3, 0, mini=True)
        A.pop()
        for ch in range(16):
            p.ts("DVE", self.xchalo, self.xchalo[0:128, 3 * ch:3 * ch + 3],
                 self.xchalo, self.xchalo[0:128, 3 * ch:3 * ch + 3],
                 (self.hgate, self.hgate[0:128, 0:1]), None, ALU.mult)
        for t in range(T // MTILE):
            self._mamba_tile(t)
        A.pop()
        self.dump("dt8", self.dt8, prow=8)
        self.dump("Lg", self.Lg, prow=8, cols=T + 1)
        self.dump("chunk32", self.chunk32)

    def _mamba_tile(self, t):
        p, A = self.p, self.A
        n, c0 = MTILE, t * MTILE
        A.push()
        h = self._ln_mod_x(self.xin, 3 + c0, n, (0, 0))
        if t == 0:
            self.dump("h0", h[0], cols=MTILE)
            self.dump("mur0", self.mur, prow=1, cols=MTILE)
            self.dump("rinv0", self.rinv, prow=1, cols=MTILE)
            if "bmu0" in self.dbg:
                tmp = A.alloc("bmudump", MTILE)
                p.copy("DVE", tmp, tmp[0:128, 0:MTILE], self.pb[2],
                       self.pb[2][0:128, 0:MTILE])
                self.dump("bmu0", tmp, cols=MTILE)
        for ch in range(16):
            p.copy("DVE", self.xct[ch], self.xct[ch][0:128, 0:3],
                   self.xchalo, self.xchalo[0:128, 3 * ch:3 * ch + 3])
        self._inproj(h, n, c0)
        for ch in range(16):
            p.copy("DVE", self.xchalo, self.xchalo[0:128, 3 * ch:3 * ch + 3],
                   self.xct[ch], self.xct[ch][0:128, n:n + 3])
        self.xcs = [A.alloc(f"xcs{ch}", n) for ch in range(16)]
        cw, cb = self.convw, self.convb
        for ch in range(16):
            acc = self.xcs[ch]
            p.ts("DVE", acc, acc[0:128, 0:n], self.xct[ch],
                 self.xct[ch][0:128, 0:n],
                 (cw, cw[0:128, 4 * ch:4 * ch + 1]),
                 (cb, cb[0:128, ch:ch + 1]), ALU.mult, ALU.add)
            for k in range(1, 4):
                p.stt("DVE", acc, acc[0:128, 0:n], self.xct[ch],
                      self.xct[ch][0:128, k:k + n],
                      (cw, cw[0:128, 4 * ch + k:4 * ch + k + 1]),
                      acc, acc[0:128, 0:n], ALU.mult, ALU.add)
            p.act(acc, acc[0:128, 0:n], acc, acc[0:128, 0:n], AF.Silu)
        p.ts("DVE", self.Lg, self.Lg[0:8, 1 + c0:1 + c0 + n], self.dt8,
             self.dt8[0:8, c0:c0 + n], (self.negA, self.negA[0:8, 0:1]),
             None, ALU.mult)
        p.tt("DVE", ALU.add, self.Lg, self.Lg[0:8, 1 + c0:2 + c0],
             self.Lg, self.Lg[0:8, 1 + c0:2 + c0],
             self.Lg, self.Lg[0:8, c0:c0 + 1])
        p.scan("DVE", self.Lg, self.Lg[0:8, 1 + c0:1 + c0 + n],
               self.onesT, self.onesT[0:8, 0:n],
               self.Lg, self.Lg[0:8, 1 + c0:1 + c0 + n])
        bpad = A.alloc("bpad", n)
        p.memset("DVE", bpad, bpad[64:128, 0:n], 0.0)
        p.copy("DVE", bpad, bpad[0:64, 0:n], self.chunk32,
               self.chunk32[64:128, c0:c0 + n])
        for cc in range(n // QCH):
            self._scan_chunk(c0 + cc * QCH, cc, bpad)
        A.pop()

    def _scan_chunk(self, tc0, cc, bpad):
        p, A = self.p, self.A
        pb = self.pb
        A.push()
        bT = pb[1]
        p.transpose(bT, bT[0:128, 0:128], self.Lg,
                    self.Lg.t[0:128, self.Lg.pcol + 1 + tc0:
                              self.Lg.pcol + 1 + tc0 + 128], self.ident)
        LgT = A.alloc("LgT", 8)
        p.copy("ACT", LgT, LgT[0:128, 0:8], bT, bT[0:128, 0:8])
        p.transpose(bT, bT[0:128, 128:256], self.chunk32,
                    self.chunk32[0:128, tc0:tc0 + 128], self.ident)
        CBtm = A.alloc("CBtm", 128)
        p.copy("ACT", CBtm, CBtm[0:128, 0:128], bT, bT[0:128, 128:256])
        eL8 = A.alloc("eL8", 128)
        p.ts("DVE", eL8, eL8[0:8, 0:128], self.Lg,
             self.Lg[0:8, 1 + tc0:1 + tc0 + 128],
             (self.Lg, self.Lg[0:8, tc0:tc0 + 1]), None, ALU.subtract)
        p.act(eL8, eL8[0:8, 0:128], eL8, eL8[0:8, 0:128], AF.Exp)
        bG = pb[0]
        p.mm(bG, bG[0:128, 0:128], bpad, bpad[0:128, 128 * cc:128 * cc + 128],
             self.chunk32, self.chunk32[0:128, tc0:tc0 + 128],
             start=True, stop=True)
        Gm = A.alloc("Gm", 128)
        p.tt("DVE", ALU.mult, Gm, Gm[0:128, 0:128], bG, bG[0:128, 0:128],
             self.tri, self.tri[0:128, 0:128])
        yct = [A.alloc(f"yct{ch}", 128) for ch in range(16)]
        Ew = A.alloc("Ew", 128)
        Sw = A.alloc("Sw", 128)
        Ct = A.alloc("Ct", 128)
        XT = A.alloc("XT", 256)
        Bptm = A.alloc("Bptm", 64)
        for hh in range(NH):
            bB, bB2 = pb[2], pb[3]
            self.bcast(bB, self.Lg, hh, 128, c0=1 + tc0)
            p.ts("DVE", Ew, Ew[0:128, 0:128], bB, bB[0:128, 0:128],
                 (LgT, LgT[0:128, hh:hh + 1]), 0.0, ALU.subtract, ALU.min)
            p.act(Ew, Ew[0:128, 0:128], Ew, Ew[0:128, 0:128], AF.Exp)
            p.tt("DVE", ALU.mult, Sw, Sw[0:128, 0:128], Gm, Gm[0:128, 0:128],
                 Ew, Ew[0:128, 0:128])
            self.bcast(bB2, eL8, hh, 128)
            p.tt("DVE", ALU.mult, Ct, Ct[0:128, 0:128], self.chunk32,
                 self.chunk32[0:128, tc0:tc0 + 128], bB2, bB2[0:128, 0:128])
            for j in range(2):
                src = self.xcs[2 * hh + j]
                p.transpose(bT, bT[0:128, 256 + 128 * j:256 + 128 * (j + 1)],
                            src, src[0:128, 128 * cc:128 * cc + 128],
                            self.ident)
                p.copy("ACT", XT, XT[0:128, 128 * j:128 * (j + 1)], bT,
                       bT[0:128, 256 + 128 * j:256 + 128 * (j + 1)])
            for j in range(2):
                bY = pb[4 + j]
                p.mm(bY, bY[0:128, 0:128], XT, XT[0:128, 128 * j:128 * (j + 1)],
                     Sw, Sw[0:128, 0:128], start=True, stop=False)
                st = self.state[hh]
                p.mm(bY, bY[0:128, 0:128], st, st[0:128, 128 * j:128 * (j + 1)],
                     Ct, Ct[0:128, 0:128], start=False, stop=True)
                ch = 2 * hh + j
                p.stt("DVE", yct[ch], yct[ch][0:128, 0:128], self.xcs[ch],
                      self.xcs[ch][0:128, 128 * cc:128 * cc + 128],
                      (self.Dp, self.Dp[0:128, hh:hh + 1]),
                      bY, bY[0:128, 0:128], ALU.mult, ALU.add)
            p.ts("DVE", Bptm, Bptm[0:128, 0:64], CBtm, CBtm[0:128, 64:128],
                 (Ew, Ew[0:128, 127:128]), None, ALU.mult)
            bS = pb[6]
            p.mm(bS, bS[0:64, 0:256], Bptm, Bptm[0:128, 0:64],
                 XT, XT[0:128, 0:256], start=True, stop=True)
            st = self.state[hh]
            p.stt("DVE", st, st[0:64, 0:256], st, st[0:64, 0:256],
                  (bB2, bB2[0:64, 127:128]), bS, bS[0:64, 0:256],
                  ALU.mult, ALU.add)
        ydst = self.y_scr.t.ap().rearrange("(ch p) t -> p ch t", p=128)
        ysrc = self.A.t[0:128, yct[0].pcol:yct[0].pcol + 16 * 128].rearrange(
            "p (ch t) -> p ch t", ch=16)
        ybuf = Buf("yblk", self.A.t, "sbuf", pcol=yct[0].pcol, ncol=16 * 128)
        for b in yct:
            _dmerge(ybuf.last_w, b.last_w)
        tag = p.dma("qPOOL", self.y_scr, ydst[:, :, tc0:tc0 + 128], ybuf, ysrc)
        for b in yct:
            _dmerge(b.readers, {tag[0]: tag[1]})
        A.pop()



    def e1_exchange(self):
        p, A = self.p, self.A
        for h in range(NH):
            p.dma("qPOOL", self.e1_in,
                  self.e1_in.t[h * NST:(h + 1) * NST, :],
                  self.state[h], self.state[h][0:64, 0:HD])
        A.push()
        pexp = A.alloc("pexp", HD)
        p.memset("DVE", pexp, pexp[0:8, 0:HD], 0.0)
        p.act(pexp, pexp[0:8, 0:1], self.Lg, self.Lg[0:8, T:T + 1], AF.Exp)
        p.dma("qPOOL", self.e1_in, self.e1_in.t[NH * NST:NH * NST + 8, :],
              pexp, pexp[0:8, 0:HD])
        p.collective("AllGather", ALU.bypass,
                     [[0, 1, 2, 3], [4, 5, 6, 7]], self.e1_in, self.e1_out)
        R = NH * NST + 8
        prow = A.alloc("prow", 8)
        peff = A.alloc("peff", 8)
        spe = A.alloc("spe", 2048)
        for j in range(3):
            p.dma("qSP", prow, prow[0:1, 0:8],
                  self.e1_out, self.e1_out.t[j * R + NH * NST:
                                             j * R + NH * NST + 8, 0:1],
                  allow_slow_non_contiguous=True)
            bk = self.pb[7]
            self.bcast(bk, prow, 0, 8)
            p.ts("DVE", peff, peff[0:128, 0:8], bk, bk[0:128, 0:8],
                 (self.e1g, self.e1g[0:128, j:j + 1]),
                 (self.e1g1m, self.e1g1m[0:128, j:j + 1]), ALU.mult, ALU.add)
            R_ = NH * NST + 8
            for h in range(NH):
                p.dma("qSP", spe, spe[0:64, 256 * h:256 * h + HD],
                      self.e1_out,
                      self.e1_out.t[j * R_ + h * NST:j * R_ + (h + 1) * NST, :])
            p.ts("DVE", spe, spe[0:64, 0:2048], spe, spe[0:64, 0:2048],
                 (self.e1g, self.e1g[0:64, j:j + 1]), None, ALU.mult)
            for h in range(NH):
                if j == 0:
                    p.copy("DVE", self.state[h], self.state[h][0:64, 0:HD],
                           spe, spe[0:64, 256 * h:256 * h + HD])
                else:
                    p.stt("DVE", self.state[h], self.state[h][0:64, 0:HD],
                          self.state[h], self.state[h][0:64, 0:HD],
                          (peff, peff[0:64, h:h + 1]),
                          spe, spe[0:64, 256 * h:256 * h + HD],
                          ALU.mult, ALU.add)
        A.pop()

    def mamba_b(self):
        for t in range(T // MTILE):
            self._mamba_b_tile(t)
        self.A.pop()

    def _mamba_b_tile(self, t):
        p, A = self.p, self.A
        n, c0 = MTILE, t * MTILE
        A.push()
        eL = A.alloc("eL", n)
        p.act(eL, eL[0:8, 0:n], self.Lg, self.Lg[0:8, 1 + c0:1 + c0 + n],
              AF.Exp)
        yt = [A.alloc(f"by{ch}", n) for ch in range(16)]
        zt = [A.alloc(f"bz{ch}", n) for ch in range(16)]
        for ch in range(16):
            p.dma("qACT", yt[ch], yt[ch][0:128, 0:n], self.y_scr,
                  self.y_scr.t[128 * ch:128 * (ch + 1), c0:c0 + n])
            p.dma("qSP", zt[ch], zt[ch][0:128, 0:n], self.z_scr,
                  self.z_scr.t[128 * ch:128 * (ch + 1), c0:c0 + n])
        Ctb = A.alloc("Ctb", 128)
        for cc in range(n // QCH):
            tc0 = c0 + cc * QCH
            for h in range(NH):
                bk = self.pb[2]
                self.bcast(bk, eL, h, QCH, c0=cc * QCH)
                p.tt("DVE", ALU.mult, Ctb, Ctb[0:128, 0:QCH], self.chunk32,
                     self.chunk32[0:128, tc0:tc0 + QCH], bk, bk[0:128, 0:QCH])
                for j in range(2):
                    bY = self.pb[4 + j]
                    st = self.state[h]
                    p.mm(bY, bY[0:128, 0:QCH], st,
                         st[0:128, 128 * j:128 * (j + 1)],
                         Ctb, Ctb[0:128, 0:QCH], start=True, stop=True)
                    ch = 2 * h + j
                    p.tt("DVE", ALU.add, yt[ch],
                         yt[ch][0:128, cc * QCH:cc * QCH + QCH],
                         yt[ch], yt[ch][0:128, cc * QCH:cc * QCH + QCH],
                         bY, bY[0:128, 0:QCH])
        x2t = A.alloc("bx2", n)
        bs = self.pb[0]
        for ch in range(16):
            p.act(zt[ch], zt[ch][0:128, 0:n], zt[ch], zt[ch][0:128, 0:n],
                  AF.Silu)
            p.tt("DVE", ALU.mult, yt[ch], yt[ch][0:128, 0:n],
                 yt[ch], yt[ch][0:128, 0:n], zt[ch], zt[ch][0:128, 0:n])
            p.act(x2t, x2t[0:128, 0:n], yt[ch], yt[ch][0:128, 0:n], AF.Square)
            p.mm(bs, bs[0:1, 0:n], self.onescol, self.onescol[:, 0:1],
                 x2t, x2t[0:128, 0:n], start=(ch == 0), stop=(ch == 15))
        p.ts("DVE", self.r0, self.r0[0:1, 0:n], bs, bs[0:1, 0:n],
             (self.krow, self.krow[0:1, 3:4]), None, ALU.mult)
        p.act(self.r0, self.r0[0:1, 0:n], self.r0, self.r0[0:1, 0:n], AF.Sqrt,
              bias=(self.krow, self.krow[0:1, 1:2]))
        p.recip(self.rinv, self.rinv[0:1, 0:n], self.r0, self.r0[0:1, 0:n])
        br = self.pb[1]
        self.bcast(br, self.rinv, 0, n)
        for ch in range(16):
            p.stt("DVE", yt[ch], yt[ch][0:128, 0:n], yt[ch],
                  yt[ch][0:128, 0:n], (self.nw, self.nw[0:128, ch:ch + 1]),
                  br, br[0:128, 0:n], ALU.mult, ALU.mult)
        wb = [A.alloc(f"wob{i}", 2048) for i in range(2)]
        xnew = [A.alloc(f"xn{j}", n) for j in range(8)]
        for m in range(8):
            w = wb[m % 2]
            p.dma("qSP", w, w[0:128, 0:2048], self.wop,
                  self.wop.t[m * 128:(m + 1) * 128, :])
            bank = self.pb[4 + (m % 2)]
            for kk in range(16):
                p.mm(bank, bank[0:128, 0:n], w,
                     w[0:128, 128 * kk:128 * (kk + 1)],
                     yt[kk], yt[kk][0:128, 0:n],
                     start=(kk == 0), stop=(kk == 15))
            xi = xnew[m]
            p.dma("qACT", xi, xi[0:128, 0:n], self.xin,
                  self.xin.t[128 * m:128 * (m + 1), 3 + c0:3 + c0 + n])
            p.stt("DVE", xi, xi[0:128, 0:n], bank, bank[0:128, 0:n],
                  (self.mod, self.mod[0:128, 16 + m:17 + m]),
                  xi, xi[0:128, 0:n], ALU.mult, ALU.add)
            p.dma("qPOOL", self.xmid, self.xmid.t[128 * m:128 * (m + 1),
                                                  c0:c0 + n],
                  xi, xi[0:128, 0:n])
        h2 = self._ln_mod_chunks(xnew, n, (8, 24))
        for j in range(8):
            p.dma("qPOOL", self.xffn, self.xffn.t[128 * j:128 * (j + 1),
                                                  c0:c0 + n],
                  h2[j], h2[j][0:128, 0:n])
            g0 = T - HALO
            lo = max(g0, c0)
            if lo < c0 + n:
                o = lo - c0
                p.dma("qPOOL", self.e2_in,
                      self.e2_in.t[128 * j:128 * (j + 1), lo - g0:c0 + n - g0],
                      h2[j], h2[j][0:128, o:n])
        A.pop()

    def _ln_mod_chunks(self, xb, n, modoff):
        p, A = self.p, self.A
        x2t = A.alloc("lnx2", n)
        bs, bq = self.pb[0], self.pb[1]
        self.ln_stats(xb, [b[0:128, 0:n] for b in xb], n, bs, bq, x2t)
        self.ln_rows(bs, bq, n, self.rinv, self.mur, eps_idx=0, dinv_idx=2)
        bmu, br = self.pb[2], self.pb[3]
        self.bcast(bmu, self.mur, 0, n)
        self.bcast(br, self.rinv, 0, n)
        po, mo = modoff
        for j in range(8):
            p.tt("DVE", ALU.mult, xb[j], xb[j][0:128, 0:n],
                 xb[j], xb[j][0:128, 0:n], br, br[0:128, 0:n])
            p.tt("DVE", ALU.subtract, xb[j], xb[j][0:128, 0:n],
                 xb[j], xb[j][0:128, 0:n], bmu, bmu[0:128, 0:n])
            p.ts("DVE", xb[j], xb[j][0:128, 0:n], xb[j], xb[j][0:128, 0:n],
                 (self.modp1, self.modp1[0:128, po + j:po + j + 1]),
                 (self.mod, self.mod[0:128, mo + j:mo + j + 1]),
                 ALU.mult, ALU.add)
        return xb

    def ffn_layer(self, l):
        p, A = self.p, self.A
        xin_d = self.xffn if l == 0 else self.xffn_b
        ein = self.e2_in if l == 0 else self.e3_in
        eout = self.e2_out if l == 0 else self.e3_out
        p.collective("AllGather", ALU.bypass, [[0, 1, 2, 3], [4, 5, 6, 7]],
                     ein, eout)
        if "stop=cc" in self.dbg:
            return
        A.push()
        self.bandb = A.alloc("bandb", 1408)
        p.dma("qACT", self.bandb, self.bandb[0:128, 0:1408], self.band,
              self.band.t[:, :])
        if "stop=band" in self.dbg:
            return
        self.hbcol = A.alloc("hbcol", 8)
        self.wv = [A.alloc(f"wvf{kk}", 512) for kk in range(8)]
        kslot = [[A.alloc(f"k{sl}_{pr}", 512) for pr in range(4)]
                 for sl in range(2)]
        vslot = [[A.alloc(f"v{sl}_{tc}", 512) for tc in range(4)]
                 for sl in range(2)]
        khx = [A.alloc(f"khx{pr}", 128) for pr in range(4)]
        vhx = A.alloc("vhx", 512)
        self.x1carry = A.alloc("x1carry", 8)
        self.x1acol = A.alloc("x1acol", 8)
        A.push()
        hb = [A.alloc(f"hb{j}", HALO) for j in range(8)]
        tmph = A.alloc("tmph", HALO)
        for j in range(8):
            if f"stop=hbj{j}" in self.dbg:
                return
            for jb in range(4):
                p.dma("qSP", tmph, tmph[0:128, 0:HALO], eout,
                      eout.t[jb * D + 128 * j:jb * D + 128 * (j + 1), :])
                if jb == 0:
                    p.ts("DVE", hb[j], hb[j][0:128, 0:HALO], tmph,
                         tmph[0:128, 0:HALO],
                         (self.eselg, self.eselg[0:128, 0:1]), None, ALU.mult)
                else:
                    p.stt("DVE", hb[j], hb[j][0:128, 0:HALO], tmph,
                          tmph[0:128, 0:HALO],
                          (self.eselg, self.eselg[0:128, jb:jb + 1]),
                          hb[j], hb[j][0:128, 0:HALO], ALU.mult, ALU.add)
        x1a = A.alloc("x1a", 128)
        if "stop=hb" in self.dbg:
            A.pop()
            return
        if "skip=hkv" not in self.dbg:
            self._halo_kv(l, hb, kslot[1], vslot[1], khx, vhx)
        if "stop=hkv" in self.dbg:
            A.pop()
            return
        if "skip=hattn" not in self.dbg:
            self._halo_attn(l, hb, kslot[1], vslot[1], khx, vhx, x1a)
        if "stop=hattn" in self.dbg:
            A.pop()
            return
        for j in range(8):
            p.copy("DVE", self.hbcol, self.hbcol[0:128, j:j + 1], hb[j],
                   hb[j][0:128, HALO - 1:HALO])
        A.pop()
        nblk = 4
        for d in self.dbg:
            if d.startswith("nblk="):
                nblk = int(d.split("=")[1])
        for w in range(nblk):
            self._ffn_block(l, w, xin_d, kslot, vslot)
        A.pop()

    def _qkv_chunks(self, l, src_getter, n, which, dst, rope_c0):
        p, A = self.p, self.A
        wq = self.wqkv[l]
        scv = self.qsc[l] if which == 0 else self.ksc[l]
        tc = self.ropeqc if which == 0 else self.ropekc
        tsn = self.ropeqs if which == 0 else self.ropeks
        A.push()
        rc = A.alloc("rc", n)
        rs = A.alloc("rs", n)
        p.dma("qACT", rc, rc[0:128, 0:n], tc, tc.t[:, rope_c0:rope_c0 + n])
        p.dma("qACT", rs, rs[0:128, 0:n], tsn, tsn.t[:, rope_c0:rope_c0 + n])
        qs_lim = 99
        for d in self.dbg:
            if d.startswith("qkstop="):
                qs_lim = int(d.split("=")[1])
        wb = [A.alloc(f"wqk{i}", 1024) for i in range(2)]
        q2 = A.alloc("q2", n)
        qraw = A.alloc("qraw", n)
        s2 = A.alloc("s2", n)
        t2 = A.alloc("t2", n)
        for m4 in range(4):
            m = m4 + (0 if which == 0 else 4)
            w = wb[m4 % 2]
            p.dma("qSP", w, w[0:128, 0:1024], wq,
                  wq.t[m * 128:(m + 1) * 128, :])
            for n0 in range(0, n, 512):
                nn = min(512, n - n0)
                bank = self.pb[0 + (m4 % 2)]
                for kk in range(8):
                    sb_, sap = src_getter(kk)
                    sap = sap.slice_free(n0, nn) if hasattr(sap, "slice_free")                         else sap
                    p.mm(bank, bank[0:128, 0:nn], w,
                         w[0:128, 128 * kk:128 * (kk + 1)], sb_,
                         src_getter(kk, n0, nn)[1]
                         if src_getter.__code__.co_argcount > 1 else sap,
                         start=(kk == 0), stop=(kk == 7))
                if qs_lim < 2:
                    continue
                p.act(q2, q2[0:128, n0:n0 + nn], bank, bank[0:128, 0:nn],
                      AF.Square)
                if qs_lim < 3:
                    continue
                p.copy("DVE", qraw, qraw[0:128, n0:n0 + nn], bank,
                       bank[0:128, 0:nn])
                if qs_lim < 4:
                    continue
                bss = self.pb[6]
                p.mm(bss, bss[0:2, 0:nn], self.hsel, self.hsel[0:128, 0:2],
                     q2, q2[0:128, n0:n0 + nn], start=True, stop=True)
                if qs_lim < 5:
                    continue
                p.act(s2, s2[0:2, n0:n0 + nn], bss, bss[0:2, 0:nn], AF.Sqrt)
                p.ts("DVE", s2, s2[0:2, n0:n0 + nn], s2, s2[0:2, n0:n0 + nn],
                     1e-12, None, ALU.max)
                p.recip(t2, t2[0:2, n0:n0 + nn], s2, s2[0:2, n0:n0 + nn])
                if qs_lim < 6:
                    continue
                brb = self.pb[5]
                p.mm(brb, brb[0:128, 0:nn], self.hsel2,
                     self.hsel2[0:128, 0:128], t2,
                     self.A.t[0:128, t2.pcol + n0:t2.pcol + n0 + nn],
                     start=True, stop=True)
                if qs_lim < 7:
                    continue
                p.stt("DVE", qraw, qraw[0:128, n0:n0 + nn], qraw,
                      qraw[0:128, n0:n0 + nn],
                      (scv, scv[0:128, m4:m4 + 1]), brb, brb[0:128, 0:nn],
                      ALU.mult, ALU.mult)
                if qs_lim < 8:
                    continue
                brt = self.pb[7]
                p.mm(brt, brt[0:128, 0:nn], self.rot, self.rot[0:128, 0:128],
                     qraw, qraw[0:128, n0:n0 + nn], start=True, stop=True)
                d = dst[m4]
                p.tt("DVE", ALU.mult, d, d[0:128, n0:n0 + nn], qraw,
                     qraw[0:128, n0:n0 + nn], rc, rc[0:128, n0:n0 + nn])
                p.tt("DVE", ALU.mult, t2, t2[0:128, n0:n0 + nn], brt,
                     brt[0:128, 0:nn], rs, rs[0:128, n0:n0 + nn])
                p.tt("DVE", ALU.add, d, d[0:128, n0:n0 + nn], d,
                     d[0:128, n0:n0 + nn], t2, t2[0:128, n0:n0 + nn])
        A.pop()

    def _halo_kv(self, l, hb, kslot1, vslot1, khx, vhx):
        p, A = self.p, self.A
        A.push()
        def src(kk, n0=0, nn=KHALO):
            off = 4 if kk < 4 else 3
            return (hb[kk], hb[kk][0:128, off + n0:off + n0 + nn])
        kd = [A.alloc(f"hkd{pr}", KHALO) for pr in range(4)]
        if "hkv=v" not in self.dbg:
            self._qkv_chunks(l, src, KHALO, 1, kd, 0)
            for pr in range(4):
                p.copy("DVE", khx[pr], khx[pr][0:128, 0:128], kd[pr],
                       kd[pr][0:128, 0:128])
                p.copy("DVE", kslot1[pr], kslot1[pr][0:128, 0:512], kd[pr],
                       kd[pr][0:128, 128:640])
        if "hkv=qk" in self.dbg:
            A.pop()
            return
        wq = self.wqkv[l]
        wb = [A.alloc(f"wv{i}", 1024) for i in range(2)]
        wvt = [A.alloc(f"wvt{i}", 512) for i in range(2)]
        for tc5 in range(5):
            bank = self.pb[0 + (tc5 % 2)]
            for kk in range(8):
                w = wb[kk % 2]
                if tc5 == 0:
                    for mi, m in enumerate(range(8, 12)):
                        p.dma("qSP", self.wv[kk], 
                              self.wv[kk][0:128, 128 * mi:128 * (mi + 1)],
                              wq, wq.t[m * 128:(m + 1) * 128,
                                       128 * kk:128 * (kk + 1)])
                off = 4 if kk < 4 else 3
                p.mm(bank, bank[0:128, 0:512], hb[kk],
                     hb[kk][0:128, off + 128 * tc5:off + 128 * tc5 + 128],
                     self.wv[kk], self.wv[kk][0:128, 0:512],
                     start=(kk == 0), stop=(kk == 7))
            dst = vhx if tc5 == 0 else vslot1[tc5 - 1]
            p.copy("DVE", dst, dst[0:128, 0:512], bank, bank[0:128, 0:512])
        A.pop()

    def _halo_attn(self, l, hb, kslot1, vslot1, khx, vhx, x1a):
        p, A = self.p, self.A
        A.push()
        def src(kk, n0=0, nn=128):
            off = 516 if kk < 4 else 515
            return (hb[kk], hb[kk][0:128, off + n0:off + n0 + nn])
        qd = [A.alloc(f"hq{pr}", 128) for pr in range(4)]
        self._qkv_chunks(l, src, 128, 0, qd, 0)
        hm = A.alloc("hm", KHALO)
        p.dma("qACT", hm, hm[0:128, 0:KHALO], self.hmask, self.hmask.t[:, :])
        ot = [A.alloc(f"ho{j}", 128) for j in range(4)]
        PT = [A.alloc(f"hPT{c}", 128) for c in range(5)]
        Sm = A.alloc("hSm", KHALO)
        rl = A.alloc("hrl", 2)
        for hh in range(NH):
            pr, par = hh // 2, hh % 2
            qp = self.qpadO if par else self.qpadE
            p.copy("DVE", qp, qp[64 * par:64 * par + 64, 0:128], qd[pr],
                   qd[pr][64 * par:64 * par + 64, 0:128])
            b0, b1 = self.pb[0], self.pb[1]
            p.mm(b0, b0[0:128, 0:128], qp, qp[0:128, 0:128],
                 khx[pr], khx[pr][0:128, 0:128], start=True, stop=True)
            p.mm(b1, b1[0:128, 0:512], qp, qp[0:128, 0:128],
                 kslot1[pr], kslot1[pr][0:128, 0:512], start=True, stop=True)
            p.tt("DVE", ALU.add, Sm, Sm[0:128, 0:128], b0, b0[0:128, 0:128],
                 hm, hm[0:128, 0:128])
            p.tt("DVE", ALU.add, Sm, Sm[0:128, 128:640], b1,
                 b1[0:128, 0:512], hm, hm[0:128, 128:640])
            p.reduce("DVE", rl, rl[0:128, 0:1], Sm, Sm[0:128, 0:KHALO],
                     op=ALU.max, negate=True)
            p.act(Sm, Sm[0:128, 0:KHALO], Sm, Sm[0:128, 0:KHALO], AF.Exp,
                  bias=(rl, rl[0:128, 0:1]), accum=(rl, rl[0:128, 1:2]))
            p.recip(rl, rl[0:128, 0:1], rl, rl[0:128, 1:2])
            p.act(Sm, Sm[0:128, 0:KHALO], Sm, Sm[0:128, 0:KHALO], AF.Copy,
                  scale=(rl, rl[0:128, 0:1]))
            for c in range(5):
                bt = self.pb[2 + (c % 2)]
                p.transpose(bt, bt[0:128, 0:128], Sm,
                            Sm[0:128, 128 * c:128 * (c + 1)], self.ident)
                p.copy("ACT", PT[c], PT[c][0:128, 0:128], bt,
                       bt[0:128, 0:128])
            bat = self.pb[4]
            for c in range(5):
                vsrc = vhx if c == 0 else vslot1[c - 1]
                p.mm(bat, bat[64 * par:64 * par + 64, 0:128], vsrc,
                     vsrc[0:128, 64 * hh:64 * hh + 64], PT[c],
                     PT[c][0:128, 0:128], start=(c == 0), stop=(c == 4))
            if par == 1:
                p.copy("DVE", ot[pr], ot[pr][0:128, 0:128], bat,
                       bat[0:128, 0:128])
        wb = [A.alloc(f"hwo{i}", 512) for i in range(2)]
        for m in range(8):
            w = wb[m % 2]
            p.dma("qSP", w, w[0:128, 0:512], self.wo[l],
                  self.wo[l].t[m * 128:(m + 1) * 128, :])
            bank = self.pb[5 + (m % 2)]
            for kk in range(4):
                p.mm(bank, bank[0:128, 0:128], w,
                     w[0:128, 128 * kk:128 * (kk + 1)],
                     ot[kk], ot[kk][0:128, 0:128], start=(kk == 0),
                     stop=(kk == 3))
            p.tt("DVE", ALU.add, x1a, x1a[0:128, 0:128], bank,
                 bank[0:128, 0:128], hb[m], hb[m][0:128, 516:644])
            p.ts("DVE", x1a, x1a[0:128, 0:128], x1a, x1a[0:128, 0:128],
                 (self.hgate, self.hgate[0:128, 0:1]), None, ALU.mult)
            p.copy("DVE", self.x1acol, self.x1acol[0:128, m:m + 1], x1a,
                   x1a[0:128, 127:128])
        A.pop()

    def _ffn_block(self, l, w, xin_d, kslot, vslot):
        p, A = self.p, self.A
        qb = 512 * w
        A.push()
        xt = [A.alloc(f"fx{j}", 513) for j in range(8)]
        for j in range(8):
            if w == 0:
                p.dma("qACT", xt[j], xt[j][0:128, 1:513], xin_d,
                      xin_d.t[128 * j:128 * (j + 1), 0:512])
                p.copy("DVE", xt[j], xt[j][0:128, 0:1], self.hbcol,
                       self.hbcol[0:128, j:j + 1])
            else:
                p.dma("qACT", xt[j], xt[j][0:128, 0:513], xin_d,
                      xin_d.t[128 * j:128 * (j + 1), qb - 1:qb + 512])
        def src(kk, n0=0, nn=512):
            off = 1 if kk < 4 else 0
            return (xt[kk], xt[kk][0:128, off + n0:off + n0 + nn])
        sl = w % 2
        A.push()
        qd = [A.alloc(f"fq{pr}", 512) for pr in range(4)]
        self._qkv_chunks(l, src, 512, 0, qd, 128 + qb)
        self._qkv_chunks(l, src, 512, 1, kslot[sl], KHALO + qb)
        for tc4 in range(4):
            bank = self.pb[0 + (tc4 % 2)]
            for kk in range(8):
                off = 1 if kk < 4 else 0
                p.mm(bank, bank[0:128, 0:512], xt[kk],
                     xt[kk][0:128, off + 128 * tc4:off + 128 * tc4 + 128],
                     self.wv[kk], self.wv[kk][0:128, 0:512],
                     start=(kk == 0), stop=(kk == 7))
            d = vslot[sl][tc4]
            p.copy("DVE", d, d[0:128, 0:512], bank, bank[0:128, 0:512])
        prv = 1 - sl
        ot = [A.alloc(f"fo{j}", 512) for j in range(4)]
        A.push()
        if w == 0:
            mkb = [A.alloc(f"mkb{st}", 1024) for st in range(4)]
            for st in range(4):
                p.dma("qACT", mkb[st], mkb[st][0:128, 0:1024], self.blk0,
                      self.blk0.t[128 * st:128 * (st + 1), :])
        else:
            mkb = None
        PT = [A.alloc(f"PT{c}", 512) for c in range(8)]
        Sm = A.alloc("Sm", 1024)
        rl = A.alloc("rl", 2)
        bandb = self.bandb
        for hh in range(NH):
            pr, par = hh // 2, hh % 2
            qp = self.qpadO if par else self.qpadE
            p.copy("DVE", qp, qp[64 * par:64 * par + 64, 0:512], qd[pr],
                   qd[pr][64 * par:64 * par + 64, 0:512])
            for st in range(4):
                b0, b1 = self.pb[0], self.pb[1]
                p.mm(b0, b0[0:128, 0:512], qp,
                     qp[0:128, 128 * st:128 * st + 128],
                     kslot[prv][pr], kslot[prv][pr][0:128, 0:512],
                     start=True, stop=True)
                p.mm(b1, b1[0:128, 0:512], qp,
                     qp[0:128, 128 * st:128 * st + 128],
                     kslot[sl][pr], kslot[sl][pr][0:128, 0:512],
                     start=True, stop=True)
                if w == 0:
                    p.tt("DVE", ALU.add, Sm, Sm[0:128, 0:512], b0,
                         b0[0:128, 0:512], mkb[st], mkb[st][0:128, 0:512])
                    p.tt("DVE", ALU.add, Sm, Sm[0:128, 512:1024], b1,
                         b1[0:128, 0:512], mkb[st], mkb[st][0:128, 512:1024])
                else:
                    c0b = 384 - st * 128
                    p.tt("DVE", ALU.add, Sm, Sm[0:128, 0:512], b0,
                         b0[0:128, 0:512], bandb, bandb[0:128, c0b:c0b + 512])
                    p.tt("DVE", ALU.add, Sm, Sm[0:128, 512:1024], b1,
                         b1[0:128, 0:512], bandb,
                         bandb[0:128, c0b + 512:c0b + 1024])
                p.reduce("DVE", rl, rl[0:128, 0:1], Sm, Sm[0:128, 0:1024],
                         op=ALU.max, negate=True)
                p.act(Sm, Sm[0:128, 0:1024], Sm, Sm[0:128, 0:1024], AF.Exp,
                      bias=(rl, rl[0:128, 0:1]), accum=(rl, rl[0:128, 1:2]))
                p.recip(rl, rl[0:128, 0:1], rl, rl[0:128, 1:2])
                p.act(Sm, Sm[0:128, 0:1024], Sm, Sm[0:128, 0:1024], AF.Copy,
                      scale=(rl, rl[0:128, 0:1]))
                for c in range(8):
                    bt = self.pb[2 + (c % 2)]
                    p.transpose(bt, bt[0:128, 0:128], Sm,
                                Sm[0:128, 128 * c:128 * (c + 1)], self.ident)
                    p.copy("ACT", PT[c], PT[c][0:128, 128 * st:128 * st + 128],
                           bt, bt[0:128, 0:128])
            bat = self.pb[4]
            for c in range(8):
                vt = vslot[prv][c] if c < 4 else vslot[sl][c - 4]
                p.mm(bat, bat[64 * par:64 * par + 64, 0:512], vt,
                     vt[0:128, 64 * hh:64 * hh + 64], PT[c],
                     PT[c][0:128, 0:512], start=(c == 0), stop=(c == 7))
            if par == 1:
                p.copy("DVE", ot[pr], ot[pr][0:128, 0:512], bat,
                       bat[0:128, 0:512])
        A.pop()
        wb = [A.alloc(f"fwo{i}", 512) for i in range(2)]
        for m in range(8):
            wv_ = wb[m % 2]
            p.dma("qSP", wv_, wv_[0:128, 0:512], self.wo[l],
                  self.wo[l].t[m * 128:(m + 1) * 128, :])
            bank = self.pb[5 + (m % 2)]
            for kk in range(4):
                p.mm(bank, bank[0:128, 0:512], wv_,
                     wv_[0:128, 128 * kk:128 * (kk + 1)],
                     ot[kk], ot[kk][0:128, 0:512],
                     start=(kk == 0), stop=(kk == 3))
            p.tt("DVE", ALU.add, xt[m], xt[m][0:128, 1:513], bank,
                 bank[0:128, 0:512], xt[m], xt[m][0:128, 1:513])
        A.pop()
        for m in range(8):
            if w == 0:
                p.copy("DVE", xt[m], xt[m][0:128, 0:1], self.x1acol,
                       self.x1acol[0:128, m:m + 1])
            else:
                p.copy("DVE", xt[m], xt[m][0:128, 0:1], self.x1carry,
                       self.x1carry[0:128, m:m + 1])
        for m in range(8):
            p.copy("DVE", self.x1carry, self.x1carry[0:128, m:m + 1],
                   xt[m], xt[m][0:128, 512:513])
        A.push()
        x2t = A.alloc("fx2", 512)
        hmid = [A.alloc(f"hm{j}", 512) for j in range(22)]
        A.push()
        bs, bq = self.pb[0], self.pb[1]
        caps = []
        for j in range(8):
            off = 1 if j < 4 else 0
            caps.append(xt[j][0:128, off:off + 512])
        self.ln_stats(xt, caps, 512, bs, bq, x2t)
        self.ln_rows(bs, bq, 512, self.rinv, self.mur, eps_idx=0, dinv_idx=2)
        bmu, br = self.pb[2], self.pb[3]
        self.bcast(bmu, self.mur, 0, 512)
        self.bcast(br, self.rinv, 0, 512)
        hn = [A.alloc(f"hn{j}", 512) for j in range(8)]
        for j in range(8):
            p.tt("DVE", ALU.mult, hn[j], hn[j][0:128, 0:512], xt[j], caps[j],
                 br, br[0:128, 0:512])
            p.tt("DVE", ALU.subtract, hn[j], hn[j][0:128, 0:512], hn[j],
                 hn[j][0:128, 0:512], bmu, bmu[0:128, 0:512])
            p.ts("DVE", hn[j], hn[j][0:128, 0:512], hn[j],
                 hn[j][0:128, 0:512],
                 (self.lg[l], self.lg[l][0:128, j:j + 1]),
                 (self.lbv[l], self.lbv[l][0:128, j:j + 1]),
                 ALU.mult, ALU.add)
        wb1 = [A.alloc(f"fw1_{i}", 1024) for i in range(2)]
        gel = A.alloc("gel", 512)
        for j in range(22):
            for half in range(2):
                m = 2 * j + half
                wv_ = wb1[m % 2]
                p.dma("qSP", wv_, wv_[0:128, 0:1024], self.w1[l],
                      self.w1[l].t[m * 128:(m + 1) * 128, :])
                bank = self.pb[4 + half]
                for kk in range(8):
                    p.mm(bank, bank[0:128, 0:512], wv_,
                         wv_[0:128, 128 * kk:128 * (kk + 1)], hn[kk],
                         hn[kk][0:128, 0:512], start=(kk == 0), stop=(kk == 7))
            nrow = 128 if j < 21 else HID - 128 * 21
            bA, bG = self.pb[4], self.pb[5]
            p.act(gel, gel[0:nrow, 0:512], bG, bG[0:nrow, 0:512], AF.Gelu,
                  bias=(self.b1[l], self.b1[l][0:nrow, 2 * j + 1:2 * j + 2]))
            p.ts("DVE", hmid[j], hmid[j][0:nrow, 0:512], bA, bA[0:nrow, 0:512],
                 (self.b1[l], self.b1[l][0:nrow, 2 * j:2 * j + 1]), None,
                 ALU.add)
            p.tt("DVE", ALU.mult, hmid[j], hmid[j][0:nrow, 0:512], hmid[j],
                 hmid[j][0:nrow, 0:512], gel, gel[0:nrow, 0:512])
            if nrow < 128:
                p.memset("DVE", hmid[j], hmid[j][64:128, 0:512], 0.0)
        A.pop()
        wb2 = [A.alloc(f"fw2_{i}", 2816) for i in range(2)]
        for m in range(8):
            wv_ = wb2[m % 2]
            p.dma("qSP", wv_, wv_[0:128, 0:2816], self.w2[l],
                  self.w2[l].t[m * 128:(m + 1) * 128, :])
            bank = self.pb[6 + (m % 2)]
            for kk in range(22):
                p.mm(bank, bank[0:128, 0:512], wv_,
                     wv_[0:128, 128 * kk:128 * (kk + 1)], hmid[kk],
                     hmid[kk][0:128, 0:512], start=(kk == 0), stop=(kk == 21))
            p.stt("DVE", x2t, x2t[0:128, 0:512], bank, bank[0:128, 0:512],
                  (self.b2[l], self.b2[l][0:128, m:m + 1]),
                  xt[m], xt[m][0:128, 1:513], ALU.add, ALU.add)
            if l == 0:
                p.dma("qPOOL", self.xffn_b,
                      self.xffn_b.t[128 * m:128 * (m + 1), qb:qb + 512],
                      x2t, x2t[0:128, 0:512])
                g0 = T - HALO
                lo = max(g0, qb)
                if lo < qb + 512:
                    o = lo - qb
                    p.dma("qPOOL", self.e3_in,
                          self.e3_in.t[128 * m:128 * (m + 1),
                                       lo - g0:qb + 512 - g0],
                          x2t, x2t[0:128, o:512])
            else:
                xm = A.alloc("xmt", 512) if m == 0 else xm
                p.dma("qACT", xm, xm[0:128, 0:512], self.xmid,
                      self.xmid.t[128 * m:128 * (m + 1), qb:qb + 512])
                p.stt("DVE", x2t, x2t[0:128, 0:512], x2t, x2t[0:128, 0:512],
                      (self.mod, self.mod[0:128, 40 + m:41 + m]),
                      xm, xm[0:128, 0:512], ALU.mult, ALU.add)
                p.dma("qPOOL", self.out_fm,
                      self.out_fm.t[128 * m:128 * (m + 1), qb:qb + 512],
                      x2t, x2t[0:128, 0:512])
        A.pop()
        A.pop()

    def finish(self):
        pass


def build(dbg=(), upto="mamba_a"):
    nc = bass.Bass("TRN2", target_bir_lowering=False, debug=False)
    p = Prog(nc)
    k = Kern(p, dbg)
    k.setup()
    k.mod_vector()
    stages = ["mamba_a", "e1", "mamba_b", "ffn1", "ffn2", "finish"]
    iu = stages.index(upto)
    k.mamba_a()
    if iu >= 1:
        k.e1_exchange()
    if iu >= 2:
        k.mamba_b()
    if iu >= 3:
        k.ffn_layer(0)
    if iu >= 4:
        k.ffn_layer(1)
    if iu >= 5:
        k.finish()
    p.wait_all("POOL", [s for s in p.cnt if s.startswith("q")] + ["cc"])
    p.finalize()
    return nc, p, k


def kernel(**inputs):
    maps = stage_inputs(inputs)
    nc, p, k = build(upto="finish")
    res = run_bass_kernel_spmd(nc, maps, core_ids=list(range(8)))
    out = np.empty((NB, 8192, D), np.float32)
    for c in range(8):
        b, q = c // 4, c % 4
        out[b, T * q:T * (q + 1), :] = np.asarray(res.results[c]["out_fm"]).T
    return out


# version 04
# speedup vs baseline: 1.0089x; 1.0089x over previous
import contextlib
import math

import numpy as np

import concourse.bass as bass
import concourse.mybir as mybir
from concourse.bass_utils import run_bass_kernel_spmd

F32 = mybir.dt.float32
BF16 = mybir.dt.bfloat16
AF = mybir.ActivationFunctionType
ALU = mybir.AluOpType
AX = mybir.AxisListType

NEG_BIG = -3.0e38


ENGS = ("PE", "ACT", "DVE", "POOL", "SP")
_BLOCK_ATTR = {"PE": "tensor", "ACT": "scalar", "DVE": "vector",
               "POOL": "gpsimd", "SP": "sync"}


class Buf:
    __slots__ = ("name", "t", "last_w", "readers", "prow", "pcol", "ncol",
                 "space", "init_deps")

    def __init__(self, name, t, space, prow=0, pcol=0, ncol=None):
        self.name = name
        self.t = t
        self.space = space
        self.prow = prow
        self.pcol = pcol
        self.ncol = ncol
        self.last_w = {}
        self.readers = {}
        self.init_deps = {}

    def __getitem__(self, idx):
        if not isinstance(idx, tuple):
            idx = (idx, slice(None))
        pidx, cidx = idx
        p0, p1 = pidx.start or 0, pidx.stop if pidx.stop is not None else 128
        c0 = cidx.start or 0
        c1 = cidx.stop if cidx.stop is not None else self.ncol
        return self.t[self.prow + p0:self.prow + p1,
                      self.pcol + c0:self.pcol + c1]

    @property
    def ap(self):
        return self[:, :]


class Arena:

    def __init__(self, t, ncols, name):
        self.t = t
        self.ncols = ncols
        self.name = name
        self.free_at = 0
        self.live = []
        self.dead = []
        self.scopes = []

    def push(self):
        self.scopes.append(self.free_at)

    def pop(self):
        base = self.scopes.pop()
        still = []
        for (s, e, b) in self.live:
            if s >= base:
                deps = dict(b.last_w)
                _dmerge(deps, b.readers)
                self.dead.append((s, e, deps))
            else:
                still.append((s, e, b))
        self.live = still
        self.free_at = base

    def alloc(self, name, ncol, align=1):
        start = self.free_at
        if align > 1:
            start = (start + align - 1) // align * align
        end = start + ncol
        if end > self.ncols:
            raise MemoryError(
                f"arena {self.name} overflow: need {ncol} at {start}, "
                f"cap {self.ncols}")
        self.free_at = end
        b = Buf(name, self.t, "sbuf", prow=0, pcol=start, ncol=ncol)
        deps = {}
        for (s, e, d) in self.dead:
            if s < end and e > start:
                _dmerge(deps, d)
        b.init_deps = deps
        self.live.append((start, end, b))
        return b


def _dmerge(dst, src_):
    for s, c in src_.items():
        if dst.get(s, -1) < c:
            dst[s] = c


class Prog:
    def __init__(self, nc):
        self.nc = nc
        self.es = contextlib.ExitStack()
        self.cnt = {}
        self.know = {e: {} for e in ENGS}
        self.sems = {}
        self.q = {e: [] for e in ENGS}
        self.ninst = {e: 0 for e in ENGS}
        for e in ENGS:
            self._mk_sem("s" + e)
        self.NDSEM = 8
        for qn in ("qSP", "qPOOL", "qACT"):
            for i in range(self.NDSEM):
                self._mk_sem(qn + str(i))
        self._mk_sem("cc")
        self.dma_queue_eng = {"qSP": "SP", "qPOOL": "POOL", "qACT": "ACT"}
        self.dma_rr = {"qSP": 0, "qPOOL": 0, "qACT": 0}
        self.dma_inflight = {}
        self.dram_tensors = {}

    def _mk_sem(self, name):
        self.sems[name] = self.es.enter_context(self.nc.semaphore(name))
        self.cnt[name] = 0

    def sbuf(self, name, shape, dtype=F32):
        t = self.es.enter_context(self.nc.sbuf_tensor(name, list(shape), dtype))
        return Buf(name, t, "sbuf", ncol=shape[1])

    def arena(self, name, ncols, dtype=F32):
        t = self.es.enter_context(self.nc.sbuf_tensor(name, [128, ncols], dtype))
        return Arena(t, ncols, name)

    def psum(self, name):
        t = self.es.enter_context(self.nc.psum_tensor(name, [128, 512], F32))
        return Buf(name, t, "psum", ncol=512)

    def din(self, name, shape, dtype=F32):
        t = self.nc.dram_tensor(name, list(shape), dtype, kind="ExternalInput")
        b = Buf(name, t, "dram", ncol=shape[-1])
        self.dram_tensors[name] = b
        return b

    def dout(self, name, shape, dtype=F32):
        t = self.nc.dram_tensor(name, list(shape), dtype, kind="ExternalOutput")
        b = Buf(name, t, "dram", ncol=shape[-1])
        self.dram_tensors[name] = b
        return b

    def dram(self, name, shape, dtype=F32):
        t = self.nc.dram_tensor(name, list(shape), dtype)
        b = Buf(name, t, "dram", ncol=shape[-1])
        self.dram_tensors[name] = b
        return b

    def _waits_for(self, eng, reads, writes):
        need = {}
        for b in reads:
            _dmerge(need, b.last_w)
            if b.space == "psum":
                _dmerge(need, b.readers)
        for b in writes:
            _dmerge(need, b.last_w)
            _dmerge(need, b.readers)
            _dmerge(need, b.init_deps)
        out = []
        for s, c in need.items():
            if self.know[eng].get(s, 0) < c:
                out.append((s, c))
                self.know[eng][s] = c
        return out

    def _post(self, sem, reads, writes):
        n = self.cnt[sem]
        for b in writes:
            _dmerge(b.last_w, {sem: n})
        for b in reads:
            _dmerge(b.readers, {sem: n})

    def op(self, eng, fn, reads=(), writes=(), acc=False):
        reads = list(reads)
        writes = list(writes)
        if acc:
            waits = self._waits_for(eng, reads, [])
            extra = {}
            for b in writes:
                _dmerge(extra, b.readers)
                _dmerge(extra, b.init_deps)
            for s, c in extra.items():
                if self.know[eng].get(s, 0) < c:
                    waits.append((s, c))
                    self.know[eng][s] = c
        else:
            waits = self._waits_for(eng, reads, writes)
        sem = "s" + eng
        self.cnt[sem] += 1
        n = self.cnt[sem]
        semh = self.sems
        wl = list(waits)

        def emit(engobj, fn=fn, wl=wl, sem=sem, n=n):
            for (s, c) in wl:
                engobj.wait_ge(semh[s], c)
            ins = fn(engobj)
            ins.then_inc(semh[sem], 1)
        self.q[eng].append(emit)
        self.ninst[eng] += 1 + len(wl)
        self._post(sem, reads, writes)
        return n

    def dma(self, queue, out, out_ap, in_, in_ap, **kw):
        eng = self.dma_queue_eng[queue]
        i = self.dma_rr[queue]
        self.dma_rr[queue] = (i + 1) % self.NDSEM
        sname = queue + str(i)
        waits = self._waits_for(eng, [in_], [out])
        prev = self.dma_inflight.get(sname)
        if prev is not None and self.know[eng].get(sname, 0) < prev:
            waits.append((sname, prev))
            self.know[eng][sname] = prev
        self.cnt[sname] += 16
        n = self.cnt[sname]
        self.dma_inflight[sname] = n
        semh = self.sems

        def emit(engobj, out_ap=out_ap, in_ap=in_ap, wl=list(waits), kw=kw):
            for (s, c) in wl:
                engobj.wait_ge(semh[s], c)
            engobj.dma_start(out=out_ap, in_=in_ap, **kw).then_inc(
                semh[sname], 16)
        self.q[eng].append(emit)
        self.ninst[eng] += 1 + len(waits)
        _dmerge(out.last_w, {sname: n})
        _dmerge(in_.readers, {sname: n})
        return (sname, n)

    def wait_all(self, eng, sems=None):
        wl = []
        for s in (sems or list(self.cnt)):
            c = self.cnt[s]
            if c > 0 and self.know[eng].get(s, 0) < c:
                wl.append((s, c))
                self.know[eng][s] = c
        semh = self.sems

        def emit(engobj, wl=wl):
            for (s, c) in wl:
                engobj.wait_ge(semh[s], c)
        self.q[eng].append(emit)
        self.ninst[eng] += len(wl)

    def mm(self, psum, pap, lhsT, lap, rhs, rap, start, stop, extra_reads=()):
        return self.op(
            "PE", lambda t: t.matmul(pap, lap, rap, start=start, stop=stop),
            reads=[lhsT, rhs, *extra_reads], writes=[psum], acc=not start)

    def transpose(self, psum, pap, src, sap, ident):
        return self.op(
            "PE", lambda t: t.transpose(pap, sap, ident.ap),
            reads=[src, ident], writes=[psum])

    def copy(self, eng, dst, dap, src, sap):
        if eng == "ACT":
            return self.op(eng, lambda e: e.copy(dap, sap),
                           reads=[src], writes=[dst])
        return self.op(eng, lambda e: e.tensor_copy(dap, sap),
                       reads=[src], writes=[dst])

    def memset(self, eng, dst, dap, val=0.0):
        return self.op(eng, lambda e: e.memset(dap, val), writes=[dst])

    def tt(self, eng, aop, dst, dap, a, aap, b, bap):
        return self.op(eng, lambda e: e.tensor_tensor(dap, aap, bap, aop),
                       reads=[a, b], writes=[dst])

    def ts(self, eng, dst, dap, a, aap, s1, s2=None, op0=ALU.mult,
           op1=ALU.bypass, sbufs=()):
        reads = [a]
        v1, v2 = s1, s2
        if isinstance(s1, tuple):
            reads.append(s1[0]); v1 = s1[1]
        if isinstance(s2, tuple):
            reads.append(s2[0]); v2 = s2[1]
        return self.op(eng, lambda e: e.tensor_scalar(dap, aap, v1, v2, op0, op1),
                       reads=reads, writes=[dst])

    def stt(self, eng, dst, dap, a, aap, scal, b, bap, op0, op1):
        reads = [a, b]
        v = scal
        if isinstance(scal, tuple):
            reads.append(scal[0]); v = scal[1]
        return self.op(
            eng, lambda e: e.scalar_tensor_tensor(dap, aap, v, bap, op0, op1),
            reads=reads, writes=[dst])

    def act(self, dst, dap, src, sap, func, bias=0.0, scale=1.0, accum=None):
        reads = [src]
        writes = [dst]
        bv, sv = bias, scale
        if isinstance(bias, tuple):
            reads.append(bias[0]); bv = bias[1]
        if isinstance(scale, tuple):
            reads.append(scale[0]); sv = scale[1]
        aap = None
        if accum is not None:
            writes.append(accum[0]); aap = accum[1]
        return self.op(
            "ACT",
            lambda e: e.activation(dap, sap, func, bias=bv, scale=sv,
                                   accum_out=aap),
            reads=reads, writes=writes)

    def reduce(self, eng, dst, dap, src, sap, op=ALU.max, negate=False):
        return self.op(
            eng, lambda e: e.tensor_reduce(dap, sap, AX.X, op, negate=negate),
            reads=[src], writes=[dst])

    def recip(self, dst, dap, src, sap):
        return self.op("DVE", lambda e: e.reciprocal(dap, sap),
                       reads=[src], writes=[dst])

    def scan(self, eng, dst, dap, d0, d0ap, d1, d1ap, init=0.0):
        return self.op(
            eng, lambda e: e.tensor_tensor_scan(dap, d0ap, d1ap, init,
                                                ALU.mult, ALU.add),
            reads=[d0, d1], writes=[dst])

    def collective(self, kind, op, groups, in_b, out_b):
        waits = self._waits_for("POOL", [in_b], [out_b])
        self.cnt["cc"] += 1
        n = self.cnt["cc"]
        semh = self.sems

        def emit(engobj, wl=list(waits)):
            for (s, c) in wl:
                engobj.wait_ge(semh[s], c)
            engobj.collective_compute(
                kind, op, replica_groups=groups,
                ins=[in_b.t.ap().opt()], outs=[out_b.t.ap().opt()],
            ).then_inc(semh["cc"], 1)
        self.q["POOL"].append(emit)
        self.ninst["POOL"] += 1 + len(waits)
        _dmerge(out_b.last_w, {"cc": n})
        _dmerge(in_b.readers, {"cc": n})
        return n

    def finalize(self):
        with self.nc.Block() as block:
            for e in ENGS:
                fns = self.q[e]
                if not fns:
                    continue
                def body(engobj, fns=fns):
                    for f in fns:
                        f(engobj)
                getattr(block, _BLOCK_ATTR[e])(body)
        self.es.close()
        return self.nc


D = 1024
T = 2048
NB = 2
NQ = 4
DIN = 2048
NH = 8
HD = 256
NST = 64
QCH = 128
MTILE = 512
HALO = 644
KHALO = 640
WIN = 512
AHD = 64
AH = 8
HID = 2730
HIDP = 2816


def _chunked(w, kdim=128, mdim=128):
    K, M = w.shape
    nk = (K + kdim - 1) // kdim
    nm = (M + mdim - 1) // mdim
    wp = np.zeros((nk * kdim, nm * mdim), np.float32)
    wp[:K, :M] = w
    out = np.zeros((nm * 128, nk * mdim), np.float32)
    for m in range(nm):
        blk = wp[:, m * mdim:(m + 1) * mdim]
        out[m * 128:(m + 1) * 128] = (
            blk.reshape(nk, 128, mdim).transpose(1, 0, 2).reshape(128, nk * mdim))
    return np.ascontiguousarray(out)


def stage_inputs(inputs):
    x = np.asarray(inputs["L0_attn_x"], np.float32)
    maps = []
    ipw = np.asarray(inputs["L0_attn_in_proj_w"], np.float32)
    perm = np.concatenate([np.arange(0, 4096),
                           np.arange(4168, 4232),
                           np.arange(4104, 4168),
                           np.arange(4096, 4104)])
    wip = _chunked(np.ascontiguousarray(ipw.T[:, perm]))
    wop = _chunked(np.ascontiguousarray(
        np.asarray(inputs["L0_attn_out_proj_w"], np.float32).T))
    wmod = _chunked(np.asarray(inputs["L0_wmod"], np.float32))
    wqkv = [_chunked(np.asarray(inputs["L0_ffn_wqkv"], np.float32)[l])
            for l in range(2)]
    wo = [_chunked(np.asarray(inputs["L0_ffn_wo"], np.float32)[l])
          for l in range(2)]
    w1p, b1p = [], []
    for l in range(2):
        w1l = np.asarray(inputs["L0_ffn_w1"], np.float32)[l]
        b1l = np.asarray(inputs["L0_ffn_b1"], np.float32)[l]
        wpad = np.zeros((1024, 44 * 128), np.float32)
        bb = np.zeros((128, 44), np.float32)
        for j in range(22):
            a0, a1 = 128 * j, min(128 * (j + 1), HID)
            n = a1 - a0
            wpad[:, 256 * j:256 * j + n] = w1l[:, a0:a1]
            wpad[:, 256 * j + 128:256 * j + 128 + n] = w1l[:, HID + a0:HID + a1]
            bb[:n, 2 * j] = b1l[a0:a1]
            bb[:n, 2 * j + 1] = b1l[HID + a0:HID + a1]
        w1p.append(_chunked(wpad))
        b1p.append(bb)
    w2 = [_chunked(np.pad(np.asarray(inputs["L0_ffn_w2"], np.float32)[l],
                          ((0, HIDP - HID), (0, 0))))
          for l in range(2)]
    def fm(v, nch):
        return np.ascontiguousarray(
            np.asarray(v, np.float32).reshape(nch, 128).T)
    lg = [fm(np.asarray(inputs["L0_ffn_ln_g"], np.float32)[l], 8) for l in range(2)]
    lb = [fm(np.asarray(inputs["L0_ffn_ln_b"], np.float32)[l], 8) for l in range(2)]
    b2 = [fm(np.asarray(inputs["L0_ffn_b2"], np.float32)[l], 8) for l in range(2)]
    qsc = [fm(np.tile(np.asarray(inputs["L0_ffn_q_scale"], np.float32)[l], 8), 4)
           for l in range(2)]
    ksc = [fm(np.tile(np.asarray(inputs["L0_ffn_k_scale"], np.float32)[l], 8), 4)
           for l in range(2)]
    convw = np.ascontiguousarray(
        np.asarray(inputs["L0_attn_conv_w"], np.float32).reshape(16, 128, 4)
        .transpose(1, 0, 2).reshape(128, 64))
    convb = fm(inputs["L0_attn_conv_b"], 16)
    nw = fm(inputs["L0_attn_norm_w"], 16)
    Dp = np.tile(np.asarray(inputs["L0_attn_D_param"], np.float32)[None, :],
                 (128, 1))
    Alog = np.asarray(inputs["L0_attn_A_log"], np.float32).reshape(8, 1)
    dtb = np.asarray(inputs["L0_attn_dt_bias"], np.float32).reshape(8, 1)
    ident = np.eye(128, dtype=np.float32)
    sel = np.zeros((128, 8 * 128), np.float32)
    for r in range(8):
        sel[r, r * 128:(r + 1) * 128] = 1.0
    hsel2 = np.zeros((128, 128), np.float32)
    hsel2[0, :64] = 1.0
    hsel2[1, 64:] = 1.0
    hsel = np.zeros((128, 2), np.float32)
    hsel[:64, 0] = 1.0
    hsel[64:, 1] = 1.0
    rot = np.zeros((128, 128), np.float32)
    for b0 in (0, 64):
        for d_ in range(32):
            rot[b0 + d_ + 32, b0 + d_] = -1.0
            rot[b0 + d_, b0 + d_ + 32] = 1.0
    tri = np.tril(np.ones((128, 128), np.float32)).T
    band = np.full((128, 1408), NEG_BIG, np.float32)
    for p in range(128):
        v0, v1 = 384 + p, 384 + p + 512
        band[p, v0:v1 + 1] = 0.0
    krow = np.array([[1e-5, 1e-6, 1.0 / D, 1.0 / DIN, 0.25, 1.0, 0.0, 1e-12]],
                    np.float32)
    kcol = np.tile(krow, (128, 1)).astype(np.float32)
    mrow = np.full((1, 512), -1.0, np.float32)
    N_FULL = 8192
    cosf, sinf, scalef = _rope_tables(N_FULL)
    for c in range(8):
        b, q = c // 4, c % 4
        s = T * q
        m = {}
        xt = x[b].T
        xh3 = np.zeros((1024, T + 3), np.float32)
        lo = max(0, s - 3)
        xh3[:, 3 - (s - lo):] = xt[:, lo:s + T]
        m["xin"] = np.ascontiguousarray(xh3)
        m.update(wip=wip, wop=wop, wmod=wmod,
                 wqkv0=wqkv[0], wqkv1=wqkv[1], wo0=wo[0], wo1=wo[1],
                 w10=w1p[0], w11=w1p[1], w20=w2[0], w21=w2[1],
                 b10=b1p[0], b11=b1p[1], b20=b2[0], b21=b2[1],
                 lg0=lg[0], lg1=lg[1], lb0=lb[0], lb1=lb[1],
                 qsc0=qsc[0], qsc1=qsc[1], ksc0=ksc[0], ksc1=ksc[1],
                 convw=convw, convb=convb, nw=nw, Dp=Dp, Alog=Alog, dtb=dtb,
                 ident=ident, sel=sel, hsel2=hsel2, hsel=hsel, rot=rot,
                 tri=tri, band=band, krow=krow, kcol=kcol, mrow=mrow)
        m["cvec"] = fm(np.asarray(inputs["L0_c"], np.float32)[b], 8)
        def ftab(tab, pos):
            v = tab[np.clip(pos, 0, N_FULL - 1)]
            return np.ascontiguousarray(np.tile(v.T, (2, 1)).astype(np.float32))
        posq = np.arange(s - 128, s + T)
        posk = np.arange(s - KHALO, s + T)
        m["ropeqc"] = ftab(cosf * scalef * 8.0, posq)
        m["ropeqs"] = ftab(sinf * scalef * 8.0, posq)
        m["ropekc"] = ftab(cosf / scalef, posk)
        m["ropeks"] = ftab(sinf / scalef, posk)
        blk0 = np.zeros((4 * 128, 1024), np.float32)
        for st in range(4):
            sl = band[:, 384 - st * 128:384 - st * 128 + 1024].copy()
            if q == 0:
                sl[:, :512] = NEG_BIG
            blk0[st * 128:(st + 1) * 128] = sl
        m["blk0"] = blk0
        hm = np.full((128, KHALO), NEG_BIG, np.float32)
        if q > 0:
            for p in range(128):
                hm[p, p:p + 513] = 0.0
        m["hmask"] = hm
        m["hgate"] = np.full((128, 1), 0.0 if q == 0 else 1.0, np.float32)
        g = np.zeros((128, 4), np.float32)
        g1m = np.zeros((128, 4), np.float32)
        for j in range(3):
            g[:, j] = 1.0 if j < q else 0.0
            g1m[:, j] = 0.0 if j < q else 1.0
        m["e1g"] = g
        m["e1g1m"] = g1m
        es = np.zeros((128, 4), np.float32)
        if q > 0:
            es[:, q - 1] = 1.0
        m["eselg"] = es
        maps.append(m)
    return maps


def _rope_tables(n):
    half = np.arange(0, AHD, 2, dtype=np.float64)
    inv_freq = 1.0 / 10000.0 ** (half / AHD)
    t = np.arange(n, dtype=np.float64)
    freqs = t[:, None] * inv_freq[None, :]
    freqs = np.concatenate([freqs, freqs], axis=-1)
    base = (half + 0.4 * AHD) / (1.4 * AHD)
    power = (t - n // 2) / float(WIN // 2)
    scale = base[None, :] ** power[:, None]
    scale = np.concatenate([scale, scale], axis=-1)
    return np.cos(freqs), np.sin(freqs), scale


class Kern:
    def __init__(self, p, dbg=()):
        self.p = p
        self.dbg = set(dbg)
        self.A = p.arena("A", 46000)
        self.pb = [p.psum(f"pb{i}") for i in range(8)]
        self.dbg_outs = {}
        self.n_dbg = 0

    def dump(self, name, buf, prow=128, cols=None):
        if name not in self.dbg:
            return
        p = self.p
        cols = cols if cols is not None else buf.ncol
        d = p.dout("dbg_" + name, [prow, cols])
        p.dma("qPOOL", d, d.t[:, :], buf, buf[0:prow, 0:cols])
        self.dbg_outs[name] = (prow, cols)

    def bcast(self, bank, src, row, ncols, c0=0):
        p = self.p
        rhs_ap = src.t[0:128, src.pcol + c0:src.pcol + c0 + ncols]
        p.mm(bank, bank[0:128, 0:ncols], self.sel,
             self.sel[:, 128 * row:128 * (row + 1)], src, rhs_ap,
             start=True, stop=True)

    def ln_stats(self, chunks, caps, n, b_sum, b_sq, x2t, nch=8):
        p = self.p
        for j in range(nch):
            cb, cap = chunks[j], caps[j]
            p.act(x2t, x2t[0:128, 0:n], cb, cap, AF.Square)
            p.mm(b_sum, b_sum[0:1, 0:n], self.onescol, self.onescol[:, 0:1],
                 cb, cap, start=(j == 0), stop=(j == nch - 1))
            p.mm(b_sq, b_sq[0:1, 0:n], self.onescol, self.onescol[:, 0:1],
                 x2t, x2t[0:128, 0:n], start=(j == 0), stop=(j == nch - 1))

    def ln_rows(self, b_sum, b_sq, n, rinv, mur, eps_idx, dinv_idx):
        p = self.p
        kr = self.krow
        t0, t1, t2 = self.r0, self.r1, self.r2
        p.ts("DVE", t0, t0[0:1, 0:n], b_sum, b_sum[0:1, 0:n],
             (kr, kr[0:1, dinv_idx:dinv_idx + 1]), None, ALU.mult)
        p.ts("DVE", t1, t1[0:1, 0:n], b_sq, b_sq[0:1, 0:n],
             (kr, kr[0:1, dinv_idx:dinv_idx + 1]), None, ALU.mult)
        p.act(t2, t2[0:1, 0:n], t0, t0[0:1, 0:n], AF.Square)
        p.tt("DVE", ALU.subtract, t1, t1[0:1, 0:n], t1, t1[0:1, 0:n],
             t2, t2[0:1, 0:n])
        p.act(t2, t2[0:1, 0:n], t1, t1[0:1, 0:n], AF.Sqrt,
              bias=(kr, kr[0:1, eps_idx:eps_idx + 1]))
        p.recip(rinv, rinv[0:1, 0:n], t2, t2[0:1, 0:n])
        p.tt("DVE", ALU.mult, mur, mur[0:1, 0:n], t0, t0[0:1, 0:n],
             rinv, rinv[0:1, 0:n])

    def setup(self):
        p = self.p
        A = self.A
        zbuf = Buf("zall", A.t, "sbuf", ncol=A.ncols)
        p.memset("DVE", zbuf, A.t[0:128, 0:A.ncols], 0.0)
        A.dead.append((0, A.ncols, dict(zbuf.last_w)))
        def cload(name, ncol, prow=128, dname=None, q="qSP"):
            d = p.din(dname or name, [prow, ncol])
            b = A.alloc(name, ncol)
            p.dma(q, b, b[0:prow, 0:ncol], d, d.t[:, :])
            return b
        self.sel = cload("sel", 1024)
        self.ident = cload("ident", 128)
        self.hsel2 = cload("hsel2", 128)
        self.hsel = cload("hsel", 2)
        self.rot = cload("rot", 128)
        self.tri = cload("tri", 128)
        self.onescol = A.alloc("onescol", 1)
        p.ts("DVE", self.onescol, self.onescol[0:128, 0:1], self.hsel,
             self.hsel[0:128, 0:1], 0.0, 1.0, ALU.mult, ALU.add)
        self.krow = cload("krow", 8, prow=1)
        self.kcol = cload("kcol", 8)
        self.convw = cload("convw", 64)
        self.convb = cload("convb", 16)
        self.nw = cload("nw", 16)
        self.Dp = cload("Dp", 8)
        self.Alog = cload("Alog", 1, prow=8)
        self.dtb = cload("dtb", 1, prow=8)
        self.cvec = cload("cvec", 8)
        self.hgate = cload("hgate", 1)
        self.e1g = cload("e1g", 4)
        self.e1g1m = cload("e1g1m", 4)
        self.r0 = A.alloc("r0", 512)
        self.r1 = A.alloc("r1", 512)
        self.r2 = A.alloc("r2", 512)
        self.rinv = A.alloc("rinv", 512)
        self.mur = A.alloc("mur", 512)
        self.negA = A.alloc("negA", 1)
        p.act(self.negA, self.negA[0:8, 0:1], self.Alog, self.Alog[0:8, 0:1],
              AF.Exp)
        p.ts("DVE", self.negA, self.negA[0:8, 0:1], self.negA,
             self.negA[0:8, 0:1], -1.0, None, ALU.mult)
        self.wip = p.din("wip", [34 * 128, 8 * 128])
        self.wop = p.din("wop", [8 * 128, 16 * 128])
        self.wmod = p.din("wmod", [48 * 128, 8 * 128])
        self.wqkv = [p.din(f"wqkv{l}", [12 * 128, 8 * 128]) for l in range(2)]
        self.wo = [p.din(f"wo{l}", [8 * 128, 4 * 128]) for l in range(2)]
        self.w1 = [p.din(f"w1{l}", [44 * 128, 8 * 128]) for l in range(2)]
        self.w2 = [p.din(f"w2{l}", [8 * 128, 22 * 128]) for l in range(2)]
        self.b1 = [cload(f"b1_{l}", 44, dname=f"b1{l}") for l in range(2)]
        self.b2 = [cload(f"b2_{l}", 8, dname=f"b2{l}") for l in range(2)]
        self.lg = [cload(f"lg_{l}", 8, dname=f"lg{l}") for l in range(2)]
        self.lbv = [cload(f"lb_{l}", 8, dname=f"lb{l}") for l in range(2)]
        self.qsc = [cload(f"qsc_{l}", 4, dname=f"qsc{l}") for l in range(2)]
        self.ksc = [cload(f"ksc_{l}", 4, dname=f"ksc{l}") for l in range(2)]
        self.xin = p.din("xin", [D, T + 3])
        self.ropeqc = p.din("ropeqc", [128, 128 + T])
        self.ropeqs = p.din("ropeqs", [128, 128 + T])
        self.ropekc = p.din("ropekc", [128, KHALO + T])
        self.ropeks = p.din("ropeks", [128, KHALO + T])
        self.band = p.din("band", [128, 1408])
        self.blk0 = p.din("blk0", [4 * 128, 1024])
        self.hmask = p.din("hmask", [128, KHALO])
        def scr(name, shape):
            return (p.dout if name in self.dbg else p.dram)(name, shape)
        self.z_scr = scr("z_scr", [DIN, T])
        self.y_scr = scr("y_scr", [DIN, T])
        self.xmid = scr("xmid", [D, T])
        self.xffn = scr("xffn", [D, T])
        self.xffn_b = scr("xffn_b", [D, T])
        self.e1_in = p.dram("e1_in", [NH * NST + 8, HD])
        self.e1_out = p.dram("e1_out", [4 * (NH * NST + 8), HD])
        self.e2_in = p.dram("e2_in", [D, HALO])
        self.e2_out = p.dram("e2_out", [4 * D, HALO])
        self.e3_in = p.dram("e3_in", [D, HALO])
        self.e3_out = p.dram("e3_out", [4 * D, HALO])
        self.eselg = cload("eselg", 4)
        self.out_fm = p.dout("out_fm", [D, T])
        self.qpadE = p.sbuf("qpadE", [128, 512])
        self.qpadO = p.sbuf("qpadO", [128, 512])
        p.memset("DVE", self.qpadE, self.qpadE[64:128, 0:512], 0.0)
        p.memset("DVE", self.qpadO, self.qpadO[0:64, 0:512], 0.0)

    def mod_vector(self):
        p, A = self.p, self.A
        self.mod = A.alloc("mod", 48)
        self.modp1 = A.alloc("modp1", 16)
        A.push()
        silc = A.alloc("silc", 8)
        p.act(silc, silc[0:128, 0:8], self.cvec, self.cvec[0:128, 0:8],
              AF.Silu)
        mod = self.mod
        wbuf = [A.alloc(f"wmv{i}", 1024) for i in range(2)]
        for m in range(48):
            w = wbuf[m % 2]
            p.dma("qSP", w, w[0:128, 0:1024], self.wmod,
                  self.wmod.t[m * 128:(m + 1) * 128, :])
            bank = self.pb[m % 8]
            for kk in range(8):
                p.mm(bank, bank[0:128, 0:1], w, w[0:128, 128 * kk:128 * (kk + 1)],
                     silc, silc[0:128, kk:kk + 1], start=(kk == 0), stop=(kk == 7))
            p.copy("ACT", mod, mod[0:128, m:m + 1], bank, bank[0:128, 0:1])
        A.pop()
        p.ts("DVE", self.modp1, self.modp1[0:128, 0:8], self.mod,
             self.mod[0:128, 8:16], 1.0, None, ALU.add)
        p.ts("DVE", self.modp1, self.modp1[0:128, 8:16], self.mod,
             self.mod[0:128, 32:40], 1.0, None, ALU.add)
        t = self.A.alloc("gtmp", 16)
        for (c0, o) in ((16, 0), (40, 8)):
            p.ts("DVE", t, t[0:128, o:o + 8], self.mod,
                 self.mod[0:128, c0:c0 + 8], 0.0, None, ALU.is_ge)
            p.ts("DVE", t, t[0:128, o:o + 8], t, t[0:128, o:o + 8],
                 2.0, -1.0, ALU.mult, ALU.add)
            p.act(self.mod, self.mod[0:128, c0:c0 + 8], self.mod,
                  self.mod[0:128, c0:c0 + 8], AF.Abs)
            p.ts("DVE", self.mod, self.mod[0:128, c0:c0 + 8], self.mod,
                 self.mod[0:128, c0:c0 + 8], 0.25, None, ALU.max)
            p.tt("DVE", ALU.mult, self.mod, self.mod[0:128, c0:c0 + 8],
                 self.mod, self.mod[0:128, c0:c0 + 8], t, t[0:128, o:o + 8])
        self.dump("mod", self.mod)

    def _ln_mod_x(self, xsrc, xc0, n, modoff):
        p, A = self.p, self.A
        xt = [A.alloc(f"lx{j}", n) for j in range(8)]
        for j in range(8):
            p.dma("qACT", xt[j], xt[j][0:128, 0:n], xsrc,
                  xsrc.t[128 * j:128 * (j + 1), xc0:xc0 + n])
        x2t = A.alloc("lx2", n)
        bs, bq = self.pb[0], self.pb[1]
        self.ln_stats(xt, [b[0:128, 0:n] for b in xt], n, bs, bq, x2t)
        self.ln_rows(bs, bq, n, self.rinv, self.mur, eps_idx=0, dinv_idx=2)
        bmu, br = self.pb[2], self.pb[3]
        self.bcast(bmu, self.mur, 0, n)
        self.bcast(br, self.rinv, 0, n)
        po, mo = modoff
        h = xt
        for j in range(8):
            p.tt("DVE", ALU.mult, h[j], h[j][0:128, 0:n],
                 xt[j], xt[j][0:128, 0:n], br, br[0:128, 0:n])
            p.tt("DVE", ALU.subtract, h[j], h[j][0:128, 0:n],
                 h[j], h[j][0:128, 0:n], bmu, bmu[0:128, 0:n])
            p.ts("DVE", h[j], h[j][0:128, 0:n], h[j], h[j][0:128, 0:n],
                 (self.modp1, self.modp1[0:128, po + j:po + j + 1]),
                 (self.mod, self.mod[0:128, mo + j:mo + j + 1]),
                 ALU.mult, ALU.add)
        return h

    def _inproj(self, h, n, c0, mini=False):
        p, A = self.p, self.A
        wb = [A.alloc(f"wip{i}", 1024) for i in range(2)]
        zt = [A.alloc(f"zt{i}", n) for i in range(2)]
        mlist = range(16, 32) if mini else range(34)
        for i, m in enumerate(mlist):
            w = wb[i % 2]
            p.dma("qSP", w, w[0:128, 0:1024], self.wip,
                  self.wip.t[m * 128:(m + 1) * 128, :])
            bank = self.pb[4 + (i % 2)]
            M = 8 if m == 33 else 128
            for kk in range(8):
                p.mm(bank, bank[0:M, 0:n], w, w[0:128, 128 * kk:128 * kk + M],
                     h[kk], h[kk][0:128, 0:n], start=(kk == 0), stop=(kk == 7))
            if mini:
                ch = m - 16
                p.copy("DVE", self.xchalo, self.xchalo[0:128, 3 * ch:3 * ch + 3],
                       bank, bank[0:128, 0:n])
            elif m < 16:
                z = zt[i % 2]
                p.copy("DVE", z, z[0:128, 0:n], bank, bank[0:128, 0:n])
                p.dma("qPOOL", self.z_scr, self.z_scr.t[128 * m:128 * (m + 1),
                                                        c0:c0 + n],
                      z, z[0:128, 0:n])
            elif m < 32:
                ch = m - 16
                p.copy("DVE", self.xct[ch], self.xct[ch][0:128, 3:3 + n],
                       bank, bank[0:128, 0:n])
            elif m == 32:
                p.copy("DVE", self.chunk32, self.chunk32[0:128, c0:c0 + n],
                       bank, bank[0:128, 0:n])
            else:
                dtt = A.alloc("dtt", n)
                dtm_ = A.alloc("dtm", n)
                p.ts("DVE", dtt, dtt[0:8, 0:n], bank, bank[0:8, 0:n],
                     (self.dtb, self.dtb[0:8, 0:1]), None, ALU.add)
                p.ts("DVE", dtm_, dtm_[0:8, 0:n], dtt, dtt[0:8, 0:n],
                     0.0, None, ALU.max)
                p.act(dtt, dtt[0:8, 0:n], dtt, dtt[0:8, 0:n], AF.Abs)
                p.ts("DVE", dtt, dtt[0:8, 0:n], dtt, dtt[0:8, 0:n],
                     -1.0, None, ALU.mult)
                p.act(dtt, dtt[0:8, 0:n], dtt, dtt[0:8, 0:n], AF.Exp)
                p.act(dtt, dtt[0:8, 0:n], dtt, dtt[0:8, 0:n], AF.Ln,
                      bias=(self.kcol, self.kcol[0:8, 5:6]))
                p.tt("DVE", ALU.add, self.dt8, self.dt8[0:8, c0:c0 + n],
                     dtm_, dtm_[0:8, 0:n], dtt, dtt[0:8, 0:n])

    def mamba_a(self):
        p, A = self.p, self.A
        A.push()
        self.Lg = A.alloc("Lg", T + 1)
        self.chunk32 = A.alloc("chunk32", T)
        self.dt8 = A.alloc("dt8", T)
        self.state = [A.alloc(f"state{h}", HD) for h in range(NH)]
        self.xchalo = A.alloc("xchalo", 48)
        self.onesT = A.alloc("onesT", MTILE)
        p.memset("DVE", self.onesT, self.onesT[0:8, 0:MTILE], 1.0)
        for h in range(NH):
            p.memset("DVE", self.state[h], self.state[h][0:128, 0:HD], 0.0)
        A.push()
        self.xct = [A.alloc(f"xct{ch}", MTILE + 3) for ch in range(16)]
        A.push()
        hmini = self._ln_mod_x(self.xin, 0, 3, (0, 0))
        self._inproj(hmini, 3, 0, mini=True)
        A.pop()
        for ch in range(16):
            p.ts("DVE", self.xchalo, self.xchalo[0:128, 3 * ch:3 * ch + 3],
                 self.xchalo, self.xchalo[0:128, 3 * ch:3 * ch + 3],
                 (self.hgate, self.hgate[0:128, 0:1]), None, ALU.mult)
        for t in range(T // MTILE):
            self._mamba_tile(t)
        A.pop()
        self.dump("dt8", self.dt8, prow=8)
        self.dump("Lg", self.Lg, prow=8, cols=T + 1)
        self.dump("chunk32", self.chunk32)

    def _mamba_tile(self, t):
        p, A = self.p, self.A
        n, c0 = MTILE, t * MTILE
        A.push()
        h = self._ln_mod_x(self.xin, 3 + c0, n, (0, 0))
        if t == 0:
            self.dump("h0", h[0], cols=MTILE)
            self.dump("mur0", self.mur, prow=1, cols=MTILE)
            self.dump("rinv0", self.rinv, prow=1, cols=MTILE)
            if "bmu0" in self.dbg:
                tmp = A.alloc("bmudump", MTILE)
                p.copy("DVE", tmp, tmp[0:128, 0:MTILE], self.pb[2],
                       self.pb[2][0:128, 0:MTILE])
                self.dump("bmu0", tmp, cols=MTILE)
        for ch in range(16):
            p.copy("DVE", self.xct[ch], self.xct[ch][0:128, 0:3],
                   self.xchalo, self.xchalo[0:128, 3 * ch:3 * ch + 3])
        self._inproj(h, n, c0)
        for ch in range(16):
            p.copy("DVE", self.xchalo, self.xchalo[0:128, 3 * ch:3 * ch + 3],
                   self.xct[ch], self.xct[ch][0:128, n:n + 3])
        self.xcs = [A.alloc(f"xcs{ch}", n) for ch in range(16)]
        cw, cb = self.convw, self.convb
        for ch in range(16):
            acc = self.xcs[ch]
            p.ts("DVE", acc, acc[0:128, 0:n], self.xct[ch],
                 self.xct[ch][0:128, 0:n],
                 (cw, cw[0:128, 4 * ch:4 * ch + 1]),
                 (cb, cb[0:128, ch:ch + 1]), ALU.mult, ALU.add)
            for k in range(1, 4):
                p.stt("DVE", acc, acc[0:128, 0:n], self.xct[ch],
                      self.xct[ch][0:128, k:k + n],
                      (cw, cw[0:128, 4 * ch + k:4 * ch + k + 1]),
                      acc, acc[0:128, 0:n], ALU.mult, ALU.add)
            p.act(acc, acc[0:128, 0:n], acc, acc[0:128, 0:n], AF.Silu)
        p.ts("DVE", self.Lg, self.Lg[0:8, 1 + c0:1 + c0 + n], self.dt8,
             self.dt8[0:8, c0:c0 + n], (self.negA, self.negA[0:8, 0:1]),
             None, ALU.mult)
        p.tt("DVE", ALU.add, self.Lg, self.Lg[0:8, 1 + c0:2 + c0],
             self.Lg, self.Lg[0:8, 1 + c0:2 + c0],
             self.Lg, self.Lg[0:8, c0:c0 + 1])
        p.scan("DVE", self.Lg, self.Lg[0:8, 1 + c0:1 + c0 + n],
               self.onesT, self.onesT[0:8, 0:n],
               self.Lg, self.Lg[0:8, 1 + c0:1 + c0 + n])
        bpad = A.alloc("bpad", n)
        p.memset("DVE", bpad, bpad[64:128, 0:n], 0.0)
        p.copy("DVE", bpad, bpad[0:64, 0:n], self.chunk32,
               self.chunk32[64:128, c0:c0 + n])
        for cc in range(n // QCH):
            self._scan_chunk(c0 + cc * QCH, cc, bpad)
        A.pop()

    def _scan_chunk(self, tc0, cc, bpad):
        p, A = self.p, self.A
        pb = self.pb
        A.push()
        bT = pb[1]
        p.transpose(bT, bT[0:128, 0:128], self.Lg,
                    self.Lg.t[0:128, self.Lg.pcol + 1 + tc0:
                              self.Lg.pcol + 1 + tc0 + 128], self.ident)
        LgT = A.alloc("LgT", 8)
        p.copy("ACT", LgT, LgT[0:128, 0:8], bT, bT[0:128, 0:8])
        p.transpose(bT, bT[0:128, 128:256], self.chunk32,
                    self.chunk32[0:128, tc0:tc0 + 128], self.ident)
        CBtm = A.alloc("CBtm", 128)
        p.copy("ACT", CBtm, CBtm[0:128, 0:128], bT, bT[0:128, 128:256])
        eL8 = A.alloc("eL8", 128)
        p.ts("DVE", eL8, eL8[0:8, 0:128], self.Lg,
             self.Lg[0:8, 1 + tc0:1 + tc0 + 128],
             (self.Lg, self.Lg[0:8, tc0:tc0 + 1]), None, ALU.subtract)
        p.act(eL8, eL8[0:8, 0:128], eL8, eL8[0:8, 0:128], AF.Exp)
        bG = pb[0]
        p.mm(bG, bG[0:128, 0:128], bpad, bpad[0:128, 128 * cc:128 * cc + 128],
             self.chunk32, self.chunk32[0:128, tc0:tc0 + 128],
             start=True, stop=True)
        Gm = A.alloc("Gm", 128)
        p.tt("DVE", ALU.mult, Gm, Gm[0:128, 0:128], bG, bG[0:128, 0:128],
             self.tri, self.tri[0:128, 0:128])
        yct = [A.alloc(f"yct{ch}", 128) for ch in range(16)]
        Ewb = [A.alloc(f"Ew{i}", 128) for i in range(2)]
        Swb = [A.alloc(f"Sw{i}", 128) for i in range(2)]
        Ctb2 = [A.alloc(f"Ct{i}", 128) for i in range(2)]
        XTb = [A.alloc(f"XT{i}", 256) for i in range(2)]
        Bptmb = [A.alloc(f"Bptm{i}", 64) for i in range(2)]
        for hh in range(NH):
            par2 = hh % 2
            Ew, Sw, Ct, XT, Bptm = (Ewb[par2], Swb[par2], Ctb2[par2],
                                    XTb[par2], Bptmb[par2])
            bB, bB2 = (pb[2], pb[3]) if par2 == 0 else (pb[7], pb[0])
            self.bcast(bB, self.Lg, hh, 128, c0=1 + tc0)
            p.ts("DVE", Ew, Ew[0:128, 0:128], bB, bB[0:128, 0:128],
                 (LgT, LgT[0:128, hh:hh + 1]), 0.0, ALU.subtract, ALU.min)
            p.act(Ew, Ew[0:128, 0:128], Ew, Ew[0:128, 0:128], AF.Exp)
            p.tt("DVE", ALU.mult, Sw, Sw[0:128, 0:128], Gm, Gm[0:128, 0:128],
                 Ew, Ew[0:128, 0:128])
            self.bcast(bB2, eL8, hh, 128)
            p.tt("DVE", ALU.mult, Ct, Ct[0:128, 0:128], self.chunk32,
                 self.chunk32[0:128, tc0:tc0 + 128], bB2, bB2[0:128, 0:128])
            for j in range(2):
                src = self.xcs[2 * hh + j]
                p.transpose(bT, bT[0:128, 256 + 128 * j:256 + 128 * (j + 1)],
                            src, src[0:128, 128 * cc:128 * cc + 128],
                            self.ident)
                p.copy("ACT", XT, XT[0:128, 128 * j:128 * (j + 1)], bT,
                       bT[0:128, 256 + 128 * j:256 + 128 * (j + 1)])
            for j in range(2):
                bY = pb[4 + j]
                p.mm(bY, bY[0:128, 0:128], XT, XT[0:128, 128 * j:128 * (j + 1)],
                     Sw, Sw[0:128, 0:128], start=True, stop=False)
                st = self.state[hh]
                p.mm(bY, bY[0:128, 0:128], st, st[0:128, 128 * j:128 * (j + 1)],
                     Ct, Ct[0:128, 0:128], start=False, stop=True)
                ch = 2 * hh + j
                p.stt("DVE", yct[ch], yct[ch][0:128, 0:128], self.xcs[ch],
                      self.xcs[ch][0:128, 128 * cc:128 * cc + 128],
                      (self.Dp, self.Dp[0:128, hh:hh + 1]),
                      bY, bY[0:128, 0:128], ALU.mult, ALU.add)
            p.ts("DVE", Bptm, Bptm[0:128, 0:64], CBtm, CBtm[0:128, 64:128],
                 (Ew, Ew[0:128, 127:128]), None, ALU.mult)
            bS = pb[6]
            p.mm(bS, bS[0:64, 0:256], Bptm, Bptm[0:128, 0:64],
                 XT, XT[0:128, 0:256], start=True, stop=True)
            st = self.state[hh]
            p.stt("DVE", st, st[0:64, 0:256], st, st[0:64, 0:256],
                  (bB2, bB2[0:64, 127:128]), bS, bS[0:64, 0:256],
                  ALU.mult, ALU.add)
        ydst = self.y_scr.t.ap().rearrange("(ch p) t -> p ch t", p=128)
        ysrc = self.A.t[0:128, yct[0].pcol:yct[0].pcol + 16 * 128].rearrange(
            "p (ch t) -> p ch t", ch=16)
        ybuf = Buf("yblk", self.A.t, "sbuf", pcol=yct[0].pcol, ncol=16 * 128)
        for b in yct:
            _dmerge(ybuf.last_w, b.last_w)
        tag = p.dma("qPOOL", self.y_scr, ydst[:, :, tc0:tc0 + 128], ybuf, ysrc)
        for b in yct:
            _dmerge(b.readers, {tag[0]: tag[1]})
        A.pop()



    def e1_exchange(self):
        p, A = self.p, self.A
        for h in range(NH):
            p.dma("qPOOL", self.e1_in,
                  self.e1_in.t[h * NST:(h + 1) * NST, :],
                  self.state[h], self.state[h][0:64, 0:HD])
        A.push()
        pexp = A.alloc("pexp", HD)
        p.memset("DVE", pexp, pexp[0:8, 0:HD], 0.0)
        p.act(pexp, pexp[0:8, 0:1], self.Lg, self.Lg[0:8, T:T + 1], AF.Exp)
        p.dma("qPOOL", self.e1_in, self.e1_in.t[NH * NST:NH * NST + 8, :],
              pexp, pexp[0:8, 0:HD])
        p.collective("AllGather", ALU.bypass,
                     [[0, 1, 2, 3], [4, 5, 6, 7]], self.e1_in, self.e1_out)
        R = NH * NST + 8
        prow = A.alloc("prow", 8)
        peff = A.alloc("peff", 8)
        spe = A.alloc("spe", 2048)
        for j in range(3):
            p.dma("qSP", prow, prow[0:1, 0:8],
                  self.e1_out, self.e1_out.t[j * R + NH * NST:
                                             j * R + NH * NST + 8, 0:1],
                  allow_slow_non_contiguous=True)
            bk = self.pb[7]
            self.bcast(bk, prow, 0, 8)
            p.ts("DVE", peff, peff[0:128, 0:8], bk, bk[0:128, 0:8],
                 (self.e1g, self.e1g[0:128, j:j + 1]),
                 (self.e1g1m, self.e1g1m[0:128, j:j + 1]), ALU.mult, ALU.add)
            R_ = NH * NST + 8
            for h in range(NH):
                p.dma("qSP", spe, spe[0:64, 256 * h:256 * h + HD],
                      self.e1_out,
                      self.e1_out.t[j * R_ + h * NST:j * R_ + (h + 1) * NST, :])
            p.ts("DVE", spe, spe[0:64, 0:2048], spe, spe[0:64, 0:2048],
                 (self.e1g, self.e1g[0:64, j:j + 1]), None, ALU.mult)
            for h in range(NH):
                if j == 0:
                    p.copy("DVE", self.state[h], self.state[h][0:64, 0:HD],
                           spe, spe[0:64, 256 * h:256 * h + HD])
                else:
                    p.stt("DVE", self.state[h], self.state[h][0:64, 0:HD],
                          self.state[h], self.state[h][0:64, 0:HD],
                          (peff, peff[0:64, h:h + 1]),
                          spe, spe[0:64, 256 * h:256 * h + HD],
                          ALU.mult, ALU.add)
        A.pop()

    def mamba_b(self):
        for t in range(T // MTILE):
            self._mamba_b_tile(t)
        self.A.pop()

    def _mamba_b_tile(self, t):
        p, A = self.p, self.A
        n, c0 = MTILE, t * MTILE
        A.push()
        eL = A.alloc("eL", n)
        p.act(eL, eL[0:8, 0:n], self.Lg, self.Lg[0:8, 1 + c0:1 + c0 + n],
              AF.Exp)
        yt = [A.alloc(f"by{ch}", n) for ch in range(16)]
        zt = [A.alloc(f"bz{ch}", n) for ch in range(16)]
        for ch in range(16):
            p.dma("qACT", yt[ch], yt[ch][0:128, 0:n], self.y_scr,
                  self.y_scr.t[128 * ch:128 * (ch + 1), c0:c0 + n])
            p.dma("qSP", zt[ch], zt[ch][0:128, 0:n], self.z_scr,
                  self.z_scr.t[128 * ch:128 * (ch + 1), c0:c0 + n])
        Ctb = A.alloc("Ctb", 128)
        for cc in range(n // QCH):
            tc0 = c0 + cc * QCH
            for h in range(NH):
                bk = self.pb[2]
                self.bcast(bk, eL, h, QCH, c0=cc * QCH)
                p.tt("DVE", ALU.mult, Ctb, Ctb[0:128, 0:QCH], self.chunk32,
                     self.chunk32[0:128, tc0:tc0 + QCH], bk, bk[0:128, 0:QCH])
                for j in range(2):
                    bY = self.pb[4 + j]
                    st = self.state[h]
                    p.mm(bY, bY[0:128, 0:QCH], st,
                         st[0:128, 128 * j:128 * (j + 1)],
                         Ctb, Ctb[0:128, 0:QCH], start=True, stop=True)
                    ch = 2 * h + j
                    p.tt("DVE", ALU.add, yt[ch],
                         yt[ch][0:128, cc * QCH:cc * QCH + QCH],
                         yt[ch], yt[ch][0:128, cc * QCH:cc * QCH + QCH],
                         bY, bY[0:128, 0:QCH])
        x2t = A.alloc("bx2", n)
        bs = self.pb[0]
        for ch in range(16):
            p.act(zt[ch], zt[ch][0:128, 0:n], zt[ch], zt[ch][0:128, 0:n],
                  AF.Silu)
            p.tt("DVE", ALU.mult, yt[ch], yt[ch][0:128, 0:n],
                 yt[ch], yt[ch][0:128, 0:n], zt[ch], zt[ch][0:128, 0:n])
            p.act(x2t, x2t[0:128, 0:n], yt[ch], yt[ch][0:128, 0:n], AF.Square)
            p.mm(bs, bs[0:1, 0:n], self.onescol, self.onescol[:, 0:1],
                 x2t, x2t[0:128, 0:n], start=(ch == 0), stop=(ch == 15))
        p.ts("DVE", self.r0, self.r0[0:1, 0:n], bs, bs[0:1, 0:n],
             (self.krow, self.krow[0:1, 3:4]), None, ALU.mult)
        p.act(self.r0, self.r0[0:1, 0:n], self.r0, self.r0[0:1, 0:n], AF.Sqrt,
              bias=(self.krow, self.krow[0:1, 1:2]))
        p.recip(self.rinv, self.rinv[0:1, 0:n], self.r0, self.r0[0:1, 0:n])
        br = self.pb[1]
        self.bcast(br, self.rinv, 0, n)
        for ch in range(16):
            p.stt("DVE", yt[ch], yt[ch][0:128, 0:n], yt[ch],
                  yt[ch][0:128, 0:n], (self.nw, self.nw[0:128, ch:ch + 1]),
                  br, br[0:128, 0:n], ALU.mult, ALU.mult)
        wb = [A.alloc(f"wob{i}", 2048) for i in range(2)]
        xnew = [A.alloc(f"xn{j}", n) for j in range(8)]
        for m in range(8):
            w = wb[m % 2]
            p.dma("qSP", w, w[0:128, 0:2048], self.wop,
                  self.wop.t[m * 128:(m + 1) * 128, :])
            bank = self.pb[4 + (m % 2)]
            for kk in range(16):
                p.mm(bank, bank[0:128, 0:n], w,
                     w[0:128, 128 * kk:128 * (kk + 1)],
                     yt[kk], yt[kk][0:128, 0:n],
                     start=(kk == 0), stop=(kk == 15))
            xi = xnew[m]
            p.dma("qACT", xi, xi[0:128, 0:n], self.xin,
                  self.xin.t[128 * m:128 * (m + 1), 3 + c0:3 + c0 + n])
            p.stt("DVE", xi, xi[0:128, 0:n], bank, bank[0:128, 0:n],
                  (self.mod, self.mod[0:128, 16 + m:17 + m]),
                  xi, xi[0:128, 0:n], ALU.mult, ALU.add)
            p.dma("qPOOL", self.xmid, self.xmid.t[128 * m:128 * (m + 1),
                                                  c0:c0 + n],
                  xi, xi[0:128, 0:n])
        h2 = self._ln_mod_chunks(xnew, n, (8, 24))
        for j in range(8):
            p.dma("qPOOL", self.xffn, self.xffn.t[128 * j:128 * (j + 1),
                                                  c0:c0 + n],
                  h2[j], h2[j][0:128, 0:n])
            g0 = T - HALO
            lo = max(g0, c0)
            if lo < c0 + n:
                o = lo - c0
                p.dma("qPOOL", self.e2_in,
                      self.e2_in.t[128 * j:128 * (j + 1), lo - g0:c0 + n - g0],
                      h2[j], h2[j][0:128, o:n])
        A.pop()

    def _ln_mod_chunks(self, xb, n, modoff):
        p, A = self.p, self.A
        x2t = A.alloc("lnx2", n)
        bs, bq = self.pb[0], self.pb[1]
        self.ln_stats(xb, [b[0:128, 0:n] for b in xb], n, bs, bq, x2t)
        self.ln_rows(bs, bq, n, self.rinv, self.mur, eps_idx=0, dinv_idx=2)
        bmu, br = self.pb[2], self.pb[3]
        self.bcast(bmu, self.mur, 0, n)
        self.bcast(br, self.rinv, 0, n)
        po, mo = modoff
        for j in range(8):
            p.tt("DVE", ALU.mult, xb[j], xb[j][0:128, 0:n],
                 xb[j], xb[j][0:128, 0:n], br, br[0:128, 0:n])
            p.tt("DVE", ALU.subtract, xb[j], xb[j][0:128, 0:n],
                 xb[j], xb[j][0:128, 0:n], bmu, bmu[0:128, 0:n])
            p.ts("DVE", xb[j], xb[j][0:128, 0:n], xb[j], xb[j][0:128, 0:n],
                 (self.modp1, self.modp1[0:128, po + j:po + j + 1]),
                 (self.mod, self.mod[0:128, mo + j:mo + j + 1]),
                 ALU.mult, ALU.add)
        return xb

    def ffn_layer(self, l):
        p, A = self.p, self.A
        xin_d = self.xffn if l == 0 else self.xffn_b
        ein = self.e2_in if l == 0 else self.e3_in
        eout = self.e2_out if l == 0 else self.e3_out
        p.collective("AllGather", ALU.bypass, [[0, 1, 2, 3], [4, 5, 6, 7]],
                     ein, eout)
        if "stop=cc" in self.dbg:
            return
        A.push()
        self.bandb = A.alloc("bandb", 1408)
        p.dma("qACT", self.bandb, self.bandb[0:128, 0:1408], self.band,
              self.band.t[:, :])
        if "stop=band" in self.dbg:
            return
        self.hbcol = A.alloc("hbcol", 8)
        self.wv = [A.alloc(f"wvf{kk}", 512) for kk in range(8)]
        kslot = [[A.alloc(f"k{sl}_{pr}", 512) for pr in range(4)]
                 for sl in range(2)]
        vslot = [[A.alloc(f"v{sl}_{tc}", 512) for tc in range(4)]
                 for sl in range(2)]
        khx = [A.alloc(f"khx{pr}", 128) for pr in range(4)]
        vhx = A.alloc("vhx", 512)
        self.x1carry = A.alloc("x1carry", 8)
        self.x1acol = A.alloc("x1acol", 8)
        A.push()
        hb = [A.alloc(f"hb{j}", HALO) for j in range(8)]
        tmph = A.alloc("tmph", HALO)
        for j in range(8):
            if f"stop=hbj{j}" in self.dbg:
                return
            for jb in range(4):
                p.dma("qSP", tmph, tmph[0:128, 0:HALO], eout,
                      eout.t[jb * D + 128 * j:jb * D + 128 * (j + 1), :])
                if jb == 0:
                    p.ts("DVE", hb[j], hb[j][0:128, 0:HALO], tmph,
                         tmph[0:128, 0:HALO],
                         (self.eselg, self.eselg[0:128, 0:1]), None, ALU.mult)
                else:
                    p.stt("DVE", hb[j], hb[j][0:128, 0:HALO], tmph,
                          tmph[0:128, 0:HALO],
                          (self.eselg, self.eselg[0:128, jb:jb + 1]),
                          hb[j], hb[j][0:128, 0:HALO], ALU.mult, ALU.add)
        x1a = A.alloc("x1a", 128)
        if "stop=hb" in self.dbg:
            A.pop()
            return
        if "skip=hkv" not in self.dbg:
            self._halo_kv(l, hb, kslot[1], vslot[1], khx, vhx)
        if "stop=hkv" in self.dbg:
            A.pop()
            return
        if "skip=hattn" not in self.dbg:
            self._halo_attn(l, hb, kslot[1], vslot[1], khx, vhx, x1a)
        if "stop=hattn" in self.dbg:
            A.pop()
            return
        for j in range(8):
            p.copy("DVE", self.hbcol, self.hbcol[0:128, j:j + 1], hb[j],
                   hb[j][0:128, HALO - 1:HALO])
        A.pop()
        nblk = 4
        for d in self.dbg:
            if d.startswith("nblk="):
                nblk = int(d.split("=")[1])
        for w in range(nblk):
            self._ffn_block(l, w, xin_d, kslot, vslot)
        A.pop()

    def _qkv_chunks(self, l, src_getter, n, which, dst, rope_c0):
        p, A = self.p, self.A
        wq = self.wqkv[l]
        scv = self.qsc[l] if which == 0 else self.ksc[l]
        tc = self.ropeqc if which == 0 else self.ropekc
        tsn = self.ropeqs if which == 0 else self.ropeks
        A.push()
        rc = A.alloc("rc", n)
        rs = A.alloc("rs", n)
        p.dma("qACT", rc, rc[0:128, 0:n], tc, tc.t[:, rope_c0:rope_c0 + n])
        p.dma("qACT", rs, rs[0:128, 0:n], tsn, tsn.t[:, rope_c0:rope_c0 + n])
        qs_lim = 99
        for d in self.dbg:
            if d.startswith("qkstop="):
                qs_lim = int(d.split("=")[1])
        wb = [A.alloc(f"wqk{i}", 1024) for i in range(2)]
        q2b = [A.alloc(f"q2_{i}", n) for i in range(2)]
        qrawb = [A.alloc(f"qraw{i}", n) for i in range(2)]
        s2b = [A.alloc(f"s2_{i}", n) for i in range(2)]
        t2b = [A.alloc(f"t2_{i}", n) for i in range(2)]
        for m4 in range(4):
            q2, qraw, s2, t2 = (q2b[m4 % 2], qrawb[m4 % 2], s2b[m4 % 2],
                                t2b[m4 % 2])
            m = m4 + (0 if which == 0 else 4)
            w = wb[m4 % 2]
            p.dma("qSP", w, w[0:128, 0:1024], wq,
                  wq.t[m * 128:(m + 1) * 128, :])
            for n0 in range(0, n, 512):
                nn = min(512, n - n0)
                bank = self.pb[0 + (m4 % 2)]
                for kk in range(8):
                    sb_, sap = src_getter(kk)
                    sap = sap.slice_free(n0, nn) if hasattr(sap, "slice_free")                         else sap
                    p.mm(bank, bank[0:128, 0:nn], w,
                         w[0:128, 128 * kk:128 * (kk + 1)], sb_,
                         src_getter(kk, n0, nn)[1]
                         if src_getter.__code__.co_argcount > 1 else sap,
                         start=(kk == 0), stop=(kk == 7))
                if qs_lim < 2:
                    continue
                p.act(q2, q2[0:128, n0:n0 + nn], bank, bank[0:128, 0:nn],
                      AF.Square)
                if qs_lim < 3:
                    continue
                p.copy("DVE", qraw, qraw[0:128, n0:n0 + nn], bank,
                       bank[0:128, 0:nn])
                if qs_lim < 4:
                    continue
                bss = self.pb[6 if m4 % 2 == 0 else 7]
                p.mm(bss, bss[0:2, 0:nn], self.hsel, self.hsel[0:128, 0:2],
                     q2, q2[0:128, n0:n0 + nn], start=True, stop=True)
                if qs_lim < 5:
                    continue
                p.act(s2, s2[0:2, n0:n0 + nn], bss, bss[0:2, 0:nn], AF.Sqrt)
                p.ts("DVE", s2, s2[0:2, n0:n0 + nn], s2, s2[0:2, n0:n0 + nn],
                     1e-12, None, ALU.max)
                p.recip(t2, t2[0:2, n0:n0 + nn], s2, s2[0:2, n0:n0 + nn])
                if qs_lim < 6:
                    continue
                brb = self.pb[4 if m4 % 2 == 0 else 5]
                p.mm(brb, brb[0:128, 0:nn], self.hsel2,
                     self.hsel2[0:128, 0:128], t2,
                     self.A.t[0:128, t2.pcol + n0:t2.pcol + n0 + nn],
                     start=True, stop=True)
                if qs_lim < 7:
                    continue
                p.stt("DVE", qraw, qraw[0:128, n0:n0 + nn], qraw,
                      qraw[0:128, n0:n0 + nn],
                      (scv, scv[0:128, m4:m4 + 1]), brb, brb[0:128, 0:nn],
                      ALU.mult, ALU.mult)
                if qs_lim < 8:
                    continue
                brt = self.pb[2 if m4 % 2 == 0 else 3]
                p.mm(brt, brt[0:128, 0:nn], self.rot, self.rot[0:128, 0:128],
                     qraw, qraw[0:128, n0:n0 + nn], start=True, stop=True)
                d = dst[m4]
                p.tt("DVE", ALU.mult, d, d[0:128, n0:n0 + nn], qraw,
                     qraw[0:128, n0:n0 + nn], rc, rc[0:128, n0:n0 + nn])
                p.tt("DVE", ALU.mult, t2, t2[0:128, n0:n0 + nn], brt,
                     brt[0:128, 0:nn], rs, rs[0:128, n0:n0 + nn])
                p.tt("DVE", ALU.add, d, d[0:128, n0:n0 + nn], d,
                     d[0:128, n0:n0 + nn], t2, t2[0:128, n0:n0 + nn])
        A.pop()

    def _halo_kv(self, l, hb, kslot1, vslot1, khx, vhx):
        p, A = self.p, self.A
        A.push()
        def src(kk, n0=0, nn=KHALO):
            off = 4 if kk < 4 else 3
            return (hb[kk], hb[kk][0:128, off + n0:off + n0 + nn])
        kd = [A.alloc(f"hkd{pr}", KHALO) for pr in range(4)]
        if "hkv=v" not in self.dbg:
            self._qkv_chunks(l, src, KHALO, 1, kd, 0)
            for pr in range(4):
                p.copy("DVE", khx[pr], khx[pr][0:128, 0:128], kd[pr],
                       kd[pr][0:128, 0:128])
                p.copy("DVE", kslot1[pr], kslot1[pr][0:128, 0:512], kd[pr],
                       kd[pr][0:128, 128:640])
        if "hkv=qk" in self.dbg:
            A.pop()
            return
        wq = self.wqkv[l]
        wb = [A.alloc(f"wv{i}", 1024) for i in range(2)]
        wvt = [A.alloc(f"wvt{i}", 512) for i in range(2)]
        for tc5 in range(5):
            bank = self.pb[0 + (tc5 % 2)]
            for kk in range(8):
                w = wb[kk % 2]
                if tc5 == 0:
                    for mi, m in enumerate(range(8, 12)):
                        p.dma("qSP", self.wv[kk], 
                              self.wv[kk][0:128, 128 * mi:128 * (mi + 1)],
                              wq, wq.t[m * 128:(m + 1) * 128,
                                       128 * kk:128 * (kk + 1)])
                off = 4 if kk < 4 else 3
                p.mm(bank, bank[0:128, 0:512], hb[kk],
                     hb[kk][0:128, off + 128 * tc5:off + 128 * tc5 + 128],
                     self.wv[kk], self.wv[kk][0:128, 0:512],
                     start=(kk == 0), stop=(kk == 7))
            dst = vhx if tc5 == 0 else vslot1[tc5 - 1]
            p.copy("DVE", dst, dst[0:128, 0:512], bank, bank[0:128, 0:512])
        A.pop()

    def _halo_attn(self, l, hb, kslot1, vslot1, khx, vhx, x1a):
        p, A = self.p, self.A
        A.push()
        def src(kk, n0=0, nn=128):
            off = 516 if kk < 4 else 515
            return (hb[kk], hb[kk][0:128, off + n0:off + n0 + nn])
        qd = [A.alloc(f"hq{pr}", 128) for pr in range(4)]
        self._qkv_chunks(l, src, 128, 0, qd, 0)
        hm = A.alloc("hm", KHALO)
        p.dma("qACT", hm, hm[0:128, 0:KHALO], self.hmask, self.hmask.t[:, :])
        ot = [A.alloc(f"ho{j}", 128) for j in range(4)]
        PT = [A.alloc(f"hPT{c}", 128) for c in range(5)]
        Sm = A.alloc("hSm", KHALO)
        rl = A.alloc("hrl", 2)
        for hh in range(NH):
            pr, par = hh // 2, hh % 2
            qp = self.qpadO if par else self.qpadE
            p.copy("DVE", qp, qp[64 * par:64 * par + 64, 0:128], qd[pr],
                   qd[pr][64 * par:64 * par + 64, 0:128])
            b0, b1 = self.pb[0], self.pb[1]
            p.mm(b0, b0[0:128, 0:128], qp, qp[0:128, 0:128],
                 khx[pr], khx[pr][0:128, 0:128], start=True, stop=True)
            p.mm(b1, b1[0:128, 0:512], qp, qp[0:128, 0:128],
                 kslot1[pr], kslot1[pr][0:128, 0:512], start=True, stop=True)
            p.tt("DVE", ALU.add, Sm, Sm[0:128, 0:128], b0, b0[0:128, 0:128],
                 hm, hm[0:128, 0:128])
            p.tt("DVE", ALU.add, Sm, Sm[0:128, 128:640], b1,
                 b1[0:128, 0:512], hm, hm[0:128, 128:640])
            p.reduce("DVE", rl, rl[0:128, 0:1], Sm, Sm[0:128, 0:KHALO],
                     op=ALU.max, negate=True)
            p.act(Sm, Sm[0:128, 0:KHALO], Sm, Sm[0:128, 0:KHALO], AF.Exp,
                  bias=(rl, rl[0:128, 0:1]), accum=(rl, rl[0:128, 1:2]))
            p.recip(rl, rl[0:128, 0:1], rl, rl[0:128, 1:2])
            p.act(Sm, Sm[0:128, 0:KHALO], Sm, Sm[0:128, 0:KHALO], AF.Copy,
                  scale=(rl, rl[0:128, 0:1]))
            for c in range(5):
                bt = self.pb[2 + (c % 2)]
                p.transpose(bt, bt[0:128, 0:128], Sm,
                            Sm[0:128, 128 * c:128 * (c + 1)], self.ident)
                p.copy("ACT", PT[c], PT[c][0:128, 0:128], bt,
                       bt[0:128, 0:128])
            bat = self.pb[4]
            for c in range(5):
                vsrc = vhx if c == 0 else vslot1[c - 1]
                p.mm(bat, bat[64 * par:64 * par + 64, 0:128], vsrc,
                     vsrc[0:128, 64 * hh:64 * hh + 64], PT[c],
                     PT[c][0:128, 0:128], start=(c == 0), stop=(c == 4))
            if par == 1:
                p.copy("DVE", ot[pr], ot[pr][0:128, 0:128], bat,
                       bat[0:128, 0:128])
        wb = [A.alloc(f"hwo{i}", 512) for i in range(2)]
        for m in range(8):
            w = wb[m % 2]
            p.dma("qSP", w, w[0:128, 0:512], self.wo[l],
                  self.wo[l].t[m * 128:(m + 1) * 128, :])
            bank = self.pb[5 + (m % 2)]
            for kk in range(4):
                p.mm(bank, bank[0:128, 0:128], w,
                     w[0:128, 128 * kk:128 * (kk + 1)],
                     ot[kk], ot[kk][0:128, 0:128], start=(kk == 0),
                     stop=(kk == 3))
            p.tt("DVE", ALU.add, x1a, x1a[0:128, 0:128], bank,
                 bank[0:128, 0:128], hb[m], hb[m][0:128, 516:644])
            p.ts("DVE", x1a, x1a[0:128, 0:128], x1a, x1a[0:128, 0:128],
                 (self.hgate, self.hgate[0:128, 0:1]), None, ALU.mult)
            p.copy("DVE", self.x1acol, self.x1acol[0:128, m:m + 1], x1a,
                   x1a[0:128, 127:128])
        A.pop()

    def _ffn_block(self, l, w, xin_d, kslot, vslot):
        p, A = self.p, self.A
        qb = 512 * w
        A.push()
        xt = [A.alloc(f"fx{j}", 513) for j in range(8)]
        for j in range(8):
            if w == 0:
                p.dma("qACT", xt[j], xt[j][0:128, 1:513], xin_d,
                      xin_d.t[128 * j:128 * (j + 1), 0:512])
                p.copy("DVE", xt[j], xt[j][0:128, 0:1], self.hbcol,
                       self.hbcol[0:128, j:j + 1])
            else:
                p.dma("qACT", xt[j], xt[j][0:128, 0:513], xin_d,
                      xin_d.t[128 * j:128 * (j + 1), qb - 1:qb + 512])
        def src(kk, n0=0, nn=512):
            off = 1 if kk < 4 else 0
            return (xt[kk], xt[kk][0:128, off + n0:off + n0 + nn])
        sl = w % 2
        A.push()
        qd = [A.alloc(f"fq{pr}", 512) for pr in range(4)]
        self._qkv_chunks(l, src, 512, 0, qd, 128 + qb)
        self._qkv_chunks(l, src, 512, 1, kslot[sl], KHALO + qb)
        for tc4 in range(4):
            bank = self.pb[0 + (tc4 % 2)]
            for kk in range(8):
                off = 1 if kk < 4 else 0
                p.mm(bank, bank[0:128, 0:512], xt[kk],
                     xt[kk][0:128, off + 128 * tc4:off + 128 * tc4 + 128],
                     self.wv[kk], self.wv[kk][0:128, 0:512],
                     start=(kk == 0), stop=(kk == 7))
            d = vslot[sl][tc4]
            p.copy("DVE", d, d[0:128, 0:512], bank, bank[0:128, 0:512])
        prv = 1 - sl
        ot = [A.alloc(f"fo{j}", 512) for j in range(4)]
        A.push()
        if w == 0:
            mkb = [A.alloc(f"mkb{st}", 1024) for st in range(4)]
            for st in range(4):
                p.dma("qACT", mkb[st], mkb[st][0:128, 0:1024], self.blk0,
                      self.blk0.t[128 * st:128 * (st + 1), :])
        else:
            mkb = None
        PT = [A.alloc(f"PT{c}", 512) for c in range(8)]
        Sm2 = [A.alloc(f"Sm{i}", 1024) for i in range(2)]
        rl2 = [A.alloc(f"rl{i}", 2) for i in range(2)]
        bandb = self.bandb
        for hh in range(NH):
            pr, par = hh // 2, hh % 2
            qp = self.qpadO if par else self.qpadE
            p.copy("DVE", qp, qp[64 * par:64 * par + 64, 0:512], qd[pr],
                   qd[pr][64 * par:64 * par + 64, 0:512])
            for st in range(4):
                Sm, rl = Sm2[st % 2], rl2[st % 2]
                b0, b1 = (self.pb[0], self.pb[1]) if st % 2 == 0 else \
                    (self.pb[5], self.pb[6])
                p.mm(b0, b0[0:128, 0:512], qp,
                     qp[0:128, 128 * st:128 * st + 128],
                     kslot[prv][pr], kslot[prv][pr][0:128, 0:512],
                     start=True, stop=True)
                p.mm(b1, b1[0:128, 0:512], qp,
                     qp[0:128, 128 * st:128 * st + 128],
                     kslot[sl][pr], kslot[sl][pr][0:128, 0:512],
                     start=True, stop=True)
                if w == 0:
                    p.tt("DVE", ALU.add, Sm, Sm[0:128, 0:512], b0,
                         b0[0:128, 0:512], mkb[st], mkb[st][0:128, 0:512])
                    p.tt("DVE", ALU.add, Sm, Sm[0:128, 512:1024], b1,
                         b1[0:128, 0:512], mkb[st], mkb[st][0:128, 512:1024])
                else:
                    c0b = 384 - st * 128
                    p.tt("DVE", ALU.add, Sm, Sm[0:128, 0:512], b0,
                         b0[0:128, 0:512], bandb, bandb[0:128, c0b:c0b + 512])
                    p.tt("DVE", ALU.add, Sm, Sm[0:128, 512:1024], b1,
                         b1[0:128, 0:512], bandb,
                         bandb[0:128, c0b + 512:c0b + 1024])
                p.reduce("DVE", rl, rl[0:128, 0:1], Sm, Sm[0:128, 0:1024],
                         op=ALU.max, negate=True)
                p.act(Sm, Sm[0:128, 0:1024], Sm, Sm[0:128, 0:1024], AF.Exp,
                      bias=(rl, rl[0:128, 0:1]), accum=(rl, rl[0:128, 1:2]))
                p.recip(rl, rl[0:128, 0:1], rl, rl[0:128, 1:2])
                p.act(Sm, Sm[0:128, 0:1024], Sm, Sm[0:128, 0:1024], AF.Copy,
                      scale=(rl, rl[0:128, 0:1]))
                for c in range(8):
                    bt = self.pb[2 + (c % 2)]
                    p.transpose(bt, bt[0:128, 0:128], Sm,
                                Sm[0:128, 128 * c:128 * (c + 1)], self.ident)
                    p.copy("ACT", PT[c], PT[c][0:128, 128 * st:128 * st + 128],
                           bt, bt[0:128, 0:128])
            bat = self.pb[4]
            for c in range(8):
                vt = vslot[prv][c] if c < 4 else vslot[sl][c - 4]
                p.mm(bat, bat[64 * par:64 * par + 64, 0:512], vt,
                     vt[0:128, 64 * hh:64 * hh + 64], PT[c],
                     PT[c][0:128, 0:512], start=(c == 0), stop=(c == 7))
            if par == 1:
                p.copy("DVE", ot[pr], ot[pr][0:128, 0:512], bat,
                       bat[0:128, 0:512])
        A.pop()
        wb = [A.alloc(f"fwo{i}", 512) for i in range(2)]
        for m in range(8):
            wv_ = wb[m % 2]
            p.dma("qSP", wv_, wv_[0:128, 0:512], self.wo[l],
                  self.wo[l].t[m * 128:(m + 1) * 128, :])
            bank = self.pb[5 + (m % 2)]
            for kk in range(4):
                p.mm(bank, bank[0:128, 0:512], wv_,
                     wv_[0:128, 128 * kk:128 * (kk + 1)],
                     ot[kk], ot[kk][0:128, 0:512],
                     start=(kk == 0), stop=(kk == 3))
            p.tt("DVE", ALU.add, xt[m], xt[m][0:128, 1:513], bank,
                 bank[0:128, 0:512], xt[m], xt[m][0:128, 1:513])
        A.pop()
        for m in range(8):
            if w == 0:
                p.copy("DVE", xt[m], xt[m][0:128, 0:1], self.x1acol,
                       self.x1acol[0:128, m:m + 1])
            else:
                p.copy("DVE", xt[m], xt[m][0:128, 0:1], self.x1carry,
                       self.x1carry[0:128, m:m + 1])
        for m in range(8):
            p.copy("DVE", self.x1carry, self.x1carry[0:128, m:m + 1],
                   xt[m], xt[m][0:128, 512:513])
        A.push()
        x2t = A.alloc("fx2", 512)
        hmid = [A.alloc(f"hm{j}", 512) for j in range(22)]
        A.push()
        bs, bq = self.pb[0], self.pb[1]
        caps = []
        for j in range(8):
            off = 1 if j < 4 else 0
            caps.append(xt[j][0:128, off:off + 512])
        self.ln_stats(xt, caps, 512, bs, bq, x2t)
        self.ln_rows(bs, bq, 512, self.rinv, self.mur, eps_idx=0, dinv_idx=2)
        bmu, br = self.pb[2], self.pb[3]
        self.bcast(bmu, self.mur, 0, 512)
        self.bcast(br, self.rinv, 0, 512)
        hn = [A.alloc(f"hn{j}", 512) for j in range(8)]
        for j in range(8):
            p.tt("DVE", ALU.mult, hn[j], hn[j][0:128, 0:512], xt[j], caps[j],
                 br, br[0:128, 0:512])
            p.tt("DVE", ALU.subtract, hn[j], hn[j][0:128, 0:512], hn[j],
                 hn[j][0:128, 0:512], bmu, bmu[0:128, 0:512])
            p.ts("DVE", hn[j], hn[j][0:128, 0:512], hn[j],
                 hn[j][0:128, 0:512],
                 (self.lg[l], self.lg[l][0:128, j:j + 1]),
                 (self.lbv[l], self.lbv[l][0:128, j:j + 1]),
                 ALU.mult, ALU.add)
        wb1 = [A.alloc(f"fw1_{i}", 1024) for i in range(2)]
        gel = A.alloc("gel", 512)
        for j in range(22):
            for half in range(2):
                m = 2 * j + half
                wv_ = wb1[m % 2]
                p.dma("qSP", wv_, wv_[0:128, 0:1024], self.w1[l],
                      self.w1[l].t[m * 128:(m + 1) * 128, :])
                bank = self.pb[4 + half]
                for kk in range(8):
                    p.mm(bank, bank[0:128, 0:512], wv_,
                         wv_[0:128, 128 * kk:128 * (kk + 1)], hn[kk],
                         hn[kk][0:128, 0:512], start=(kk == 0), stop=(kk == 7))
            nrow = 128 if j < 21 else HID - 128 * 21
            bA, bG = self.pb[4], self.pb[5]
            p.act(gel, gel[0:nrow, 0:512], bG, bG[0:nrow, 0:512], AF.Gelu,
                  bias=(self.b1[l], self.b1[l][0:nrow, 2 * j + 1:2 * j + 2]))
            p.ts("DVE", hmid[j], hmid[j][0:nrow, 0:512], bA, bA[0:nrow, 0:512],
                 (self.b1[l], self.b1[l][0:nrow, 2 * j:2 * j + 1]), None,
                 ALU.add)
            p.tt("DVE", ALU.mult, hmid[j], hmid[j][0:nrow, 0:512], hmid[j],
                 hmid[j][0:nrow, 0:512], gel, gel[0:nrow, 0:512])
            if nrow < 128:
                p.memset("DVE", hmid[j], hmid[j][64:128, 0:512], 0.0)
        A.pop()
        wb2 = [A.alloc(f"fw2_{i}", 2816) for i in range(2)]
        for m in range(8):
            wv_ = wb2[m % 2]
            p.dma("qSP", wv_, wv_[0:128, 0:2816], self.w2[l],
                  self.w2[l].t[m * 128:(m + 1) * 128, :])
            bank = self.pb[6 + (m % 2)]
            for kk in range(22):
                p.mm(bank, bank[0:128, 0:512], wv_,
                     wv_[0:128, 128 * kk:128 * (kk + 1)], hmid[kk],
                     hmid[kk][0:128, 0:512], start=(kk == 0), stop=(kk == 21))
            p.stt("DVE", x2t, x2t[0:128, 0:512], bank, bank[0:128, 0:512],
                  (self.b2[l], self.b2[l][0:128, m:m + 1]),
                  xt[m], xt[m][0:128, 1:513], ALU.add, ALU.add)
            if l == 0:
                p.dma("qPOOL", self.xffn_b,
                      self.xffn_b.t[128 * m:128 * (m + 1), qb:qb + 512],
                      x2t, x2t[0:128, 0:512])
                g0 = T - HALO
                lo = max(g0, qb)
                if lo < qb + 512:
                    o = lo - qb
                    p.dma("qPOOL", self.e3_in,
                          self.e3_in.t[128 * m:128 * (m + 1),
                                       lo - g0:qb + 512 - g0],
                          x2t, x2t[0:128, o:512])
            else:
                xm = A.alloc("xmt", 512) if m == 0 else xm
                p.dma("qACT", xm, xm[0:128, 0:512], self.xmid,
                      self.xmid.t[128 * m:128 * (m + 1), qb:qb + 512])
                p.stt("DVE", x2t, x2t[0:128, 0:512], x2t, x2t[0:128, 0:512],
                      (self.mod, self.mod[0:128, 40 + m:41 + m]),
                      xm, xm[0:128, 0:512], ALU.mult, ALU.add)
                p.dma("qPOOL", self.out_fm,
                      self.out_fm.t[128 * m:128 * (m + 1), qb:qb + 512],
                      x2t, x2t[0:128, 0:512])
        A.pop()
        A.pop()

    def finish(self):
        pass


def build(dbg=(), upto="mamba_a"):
    nc = bass.Bass("TRN2", target_bir_lowering=False, debug=False)
    p = Prog(nc)
    k = Kern(p, dbg)
    k.setup()
    k.mod_vector()
    stages = ["mamba_a", "e1", "mamba_b", "ffn1", "ffn2", "finish"]
    iu = stages.index(upto)
    k.mamba_a()
    if iu >= 1:
        k.e1_exchange()
    if iu >= 2:
        k.mamba_b()
    if iu >= 3:
        k.ffn_layer(0)
    if iu >= 4:
        k.ffn_layer(1)
    if iu >= 5:
        k.finish()
    p.wait_all("POOL", [s for s in p.cnt if s.startswith("q")] + ["cc"])
    p.finalize()
    return nc, p, k


def kernel(**inputs):
    maps = stage_inputs(inputs)
    nc, p, k = build(upto="finish")
    res = run_bass_kernel_spmd(nc, maps, core_ids=list(range(8)))
    out = np.empty((NB, 8192, D), np.float32)
    for c in range(8):
        b, q = c // 4, c % 4
        out[b, T * q:T * (q + 1), :] = np.asarray(res.results[c]["out_fm"]).T
    return out


# version 05
# speedup vs baseline: 1.0129x; 1.0040x over previous
import contextlib
import math

import numpy as np

import concourse.bass as bass
import concourse.mybir as mybir
from concourse.bass_utils import run_bass_kernel_spmd

F32 = mybir.dt.float32
BF16 = mybir.dt.bfloat16
AF = mybir.ActivationFunctionType
ALU = mybir.AluOpType
AX = mybir.AxisListType

NEG_BIG = -3.0e38


ENGS = ("PE", "ACT", "DVE", "POOL", "SP")
_BLOCK_ATTR = {"PE": "tensor", "ACT": "scalar", "DVE": "vector",
               "POOL": "gpsimd", "SP": "sync"}


class Buf:
    __slots__ = ("name", "t", "last_w", "readers", "prow", "pcol", "ncol",
                 "space", "init_deps")

    def __init__(self, name, t, space, prow=0, pcol=0, ncol=None):
        self.name = name
        self.t = t
        self.space = space
        self.prow = prow
        self.pcol = pcol
        self.ncol = ncol
        self.last_w = {}
        self.readers = {}
        self.init_deps = {}

    def __getitem__(self, idx):
        if not isinstance(idx, tuple):
            idx = (idx, slice(None))
        pidx, cidx = idx
        p0, p1 = pidx.start or 0, pidx.stop if pidx.stop is not None else 128
        c0 = cidx.start or 0
        c1 = cidx.stop if cidx.stop is not None else self.ncol
        return self.t[self.prow + p0:self.prow + p1,
                      self.pcol + c0:self.pcol + c1]

    @property
    def ap(self):
        return self[:, :]


class Arena:

    def __init__(self, t, ncols, name):
        self.t = t
        self.ncols = ncols
        self.name = name
        self.free_at = 0
        self.live = []
        self.dead = []
        self.scopes = []

    def push(self):
        self.scopes.append(self.free_at)

    def pop(self):
        base = self.scopes.pop()
        still = []
        for (s, e, b) in self.live:
            if s >= base:
                deps = dict(b.last_w)
                _dmerge(deps, b.readers)
                self.dead.append((s, e, deps))
            else:
                still.append((s, e, b))
        self.live = still
        self.free_at = base

    def alloc(self, name, ncol, align=1):
        start = self.free_at
        if align > 1:
            start = (start + align - 1) // align * align
        end = start + ncol
        if end > self.ncols:
            raise MemoryError(
                f"arena {self.name} overflow: need {ncol} at {start}, "
                f"cap {self.ncols}")
        self.free_at = end
        b = Buf(name, self.t, "sbuf", prow=0, pcol=start, ncol=ncol)
        deps = {}
        for (s, e, d) in self.dead:
            if s < end and e > start:
                _dmerge(deps, d)
        b.init_deps = deps
        self.live.append((start, end, b))
        return b


def _dmerge(dst, src_):
    for s, c in src_.items():
        if dst.get(s, -1) < c:
            dst[s] = c


class Prog:
    def __init__(self, nc):
        self.nc = nc
        self.es = contextlib.ExitStack()
        self.cnt = {}
        self.know = {e: {} for e in ENGS}
        self.sems = {}
        self.q = {e: [] for e in ENGS}
        self.ninst = {e: 0 for e in ENGS}
        for e in ENGS:
            self._mk_sem("s" + e)
        self.NDSEM = 8
        for qn in ("qSP", "qPOOL", "qACT"):
            for i in range(self.NDSEM):
                self._mk_sem(qn + str(i))
        self._mk_sem("cc")
        self.dma_queue_eng = {"qSP": "SP", "qPOOL": "POOL", "qACT": "ACT"}
        self.dma_rr = {"qSP": 0, "qPOOL": 0, "qACT": 0}
        self.dma_inflight = {}
        self.dram_tensors = {}

    def _mk_sem(self, name):
        self.sems[name] = self.es.enter_context(self.nc.semaphore(name))
        self.cnt[name] = 0

    def sbuf(self, name, shape, dtype=F32):
        t = self.es.enter_context(self.nc.sbuf_tensor(name, list(shape), dtype))
        return Buf(name, t, "sbuf", ncol=shape[1])

    def arena(self, name, ncols, dtype=F32):
        t = self.es.enter_context(self.nc.sbuf_tensor(name, [128, ncols], dtype))
        return Arena(t, ncols, name)

    def psum(self, name):
        t = self.es.enter_context(self.nc.psum_tensor(name, [128, 512], F32))
        return Buf(name, t, "psum", ncol=512)

    def din(self, name, shape, dtype=F32):
        t = self.nc.dram_tensor(name, list(shape), dtype, kind="ExternalInput")
        b = Buf(name, t, "dram", ncol=shape[-1])
        self.dram_tensors[name] = b
        return b

    def dout(self, name, shape, dtype=F32):
        t = self.nc.dram_tensor(name, list(shape), dtype, kind="ExternalOutput")
        b = Buf(name, t, "dram", ncol=shape[-1])
        self.dram_tensors[name] = b
        return b

    def dram(self, name, shape, dtype=F32):
        t = self.nc.dram_tensor(name, list(shape), dtype)
        b = Buf(name, t, "dram", ncol=shape[-1])
        self.dram_tensors[name] = b
        return b

    def _waits_for(self, eng, reads, writes):
        need = {}
        for b in reads:
            _dmerge(need, b.last_w)
            if b.space == "psum":
                _dmerge(need, b.readers)
        for b in writes:
            _dmerge(need, b.last_w)
            _dmerge(need, b.readers)
            _dmerge(need, b.init_deps)
        out = []
        for s, c in need.items():
            if self.know[eng].get(s, 0) < c:
                out.append((s, c))
                self.know[eng][s] = c
        return out

    def _post(self, sem, reads, writes):
        n = self.cnt[sem]
        for b in writes:
            _dmerge(b.last_w, {sem: n})
        for b in reads:
            _dmerge(b.readers, {sem: n})

    def op(self, eng, fn, reads=(), writes=(), acc=False):
        reads = list(reads)
        writes = list(writes)
        if acc:
            waits = self._waits_for(eng, reads, [])
            extra = {}
            for b in writes:
                _dmerge(extra, b.readers)
                _dmerge(extra, b.init_deps)
            for s, c in extra.items():
                if self.know[eng].get(s, 0) < c:
                    waits.append((s, c))
                    self.know[eng][s] = c
        else:
            waits = self._waits_for(eng, reads, writes)
        sem = "s" + eng
        self.cnt[sem] += 1
        n = self.cnt[sem]
        semh = self.sems
        wl = list(waits)

        def emit(engobj, fn=fn, wl=wl, sem=sem, n=n):
            for (s, c) in wl:
                engobj.wait_ge(semh[s], c)
            ins = fn(engobj)
            ins.then_inc(semh[sem], 1)
        self.q[eng].append(emit)
        self.ninst[eng] += 1 + len(wl)
        self._post(sem, reads, writes)
        return n

    def dma(self, queue, out, out_ap, in_, in_ap, **kw):
        eng = self.dma_queue_eng[queue]
        i = self.dma_rr[queue]
        self.dma_rr[queue] = (i + 1) % self.NDSEM
        sname = queue + str(i)
        waits = self._waits_for(eng, [in_], [out])
        prev = self.dma_inflight.get(sname)
        if prev is not None and self.know[eng].get(sname, 0) < prev:
            waits.append((sname, prev))
            self.know[eng][sname] = prev
        self.cnt[sname] += 16
        n = self.cnt[sname]
        self.dma_inflight[sname] = n
        semh = self.sems

        def emit(engobj, out_ap=out_ap, in_ap=in_ap, wl=list(waits), kw=kw):
            for (s, c) in wl:
                engobj.wait_ge(semh[s], c)
            engobj.dma_start(out=out_ap, in_=in_ap, **kw).then_inc(
                semh[sname], 16)
        self.q[eng].append(emit)
        self.ninst[eng] += 1 + len(waits)
        _dmerge(out.last_w, {sname: n})
        _dmerge(in_.readers, {sname: n})
        return (sname, n)

    def wait_all(self, eng, sems=None):
        wl = []
        for s in (sems or list(self.cnt)):
            c = self.cnt[s]
            if c > 0 and self.know[eng].get(s, 0) < c:
                wl.append((s, c))
                self.know[eng][s] = c
        semh = self.sems

        def emit(engobj, wl=wl):
            for (s, c) in wl:
                engobj.wait_ge(semh[s], c)
        self.q[eng].append(emit)
        self.ninst[eng] += len(wl)

    def mm(self, psum, pap, lhsT, lap, rhs, rap, start, stop, extra_reads=()):
        return self.op(
            "PE", lambda t: t.matmul(pap, lap, rap, start=start, stop=stop),
            reads=[lhsT, rhs, *extra_reads], writes=[psum], acc=not start)

    def transpose(self, psum, pap, src, sap, ident):
        return self.op(
            "PE", lambda t: t.transpose(pap, sap, ident.ap),
            reads=[src, ident], writes=[psum])

    def copy(self, eng, dst, dap, src, sap):
        if eng == "ACT":
            return self.op(eng, lambda e: e.copy(dap, sap),
                           reads=[src], writes=[dst])
        return self.op(eng, lambda e: e.tensor_copy(dap, sap),
                       reads=[src], writes=[dst])

    def memset(self, eng, dst, dap, val=0.0):
        return self.op(eng, lambda e: e.memset(dap, val), writes=[dst])

    def tt(self, eng, aop, dst, dap, a, aap, b, bap):
        return self.op(eng, lambda e: e.tensor_tensor(dap, aap, bap, aop),
                       reads=[a, b], writes=[dst])

    def ts(self, eng, dst, dap, a, aap, s1, s2=None, op0=ALU.mult,
           op1=ALU.bypass, sbufs=()):
        reads = [a]
        v1, v2 = s1, s2
        if isinstance(s1, tuple):
            reads.append(s1[0]); v1 = s1[1]
        if isinstance(s2, tuple):
            reads.append(s2[0]); v2 = s2[1]
        return self.op(eng, lambda e: e.tensor_scalar(dap, aap, v1, v2, op0, op1),
                       reads=reads, writes=[dst])

    def stt(self, eng, dst, dap, a, aap, scal, b, bap, op0, op1):
        reads = [a, b]
        v = scal
        if isinstance(scal, tuple):
            reads.append(scal[0]); v = scal[1]
        return self.op(
            eng, lambda e: e.scalar_tensor_tensor(dap, aap, v, bap, op0, op1),
            reads=reads, writes=[dst])

    def act(self, dst, dap, src, sap, func, bias=0.0, scale=1.0, accum=None):
        reads = [src]
        writes = [dst]
        bv, sv = bias, scale
        if isinstance(bias, tuple):
            reads.append(bias[0]); bv = bias[1]
        if isinstance(scale, tuple):
            reads.append(scale[0]); sv = scale[1]
        aap = None
        if accum is not None:
            writes.append(accum[0]); aap = accum[1]
        return self.op(
            "ACT",
            lambda e: e.activation(dap, sap, func, bias=bv, scale=sv,
                                   accum_out=aap),
            reads=reads, writes=writes)

    def reduce(self, eng, dst, dap, src, sap, op=ALU.max, negate=False):
        return self.op(
            eng, lambda e: e.tensor_reduce(dap, sap, AX.X, op, negate=negate),
            reads=[src], writes=[dst])

    def recip(self, dst, dap, src, sap):
        return self.op("DVE", lambda e: e.reciprocal(dap, sap),
                       reads=[src], writes=[dst])

    def scan(self, eng, dst, dap, d0, d0ap, d1, d1ap, init=0.0):
        return self.op(
            eng, lambda e: e.tensor_tensor_scan(dap, d0ap, d1ap, init,
                                                ALU.mult, ALU.add),
            reads=[d0, d1], writes=[dst])

    def collective(self, kind, op, groups, in_b, out_b):
        waits = self._waits_for("POOL", [in_b], [out_b])
        self.cnt["cc"] += 1
        n = self.cnt["cc"]
        semh = self.sems

        def emit(engobj, wl=list(waits)):
            for (s, c) in wl:
                engobj.wait_ge(semh[s], c)
            engobj.collective_compute(
                kind, op, replica_groups=groups,
                ins=[in_b.t.ap().opt()], outs=[out_b.t.ap().opt()],
            ).then_inc(semh["cc"], 1)
        self.q["POOL"].append(emit)
        self.ninst["POOL"] += 1 + len(waits)
        _dmerge(out_b.last_w, {"cc": n})
        _dmerge(in_b.readers, {"cc": n})
        return n

    def finalize(self):
        with self.nc.Block() as block:
            for e in ENGS:
                fns = self.q[e]
                if not fns:
                    continue
                def body(engobj, fns=fns):
                    for f in fns:
                        f(engobj)
                getattr(block, _BLOCK_ATTR[e])(body)
        self.es.close()
        return self.nc


D = 1024
T = 2048
NB = 2
NQ = 4
DIN = 2048
NH = 8
HD = 256
NST = 64
QCH = 128
MTILE = 512
HALO = 644
KHALO = 640
WIN = 512
AHD = 64
AH = 8
HID = 2730
HIDP = 2816


def _chunked(w, kdim=128, mdim=128):
    K, M = w.shape
    nk = (K + kdim - 1) // kdim
    nm = (M + mdim - 1) // mdim
    wp = np.zeros((nk * kdim, nm * mdim), np.float32)
    wp[:K, :M] = w
    out = np.zeros((nm * 128, nk * mdim), np.float32)
    for m in range(nm):
        blk = wp[:, m * mdim:(m + 1) * mdim]
        out[m * 128:(m + 1) * 128] = (
            blk.reshape(nk, 128, mdim).transpose(1, 0, 2).reshape(128, nk * mdim))
    return np.ascontiguousarray(out)


def stage_inputs(inputs):
    x = np.asarray(inputs["L0_attn_x"], np.float32)
    maps = []
    ipw = np.asarray(inputs["L0_attn_in_proj_w"], np.float32)
    perm = np.concatenate([np.arange(0, 4096),
                           np.arange(4168, 4232),
                           np.arange(4104, 4168),
                           np.arange(4096, 4104)])
    wip = _chunked(np.ascontiguousarray(ipw.T[:, perm]))
    wop = _chunked(np.ascontiguousarray(
        np.asarray(inputs["L0_attn_out_proj_w"], np.float32).T))
    wmod = _chunked(np.asarray(inputs["L0_wmod"], np.float32))
    wqkv = [_chunked(np.asarray(inputs["L0_ffn_wqkv"], np.float32)[l])
            for l in range(2)]
    wo = [_chunked(np.asarray(inputs["L0_ffn_wo"], np.float32)[l])
          for l in range(2)]
    w1p, b1p = [], []
    for l in range(2):
        w1l = np.asarray(inputs["L0_ffn_w1"], np.float32)[l]
        b1l = np.asarray(inputs["L0_ffn_b1"], np.float32)[l]
        wpad = np.zeros((1024, 44 * 128), np.float32)
        bb = np.zeros((128, 44), np.float32)
        for j in range(22):
            a0, a1 = 128 * j, min(128 * (j + 1), HID)
            n = a1 - a0
            wpad[:, 256 * j:256 * j + n] = w1l[:, a0:a1]
            wpad[:, 256 * j + 128:256 * j + 128 + n] = w1l[:, HID + a0:HID + a1]
            bb[:n, 2 * j] = b1l[a0:a1]
            bb[:n, 2 * j + 1] = b1l[HID + a0:HID + a1]
        w1p.append(_chunked(wpad))
        b1p.append(bb)
    w2 = [_chunked(np.pad(np.asarray(inputs["L0_ffn_w2"], np.float32)[l],
                          ((0, HIDP - HID), (0, 0))))
          for l in range(2)]
    def fm(v, nch):
        return np.ascontiguousarray(
            np.asarray(v, np.float32).reshape(nch, 128).T)
    lg = [fm(np.asarray(inputs["L0_ffn_ln_g"], np.float32)[l], 8) for l in range(2)]
    lb = [fm(np.asarray(inputs["L0_ffn_ln_b"], np.float32)[l], 8) for l in range(2)]
    b2 = [fm(np.asarray(inputs["L0_ffn_b2"], np.float32)[l], 8) for l in range(2)]
    qsc = [fm(np.tile(np.asarray(inputs["L0_ffn_q_scale"], np.float32)[l], 8), 4)
           for l in range(2)]
    ksc = [fm(np.tile(np.asarray(inputs["L0_ffn_k_scale"], np.float32)[l], 8), 4)
           for l in range(2)]
    convw = np.ascontiguousarray(
        np.asarray(inputs["L0_attn_conv_w"], np.float32).reshape(16, 128, 4)
        .transpose(1, 0, 2).reshape(128, 64))
    convb = fm(inputs["L0_attn_conv_b"], 16)
    nw = fm(inputs["L0_attn_norm_w"], 16)
    Dp = np.tile(np.asarray(inputs["L0_attn_D_param"], np.float32)[None, :],
                 (128, 1))
    Alog = np.asarray(inputs["L0_attn_A_log"], np.float32).reshape(8, 1)
    dtb = np.asarray(inputs["L0_attn_dt_bias"], np.float32).reshape(8, 1)
    ident = np.eye(128, dtype=np.float32)
    sel = np.zeros((128, 8 * 128), np.float32)
    for r in range(8):
        sel[r, r * 128:(r + 1) * 128] = 1.0
    hsel2 = np.zeros((128, 128), np.float32)
    hsel2[0, :64] = 1.0
    hsel2[1, 64:] = 1.0
    hsel = np.zeros((128, 2), np.float32)
    hsel[:64, 0] = 1.0
    hsel[64:, 1] = 1.0
    rot = np.zeros((128, 128), np.float32)
    for b0 in (0, 64):
        for d_ in range(32):
            rot[b0 + d_ + 32, b0 + d_] = -1.0
            rot[b0 + d_, b0 + d_ + 32] = 1.0
    tri = np.tril(np.ones((128, 128), np.float32)).T
    band = np.full((128, 1408), NEG_BIG, np.float32)
    for p in range(128):
        v0, v1 = 384 + p, 384 + p + 512
        band[p, v0:v1 + 1] = 0.0
    krow = np.array([[1e-5, 1e-6, 1.0 / D, 1.0 / DIN, 0.25, 1.0, 0.0, 1e-12]],
                    np.float32)
    kcol = np.tile(krow, (128, 1)).astype(np.float32)
    mrow = np.full((1, 512), -1.0, np.float32)
    N_FULL = 8192
    cosf, sinf, scalef = _rope_tables(N_FULL)
    for c in range(8):
        b, q = c // 4, c % 4
        s = T * q
        m = {}
        xt = x[b].T
        xh3 = np.zeros((1024, T + 3), np.float32)
        lo = max(0, s - 3)
        xh3[:, 3 - (s - lo):] = xt[:, lo:s + T]
        m["xin"] = np.ascontiguousarray(xh3)
        m.update(wip=wip, wop=wop, wmod=wmod,
                 wqkv0=wqkv[0], wqkv1=wqkv[1], wo0=wo[0], wo1=wo[1],
                 w10=w1p[0], w11=w1p[1], w20=w2[0], w21=w2[1],
                 b10=b1p[0], b11=b1p[1], b20=b2[0], b21=b2[1],
                 lg0=lg[0], lg1=lg[1], lb0=lb[0], lb1=lb[1],
                 qsc0=qsc[0], qsc1=qsc[1], ksc0=ksc[0], ksc1=ksc[1],
                 convw=convw, convb=convb, nw=nw, Dp=Dp, Alog=Alog, dtb=dtb,
                 ident=ident, sel=sel, hsel2=hsel2, hsel=hsel, rot=rot,
                 tri=tri, band=band, krow=krow, kcol=kcol, mrow=mrow)
        m["cvec"] = fm(np.asarray(inputs["L0_c"], np.float32)[b], 8)
        def ftab(tab, pos):
            v = tab[np.clip(pos, 0, N_FULL - 1)]
            return np.ascontiguousarray(np.tile(v.T, (2, 1)).astype(np.float32))
        posq = np.arange(s - 128, s + T)
        posk = np.arange(s - KHALO, s + T)
        m["ropeqc"] = ftab(cosf * scalef * 8.0, posq)
        m["ropeqs"] = ftab(sinf * scalef * 8.0, posq)
        m["ropekc"] = ftab(cosf / scalef, posk)
        m["ropeks"] = ftab(sinf / scalef, posk)
        blk0 = np.zeros((4 * 128, 1024), np.float32)
        for st in range(4):
            sl = band[:, 384 - st * 128:384 - st * 128 + 1024].copy()
            if q == 0:
                sl[:, :512] = NEG_BIG
            blk0[st * 128:(st + 1) * 128] = sl
        m["blk0"] = blk0
        hm = np.full((128, KHALO), NEG_BIG, np.float32)
        if q > 0:
            for p in range(128):
                hm[p, p:p + 513] = 0.0
        m["hmask"] = hm
        m["hgate"] = np.full((128, 1), 0.0 if q == 0 else 1.0, np.float32)
        g = np.zeros((128, 4), np.float32)
        g1m = np.zeros((128, 4), np.float32)
        for j in range(3):
            g[:, j] = 1.0 if j < q else 0.0
            g1m[:, j] = 0.0 if j < q else 1.0
        m["e1g"] = g
        m["e1g1m"] = g1m
        es = np.zeros((128, 4), np.float32)
        if q > 0:
            es[:, q - 1] = 1.0
        m["eselg"] = es
        maps.append(m)
    return maps


def _rope_tables(n):
    half = np.arange(0, AHD, 2, dtype=np.float64)
    inv_freq = 1.0 / 10000.0 ** (half / AHD)
    t = np.arange(n, dtype=np.float64)
    freqs = t[:, None] * inv_freq[None, :]
    freqs = np.concatenate([freqs, freqs], axis=-1)
    base = (half + 0.4 * AHD) / (1.4 * AHD)
    power = (t - n // 2) / float(WIN // 2)
    scale = base[None, :] ** power[:, None]
    scale = np.concatenate([scale, scale], axis=-1)
    return np.cos(freqs), np.sin(freqs), scale


class Kern:
    def __init__(self, p, dbg=()):
        self.p = p
        self.dbg = set(dbg)
        self.A = p.arena("A", 46000)
        self.pb = [p.psum(f"pb{i}") for i in range(8)]
        self.dbg_outs = {}
        self.n_dbg = 0

    def dump(self, name, buf, prow=128, cols=None):
        if name not in self.dbg:
            return
        p = self.p
        cols = cols if cols is not None else buf.ncol
        d = p.dout("dbg_" + name, [prow, cols])
        p.dma("qPOOL", d, d.t[:, :], buf, buf[0:prow, 0:cols])
        self.dbg_outs[name] = (prow, cols)

    def bcast(self, bank, src, row, ncols, c0=0):
        p = self.p
        rhs_ap = src.t[0:128, src.pcol + c0:src.pcol + c0 + ncols]
        p.mm(bank, bank[0:128, 0:ncols], self.sel,
             self.sel[:, 128 * row:128 * (row + 1)], src, rhs_ap,
             start=True, stop=True)

    def ln_stats(self, chunks, caps, n, b_sum, b_sq, x2l, nch=8):
        p = self.p
        if not isinstance(x2l, list):
            x2l = [x2l]
        for j in range(nch):
            cb, cap = chunks[j], caps[j]
            x2t = x2l[j % len(x2l)]
            p.act(x2t, x2t[0:128, 0:n], cb, cap, AF.Square)
            p.mm(b_sum, b_sum[0:1, 0:n], self.onescol, self.onescol[:, 0:1],
                 cb, cap, start=(j == 0), stop=(j == nch - 1))
            p.mm(b_sq, b_sq[0:1, 0:n], self.onescol, self.onescol[:, 0:1],
                 x2t, x2t[0:128, 0:n], start=(j == 0), stop=(j == nch - 1))

    def ln_rows(self, b_sum, b_sq, n, rinv, mur, eps_idx, dinv_idx):
        p = self.p
        kr = self.krow
        t0, t1, t2 = self.r0, self.r1, self.r2
        p.ts("DVE", t0, t0[0:1, 0:n], b_sum, b_sum[0:1, 0:n],
             (kr, kr[0:1, dinv_idx:dinv_idx + 1]), None, ALU.mult)
        p.ts("DVE", t1, t1[0:1, 0:n], b_sq, b_sq[0:1, 0:n],
             (kr, kr[0:1, dinv_idx:dinv_idx + 1]), None, ALU.mult)
        p.act(t2, t2[0:1, 0:n], t0, t0[0:1, 0:n], AF.Square)
        p.tt("DVE", ALU.subtract, t1, t1[0:1, 0:n], t1, t1[0:1, 0:n],
             t2, t2[0:1, 0:n])
        p.act(t2, t2[0:1, 0:n], t1, t1[0:1, 0:n], AF.Sqrt,
              bias=(kr, kr[0:1, eps_idx:eps_idx + 1]))
        p.recip(rinv, rinv[0:1, 0:n], t2, t2[0:1, 0:n])
        p.tt("DVE", ALU.mult, mur, mur[0:1, 0:n], t0, t0[0:1, 0:n],
             rinv, rinv[0:1, 0:n])

    def setup(self):
        p = self.p
        A = self.A
        zbuf = Buf("zall", A.t, "sbuf", ncol=A.ncols)
        p.memset("DVE", zbuf, A.t[0:128, 0:A.ncols], 0.0)
        A.dead.append((0, A.ncols, dict(zbuf.last_w)))
        def cload(name, ncol, prow=128, dname=None, q="qSP"):
            d = p.din(dname or name, [prow, ncol])
            b = A.alloc(name, ncol)
            p.dma(q, b, b[0:prow, 0:ncol], d, d.t[:, :])
            return b
        self.sel = cload("sel", 1024)
        self.ident = cload("ident", 128)
        self.hsel2 = cload("hsel2", 128)
        self.hsel = cload("hsel", 2)
        self.rot = cload("rot", 128)
        self.tri = cload("tri", 128)
        self.onescol = A.alloc("onescol", 1)
        p.ts("DVE", self.onescol, self.onescol[0:128, 0:1], self.hsel,
             self.hsel[0:128, 0:1], 0.0, 1.0, ALU.mult, ALU.add)
        self.krow = cload("krow", 8, prow=1)
        self.kcol = cload("kcol", 8)
        self.convw = cload("convw", 64)
        self.convb = cload("convb", 16)
        self.nw = cload("nw", 16)
        self.Dp = cload("Dp", 8)
        self.Alog = cload("Alog", 1, prow=8)
        self.dtb = cload("dtb", 1, prow=8)
        self.cvec = cload("cvec", 8)
        self.hgate = cload("hgate", 1)
        self.e1g = cload("e1g", 4)
        self.e1g1m = cload("e1g1m", 4)
        self.r0 = A.alloc("r0", 512)
        self.r1 = A.alloc("r1", 512)
        self.r2 = A.alloc("r2", 512)
        self.rinv = A.alloc("rinv", 512)
        self.mur = A.alloc("mur", 512)
        self.negA = A.alloc("negA", 1)
        p.act(self.negA, self.negA[0:8, 0:1], self.Alog, self.Alog[0:8, 0:1],
              AF.Exp)
        p.ts("DVE", self.negA, self.negA[0:8, 0:1], self.negA,
             self.negA[0:8, 0:1], -1.0, None, ALU.mult)
        self.wip = p.din("wip", [34 * 128, 8 * 128])
        self.wop = p.din("wop", [8 * 128, 16 * 128])
        self.wmod = p.din("wmod", [48 * 128, 8 * 128])
        self.wqkv = [p.din(f"wqkv{l}", [12 * 128, 8 * 128]) for l in range(2)]
        self.wo = [p.din(f"wo{l}", [8 * 128, 4 * 128]) for l in range(2)]
        self.w1 = [p.din(f"w1{l}", [44 * 128, 8 * 128]) for l in range(2)]
        self.w2 = [p.din(f"w2{l}", [8 * 128, 22 * 128]) for l in range(2)]
        self.b1 = [cload(f"b1_{l}", 44, dname=f"b1{l}") for l in range(2)]
        self.b2 = [cload(f"b2_{l}", 8, dname=f"b2{l}") for l in range(2)]
        self.lg = [cload(f"lg_{l}", 8, dname=f"lg{l}") for l in range(2)]
        self.lbv = [cload(f"lb_{l}", 8, dname=f"lb{l}") for l in range(2)]
        self.qsc = [cload(f"qsc_{l}", 4, dname=f"qsc{l}") for l in range(2)]
        self.ksc = [cload(f"ksc_{l}", 4, dname=f"ksc{l}") for l in range(2)]
        self.xin = p.din("xin", [D, T + 3])
        self.ropeqc = p.din("ropeqc", [128, 128 + T])
        self.ropeqs = p.din("ropeqs", [128, 128 + T])
        self.ropekc = p.din("ropekc", [128, KHALO + T])
        self.ropeks = p.din("ropeks", [128, KHALO + T])
        self.band = p.din("band", [128, 1408])
        self.blk0 = p.din("blk0", [4 * 128, 1024])
        self.hmask = p.din("hmask", [128, KHALO])
        def scr(name, shape):
            return (p.dout if name in self.dbg else p.dram)(name, shape)
        self.z_scr = scr("z_scr", [DIN, T])
        self.y_scr = scr("y_scr", [DIN, T])
        self.xmid = scr("xmid", [D, T])
        self.xffn = scr("xffn", [D, T])
        self.xffn_b = scr("xffn_b", [D, T])
        self.e1_in = p.dram("e1_in", [NH * NST + 8, HD])
        self.e1_out = p.dram("e1_out", [4 * (NH * NST + 8), HD])
        self.e2_in = p.dram("e2_in", [D, HALO])
        self.e2_out = p.dram("e2_out", [4 * D, HALO])
        self.e3_in = p.dram("e3_in", [D, HALO])
        self.e3_out = p.dram("e3_out", [4 * D, HALO])
        self.eselg = cload("eselg", 4)
        self.out_fm = p.dout("out_fm", [D, T])
        self.qpadE = p.sbuf("qpadE", [128, 512])
        self.qpadO = p.sbuf("qpadO", [128, 512])
        p.memset("DVE", self.qpadE, self.qpadE[64:128, 0:512], 0.0)
        p.memset("DVE", self.qpadO, self.qpadO[0:64, 0:512], 0.0)

    def mod_vector(self):
        p, A = self.p, self.A
        self.mod = A.alloc("mod", 48)
        self.modp1 = A.alloc("modp1", 16)
        A.push()
        silc = A.alloc("silc", 8)
        p.act(silc, silc[0:128, 0:8], self.cvec, self.cvec[0:128, 0:8],
              AF.Silu)
        mod = self.mod
        wbuf = [A.alloc(f"wmv{i}", 1024) for i in range(2)]
        for m in range(48):
            w = wbuf[m % 2]
            p.dma("qSP", w, w[0:128, 0:1024], self.wmod,
                  self.wmod.t[m * 128:(m + 1) * 128, :])
            bank = self.pb[m % 8]
            for kk in range(8):
                p.mm(bank, bank[0:128, 0:1], w, w[0:128, 128 * kk:128 * (kk + 1)],
                     silc, silc[0:128, kk:kk + 1], start=(kk == 0), stop=(kk == 7))
            p.copy("ACT", mod, mod[0:128, m:m + 1], bank, bank[0:128, 0:1])
        A.pop()
        p.ts("DVE", self.modp1, self.modp1[0:128, 0:8], self.mod,
             self.mod[0:128, 8:16], 1.0, None, ALU.add)
        p.ts("DVE", self.modp1, self.modp1[0:128, 8:16], self.mod,
             self.mod[0:128, 32:40], 1.0, None, ALU.add)
        t = self.A.alloc("gtmp", 16)
        for (c0, o) in ((16, 0), (40, 8)):
            p.ts("DVE", t, t[0:128, o:o + 8], self.mod,
                 self.mod[0:128, c0:c0 + 8], 0.0, None, ALU.is_ge)
            p.ts("DVE", t, t[0:128, o:o + 8], t, t[0:128, o:o + 8],
                 2.0, -1.0, ALU.mult, ALU.add)
            p.act(self.mod, self.mod[0:128, c0:c0 + 8], self.mod,
                  self.mod[0:128, c0:c0 + 8], AF.Abs)
            p.ts("DVE", self.mod, self.mod[0:128, c0:c0 + 8], self.mod,
                 self.mod[0:128, c0:c0 + 8], 0.25, None, ALU.max)
            p.tt("DVE", ALU.mult, self.mod, self.mod[0:128, c0:c0 + 8],
                 self.mod, self.mod[0:128, c0:c0 + 8], t, t[0:128, o:o + 8])
        self.dump("mod", self.mod)

    def _ln_mod_x(self, xsrc, xc0, n, modoff):
        p, A = self.p, self.A
        xt = [A.alloc(f"lx{j}", n) for j in range(8)]
        for j in range(8):
            p.dma("qACT", xt[j], xt[j][0:128, 0:n], xsrc,
                  xsrc.t[128 * j:128 * (j + 1), xc0:xc0 + n])
        x2t = [A.alloc("lx2a", n), A.alloc("lx2b", n)]
        bs, bq = self.pb[0], self.pb[1]
        self.ln_stats(xt, [b[0:128, 0:n] for b in xt], n, bs, bq, x2t)
        self.ln_rows(bs, bq, n, self.rinv, self.mur, eps_idx=0, dinv_idx=2)
        bmu, br = self.pb[2], self.pb[3]
        self.bcast(bmu, self.mur, 0, n)
        self.bcast(br, self.rinv, 0, n)
        po, mo = modoff
        h = xt
        for j in range(8):
            p.tt("DVE", ALU.mult, h[j], h[j][0:128, 0:n],
                 xt[j], xt[j][0:128, 0:n], br, br[0:128, 0:n])
            p.tt("DVE", ALU.subtract, h[j], h[j][0:128, 0:n],
                 h[j], h[j][0:128, 0:n], bmu, bmu[0:128, 0:n])
            p.ts("DVE", h[j], h[j][0:128, 0:n], h[j], h[j][0:128, 0:n],
                 (self.modp1, self.modp1[0:128, po + j:po + j + 1]),
                 (self.mod, self.mod[0:128, mo + j:mo + j + 1]),
                 ALU.mult, ALU.add)
        return h

    def _inproj(self, h, n, c0, mini=False):
        p, A = self.p, self.A
        wb = [A.alloc(f"wip{i}", 1024) for i in range(4)]
        zt = [A.alloc(f"zt{i}", n) for i in range(2)]
        mlist = range(16, 32) if mini else range(34)
        for i, m in enumerate(mlist):
            w = wb[i % 4]
            p.dma("qSP", w, w[0:128, 0:1024], self.wip,
                  self.wip.t[m * 128:(m + 1) * 128, :])
            bank = self.pb[4 + (i % 2)]
            M = 8 if m == 33 else 128
            for kk in range(8):
                p.mm(bank, bank[0:M, 0:n], w, w[0:128, 128 * kk:128 * kk + M],
                     h[kk], h[kk][0:128, 0:n], start=(kk == 0), stop=(kk == 7))
            if mini:
                ch = m - 16
                p.copy("DVE", self.xchalo, self.xchalo[0:128, 3 * ch:3 * ch + 3],
                       bank, bank[0:128, 0:n])
            elif m < 16:
                z = zt[i % 2]
                p.copy("DVE", z, z[0:128, 0:n], bank, bank[0:128, 0:n])
                p.dma("qPOOL", self.z_scr, self.z_scr.t[128 * m:128 * (m + 1),
                                                        c0:c0 + n],
                      z, z[0:128, 0:n])
            elif m < 32:
                ch = m - 16
                p.copy("DVE", self.xct[ch], self.xct[ch][0:128, 3:3 + n],
                       bank, bank[0:128, 0:n])
            elif m == 32:
                p.copy("DVE", self.chunk32, self.chunk32[0:128, c0:c0 + n],
                       bank, bank[0:128, 0:n])
            else:
                dtt = A.alloc("dtt", n)
                dtm_ = A.alloc("dtm", n)
                p.ts("DVE", dtt, dtt[0:8, 0:n], bank, bank[0:8, 0:n],
                     (self.dtb, self.dtb[0:8, 0:1]), None, ALU.add)
                p.ts("DVE", dtm_, dtm_[0:8, 0:n], dtt, dtt[0:8, 0:n],
                     0.0, None, ALU.max)
                p.act(dtt, dtt[0:8, 0:n], dtt, dtt[0:8, 0:n], AF.Abs)
                p.ts("DVE", dtt, dtt[0:8, 0:n], dtt, dtt[0:8, 0:n],
                     -1.0, None, ALU.mult)
                p.act(dtt, dtt[0:8, 0:n], dtt, dtt[0:8, 0:n], AF.Exp)
                p.act(dtt, dtt[0:8, 0:n], dtt, dtt[0:8, 0:n], AF.Ln,
                      bias=(self.kcol, self.kcol[0:8, 5:6]))
                p.tt("DVE", ALU.add, self.dt8, self.dt8[0:8, c0:c0 + n],
                     dtm_, dtm_[0:8, 0:n], dtt, dtt[0:8, 0:n])

    def mamba_a(self):
        p, A = self.p, self.A
        A.push()
        self.Lg = A.alloc("Lg", T + 1)
        self.chunk32 = A.alloc("chunk32", T)
        self.dt8 = A.alloc("dt8", T)
        self.state = [A.alloc(f"state{h}", HD) for h in range(NH)]
        self.xchalo = A.alloc("xchalo", 48)
        self.onesT = A.alloc("onesT", MTILE)
        p.memset("DVE", self.onesT, self.onesT[0:8, 0:MTILE], 1.0)
        for h in range(NH):
            p.memset("DVE", self.state[h], self.state[h][0:128, 0:HD], 0.0)
        A.push()
        self.xct = [A.alloc(f"xct{ch}", MTILE + 3) for ch in range(16)]
        A.push()
        hmini = self._ln_mod_x(self.xin, 0, 3, (0, 0))
        self._inproj(hmini, 3, 0, mini=True)
        A.pop()
        for ch in range(16):
            p.ts("DVE", self.xchalo, self.xchalo[0:128, 3 * ch:3 * ch + 3],
                 self.xchalo, self.xchalo[0:128, 3 * ch:3 * ch + 3],
                 (self.hgate, self.hgate[0:128, 0:1]), None, ALU.mult)
        for t in range(T // MTILE):
            self._mamba_tile(t)
        A.pop()
        self.dump("dt8", self.dt8, prow=8)
        self.dump("Lg", self.Lg, prow=8, cols=T + 1)
        self.dump("chunk32", self.chunk32)

    def _mamba_tile(self, t):
        p, A = self.p, self.A
        n, c0 = MTILE, t * MTILE
        A.push()
        h = self._ln_mod_x(self.xin, 3 + c0, n, (0, 0))
        if t == 0:
            self.dump("h0", h[0], cols=MTILE)
            self.dump("mur0", self.mur, prow=1, cols=MTILE)
            self.dump("rinv0", self.rinv, prow=1, cols=MTILE)
            if "bmu0" in self.dbg:
                tmp = A.alloc("bmudump", MTILE)
                p.copy("DVE", tmp, tmp[0:128, 0:MTILE], self.pb[2],
                       self.pb[2][0:128, 0:MTILE])
                self.dump("bmu0", tmp, cols=MTILE)
        for ch in range(16):
            p.copy("DVE", self.xct[ch], self.xct[ch][0:128, 0:3],
                   self.xchalo, self.xchalo[0:128, 3 * ch:3 * ch + 3])
        self._inproj(h, n, c0)
        for ch in range(16):
            p.copy("DVE", self.xchalo, self.xchalo[0:128, 3 * ch:3 * ch + 3],
                   self.xct[ch], self.xct[ch][0:128, n:n + 3])
        self.xcs = [A.alloc(f"xcs{ch}", n) for ch in range(16)]
        cw, cb = self.convw, self.convb
        for ch in range(16):
            acc = self.xcs[ch]
            p.ts("DVE", acc, acc[0:128, 0:n], self.xct[ch],
                 self.xct[ch][0:128, 0:n],
                 (cw, cw[0:128, 4 * ch:4 * ch + 1]),
                 (cb, cb[0:128, ch:ch + 1]), ALU.mult, ALU.add)
            for k in range(1, 4):
                p.stt("DVE", acc, acc[0:128, 0:n], self.xct[ch],
                      self.xct[ch][0:128, k:k + n],
                      (cw, cw[0:128, 4 * ch + k:4 * ch + k + 1]),
                      acc, acc[0:128, 0:n], ALU.mult, ALU.add)
            p.act(acc, acc[0:128, 0:n], acc, acc[0:128, 0:n], AF.Silu)
        p.ts("DVE", self.Lg, self.Lg[0:8, 1 + c0:1 + c0 + n], self.dt8,
             self.dt8[0:8, c0:c0 + n], (self.negA, self.negA[0:8, 0:1]),
             None, ALU.mult)
        p.tt("DVE", ALU.add, self.Lg, self.Lg[0:8, 1 + c0:2 + c0],
             self.Lg, self.Lg[0:8, 1 + c0:2 + c0],
             self.Lg, self.Lg[0:8, c0:c0 + 1])
        p.scan("DVE", self.Lg, self.Lg[0:8, 1 + c0:1 + c0 + n],
               self.onesT, self.onesT[0:8, 0:n],
               self.Lg, self.Lg[0:8, 1 + c0:1 + c0 + n])
        bpad = A.alloc("bpad", n)
        p.memset("DVE", bpad, bpad[64:128, 0:n], 0.0)
        p.copy("DVE", bpad, bpad[0:64, 0:n], self.chunk32,
               self.chunk32[64:128, c0:c0 + n])
        for cc in range(n // QCH):
            self._scan_chunk(c0 + cc * QCH, cc, bpad)
        A.pop()

    def _scan_chunk(self, tc0, cc, bpad):
        p, A = self.p, self.A
        pb = self.pb
        A.push()
        bT = pb[1]
        p.transpose(bT, bT[0:128, 0:128], self.Lg,
                    self.Lg.t[0:128, self.Lg.pcol + 1 + tc0:
                              self.Lg.pcol + 1 + tc0 + 128], self.ident)
        LgT = A.alloc("LgT", 8)
        p.copy("ACT", LgT, LgT[0:128, 0:8], bT, bT[0:128, 0:8])
        p.transpose(bT, bT[0:128, 128:256], self.chunk32,
                    self.chunk32[0:128, tc0:tc0 + 128], self.ident)
        CBtm = A.alloc("CBtm", 128)
        p.copy("ACT", CBtm, CBtm[0:128, 0:128], bT, bT[0:128, 128:256])
        eL8 = A.alloc("eL8", 128)
        p.ts("DVE", eL8, eL8[0:8, 0:128], self.Lg,
             self.Lg[0:8, 1 + tc0:1 + tc0 + 128],
             (self.Lg, self.Lg[0:8, tc0:tc0 + 1]), None, ALU.subtract)
        p.act(eL8, eL8[0:8, 0:128], eL8, eL8[0:8, 0:128], AF.Exp)
        bG = pb[0]
        p.mm(bG, bG[0:128, 0:128], bpad, bpad[0:128, 128 * cc:128 * cc + 128],
             self.chunk32, self.chunk32[0:128, tc0:tc0 + 128],
             start=True, stop=True)
        Gm = A.alloc("Gm", 128)
        p.tt("DVE", ALU.mult, Gm, Gm[0:128, 0:128], bG, bG[0:128, 0:128],
             self.tri, self.tri[0:128, 0:128])
        yct = [A.alloc(f"yct{ch}", 128) for ch in range(16)]
        Ewb = [A.alloc(f"Ew{i}", 128) for i in range(2)]
        Swb = [A.alloc(f"Sw{i}", 128) for i in range(2)]
        Ctb2 = [A.alloc(f"Ct{i}", 128) for i in range(2)]
        XTb = [A.alloc(f"XT{i}", 256) for i in range(2)]
        Bptmb = [A.alloc(f"Bptm{i}", 64) for i in range(2)]
        for hh in range(NH):
            par2 = hh % 2
            Ew, Sw, Ct, XT, Bptm = (Ewb[par2], Swb[par2], Ctb2[par2],
                                    XTb[par2], Bptmb[par2])
            bB, bB2 = (pb[2], pb[3]) if par2 == 0 else (pb[7], pb[0])
            self.bcast(bB, self.Lg, hh, 128, c0=1 + tc0)
            p.ts("DVE", Ew, Ew[0:128, 0:128], bB, bB[0:128, 0:128],
                 (LgT, LgT[0:128, hh:hh + 1]), 0.0, ALU.subtract, ALU.min)
            p.act(Ew, Ew[0:128, 0:128], Ew, Ew[0:128, 0:128], AF.Exp)
            p.tt("DVE", ALU.mult, Sw, Sw[0:128, 0:128], Gm, Gm[0:128, 0:128],
                 Ew, Ew[0:128, 0:128])
            self.bcast(bB2, eL8, hh, 128)
            p.tt("DVE", ALU.mult, Ct, Ct[0:128, 0:128], self.chunk32,
                 self.chunk32[0:128, tc0:tc0 + 128], bB2, bB2[0:128, 0:128])
            for j in range(2):
                src = self.xcs[2 * hh + j]
                p.transpose(bT, bT[0:128, 256 + 128 * j:256 + 128 * (j + 1)],
                            src, src[0:128, 128 * cc:128 * cc + 128],
                            self.ident)
                p.copy("ACT", XT, XT[0:128, 128 * j:128 * (j + 1)], bT,
                       bT[0:128, 256 + 128 * j:256 + 128 * (j + 1)])
            for j in range(2):
                bY = pb[4 + j]
                p.mm(bY, bY[0:128, 0:128], XT, XT[0:128, 128 * j:128 * (j + 1)],
                     Sw, Sw[0:128, 0:128], start=True, stop=False)
                st = self.state[hh]
                p.mm(bY, bY[0:128, 0:128], st, st[0:128, 128 * j:128 * (j + 1)],
                     Ct, Ct[0:128, 0:128], start=False, stop=True)
                ch = 2 * hh + j
                p.stt("DVE", yct[ch], yct[ch][0:128, 0:128], self.xcs[ch],
                      self.xcs[ch][0:128, 128 * cc:128 * cc + 128],
                      (self.Dp, self.Dp[0:128, hh:hh + 1]),
                      bY, bY[0:128, 0:128], ALU.mult, ALU.add)
            p.ts("DVE", Bptm, Bptm[0:128, 0:64], CBtm, CBtm[0:128, 64:128],
                 (Ew, Ew[0:128, 127:128]), None, ALU.mult)
            bS = pb[6]
            p.mm(bS, bS[0:64, 0:256], Bptm, Bptm[0:128, 0:64],
                 XT, XT[0:128, 0:256], start=True, stop=True)
            st = self.state[hh]
            p.stt("DVE", st, st[0:64, 0:256], st, st[0:64, 0:256],
                  (bB2, bB2[0:64, 127:128]), bS, bS[0:64, 0:256],
                  ALU.mult, ALU.add)
        ydst = self.y_scr.t.ap().rearrange("(ch p) t -> p ch t", p=128)
        ysrc = self.A.t[0:128, yct[0].pcol:yct[0].pcol + 16 * 128].rearrange(
            "p (ch t) -> p ch t", ch=16)
        ybuf = Buf("yblk", self.A.t, "sbuf", pcol=yct[0].pcol, ncol=16 * 128)
        for b in yct:
            _dmerge(ybuf.last_w, b.last_w)
        tag = p.dma("qPOOL", self.y_scr, ydst[:, :, tc0:tc0 + 128], ybuf, ysrc)
        for b in yct:
            _dmerge(b.readers, {tag[0]: tag[1]})
        A.pop()



    def e1_exchange(self):
        p, A = self.p, self.A
        for h in range(NH):
            p.dma("qPOOL", self.e1_in,
                  self.e1_in.t[h * NST:(h + 1) * NST, :],
                  self.state[h], self.state[h][0:64, 0:HD])
        A.push()
        pexp = A.alloc("pexp", HD)
        p.memset("DVE", pexp, pexp[0:8, 0:HD], 0.0)
        p.act(pexp, pexp[0:8, 0:1], self.Lg, self.Lg[0:8, T:T + 1], AF.Exp)
        p.dma("qPOOL", self.e1_in, self.e1_in.t[NH * NST:NH * NST + 8, :],
              pexp, pexp[0:8, 0:HD])
        p.collective("AllGather", ALU.bypass,
                     [[0, 1, 2, 3], [4, 5, 6, 7]], self.e1_in, self.e1_out)
        R = NH * NST + 8
        prow = A.alloc("prow", 8)
        peff = A.alloc("peff", 8)
        spe = A.alloc("spe", 2048)
        for j in range(3):
            p.dma("qSP", prow, prow[0:1, 0:8],
                  self.e1_out, self.e1_out.t[j * R + NH * NST:
                                             j * R + NH * NST + 8, 0:1],
                  allow_slow_non_contiguous=True)
            bk = self.pb[7]
            self.bcast(bk, prow, 0, 8)
            p.ts("DVE", peff, peff[0:128, 0:8], bk, bk[0:128, 0:8],
                 (self.e1g, self.e1g[0:128, j:j + 1]),
                 (self.e1g1m, self.e1g1m[0:128, j:j + 1]), ALU.mult, ALU.add)
            R_ = NH * NST + 8
            for h in range(NH):
                p.dma("qSP", spe, spe[0:64, 256 * h:256 * h + HD],
                      self.e1_out,
                      self.e1_out.t[j * R_ + h * NST:j * R_ + (h + 1) * NST, :])
            p.ts("DVE", spe, spe[0:64, 0:2048], spe, spe[0:64, 0:2048],
                 (self.e1g, self.e1g[0:64, j:j + 1]), None, ALU.mult)
            for h in range(NH):
                if j == 0:
                    p.copy("DVE", self.state[h], self.state[h][0:64, 0:HD],
                           spe, spe[0:64, 256 * h:256 * h + HD])
                else:
                    p.stt("DVE", self.state[h], self.state[h][0:64, 0:HD],
                          self.state[h], self.state[h][0:64, 0:HD],
                          (peff, peff[0:64, h:h + 1]),
                          spe, spe[0:64, 256 * h:256 * h + HD],
                          ALU.mult, ALU.add)
        A.pop()

    def mamba_b(self):
        for t in range(T // MTILE):
            self._mamba_b_tile(t)
        self.A.pop()

    def _mamba_b_tile(self, t):
        p, A = self.p, self.A
        n, c0 = MTILE, t * MTILE
        A.push()
        eL = A.alloc("eL", n)
        p.act(eL, eL[0:8, 0:n], self.Lg, self.Lg[0:8, 1 + c0:1 + c0 + n],
              AF.Exp)
        yt = [A.alloc(f"by{ch}", n) for ch in range(16)]
        zt = [A.alloc(f"bz{ch}", n) for ch in range(16)]
        for ch in range(16):
            p.dma("qACT", yt[ch], yt[ch][0:128, 0:n], self.y_scr,
                  self.y_scr.t[128 * ch:128 * (ch + 1), c0:c0 + n])
            p.dma("qSP", zt[ch], zt[ch][0:128, 0:n], self.z_scr,
                  self.z_scr.t[128 * ch:128 * (ch + 1), c0:c0 + n])
        Ctb = A.alloc("Ctb", 128)
        for cc in range(n // QCH):
            tc0 = c0 + cc * QCH
            for h in range(NH):
                bk = self.pb[2]
                self.bcast(bk, eL, h, QCH, c0=cc * QCH)
                p.tt("DVE", ALU.mult, Ctb, Ctb[0:128, 0:QCH], self.chunk32,
                     self.chunk32[0:128, tc0:tc0 + QCH], bk, bk[0:128, 0:QCH])
                for j in range(2):
                    bY = self.pb[4 + j]
                    st = self.state[h]
                    p.mm(bY, bY[0:128, 0:QCH], st,
                         st[0:128, 128 * j:128 * (j + 1)],
                         Ctb, Ctb[0:128, 0:QCH], start=True, stop=True)
                    ch = 2 * h + j
                    p.tt("DVE", ALU.add, yt[ch],
                         yt[ch][0:128, cc * QCH:cc * QCH + QCH],
                         yt[ch], yt[ch][0:128, cc * QCH:cc * QCH + QCH],
                         bY, bY[0:128, 0:QCH])
        x2b = [A.alloc("bx2a", n), A.alloc("bx2b", n)]
        bs = self.pb[0]
        for ch in range(16):
            x2t = x2b[ch % 2]
            p.act(zt[ch], zt[ch][0:128, 0:n], zt[ch], zt[ch][0:128, 0:n],
                  AF.Silu)
            p.tt("DVE", ALU.mult, yt[ch], yt[ch][0:128, 0:n],
                 yt[ch], yt[ch][0:128, 0:n], zt[ch], zt[ch][0:128, 0:n])
            p.act(x2t, x2t[0:128, 0:n], yt[ch], yt[ch][0:128, 0:n], AF.Square)
            p.mm(bs, bs[0:1, 0:n], self.onescol, self.onescol[:, 0:1],
                 x2t, x2t[0:128, 0:n], start=(ch == 0), stop=(ch == 15))
        p.ts("DVE", self.r0, self.r0[0:1, 0:n], bs, bs[0:1, 0:n],
             (self.krow, self.krow[0:1, 3:4]), None, ALU.mult)
        p.act(self.r0, self.r0[0:1, 0:n], self.r0, self.r0[0:1, 0:n], AF.Sqrt,
              bias=(self.krow, self.krow[0:1, 1:2]))
        p.recip(self.rinv, self.rinv[0:1, 0:n], self.r0, self.r0[0:1, 0:n])
        br = self.pb[1]
        self.bcast(br, self.rinv, 0, n)
        for ch in range(16):
            p.stt("DVE", yt[ch], yt[ch][0:128, 0:n], yt[ch],
                  yt[ch][0:128, 0:n], (self.nw, self.nw[0:128, ch:ch + 1]),
                  br, br[0:128, 0:n], ALU.mult, ALU.mult)
        wb = [A.alloc(f"wob{i}", 2048) for i in range(4)]
        xnew = [A.alloc(f"xn{j}", n) for j in range(8)]
        for m in range(8):
            w = wb[m % 4]
            p.dma("qSP", w, w[0:128, 0:2048], self.wop,
                  self.wop.t[m * 128:(m + 1) * 128, :])
            bank = self.pb[4 + (m % 2)]
            for kk in range(16):
                p.mm(bank, bank[0:128, 0:n], w,
                     w[0:128, 128 * kk:128 * (kk + 1)],
                     yt[kk], yt[kk][0:128, 0:n],
                     start=(kk == 0), stop=(kk == 15))
            xi = xnew[m]
            p.dma("qACT", xi, xi[0:128, 0:n], self.xin,
                  self.xin.t[128 * m:128 * (m + 1), 3 + c0:3 + c0 + n])
            p.stt("DVE", xi, xi[0:128, 0:n], bank, bank[0:128, 0:n],
                  (self.mod, self.mod[0:128, 16 + m:17 + m]),
                  xi, xi[0:128, 0:n], ALU.mult, ALU.add)
            p.dma("qPOOL", self.xmid, self.xmid.t[128 * m:128 * (m + 1),
                                                  c0:c0 + n],
                  xi, xi[0:128, 0:n])
        h2 = self._ln_mod_chunks(xnew, n, (8, 24))
        for j in range(8):
            p.dma("qPOOL", self.xffn, self.xffn.t[128 * j:128 * (j + 1),
                                                  c0:c0 + n],
                  h2[j], h2[j][0:128, 0:n])
            g0 = T - HALO
            lo = max(g0, c0)
            if lo < c0 + n:
                o = lo - c0
                p.dma("qPOOL", self.e2_in,
                      self.e2_in.t[128 * j:128 * (j + 1), lo - g0:c0 + n - g0],
                      h2[j], h2[j][0:128, o:n])
        A.pop()

    def _ln_mod_chunks(self, xb, n, modoff):
        p, A = self.p, self.A
        x2t = [A.alloc("lnx2a", n), A.alloc("lnx2b", n)]
        bs, bq = self.pb[0], self.pb[1]
        self.ln_stats(xb, [b[0:128, 0:n] for b in xb], n, bs, bq, x2t)
        self.ln_rows(bs, bq, n, self.rinv, self.mur, eps_idx=0, dinv_idx=2)
        bmu, br = self.pb[2], self.pb[3]
        self.bcast(bmu, self.mur, 0, n)
        self.bcast(br, self.rinv, 0, n)
        po, mo = modoff
        for j in range(8):
            p.tt("DVE", ALU.mult, xb[j], xb[j][0:128, 0:n],
                 xb[j], xb[j][0:128, 0:n], br, br[0:128, 0:n])
            p.tt("DVE", ALU.subtract, xb[j], xb[j][0:128, 0:n],
                 xb[j], xb[j][0:128, 0:n], bmu, bmu[0:128, 0:n])
            p.ts("DVE", xb[j], xb[j][0:128, 0:n], xb[j], xb[j][0:128, 0:n],
                 (self.modp1, self.modp1[0:128, po + j:po + j + 1]),
                 (self.mod, self.mod[0:128, mo + j:mo + j + 1]),
                 ALU.mult, ALU.add)
        return xb

    def ffn_layer(self, l):
        p, A = self.p, self.A
        xin_d = self.xffn if l == 0 else self.xffn_b
        ein = self.e2_in if l == 0 else self.e3_in
        eout = self.e2_out if l == 0 else self.e3_out
        p.collective("AllGather", ALU.bypass, [[0, 1, 2, 3], [4, 5, 6, 7]],
                     ein, eout)
        if "stop=cc" in self.dbg:
            return
        A.push()
        self.bandb = A.alloc("bandb", 1408)
        p.dma("qACT", self.bandb, self.bandb[0:128, 0:1408], self.band,
              self.band.t[:, :])
        if "stop=band" in self.dbg:
            return
        self.hbcol = A.alloc("hbcol", 8)
        self.wv = [A.alloc(f"wvf{kk}", 512) for kk in range(8)]
        kslot = [[A.alloc(f"k{sl}_{pr}", 512) for pr in range(4)]
                 for sl in range(2)]
        vslot = [[A.alloc(f"v{sl}_{tc}", 512) for tc in range(4)]
                 for sl in range(2)]
        khx = [A.alloc(f"khx{pr}", 128) for pr in range(4)]
        vhx = A.alloc("vhx", 512)
        self.x1carry = A.alloc("x1carry", 8)
        self.x1acol = A.alloc("x1acol", 8)
        A.push()
        hb = [A.alloc(f"hb{j}", HALO) for j in range(8)]
        tmph = A.alloc("tmph", HALO)
        for j in range(8):
            if f"stop=hbj{j}" in self.dbg:
                return
            for jb in range(4):
                p.dma("qSP", tmph, tmph[0:128, 0:HALO], eout,
                      eout.t[jb * D + 128 * j:jb * D + 128 * (j + 1), :])
                if jb == 0:
                    p.ts("DVE", hb[j], hb[j][0:128, 0:HALO], tmph,
                         tmph[0:128, 0:HALO],
                         (self.eselg, self.eselg[0:128, 0:1]), None, ALU.mult)
                else:
                    p.stt("DVE", hb[j], hb[j][0:128, 0:HALO], tmph,
                          tmph[0:128, 0:HALO],
                          (self.eselg, self.eselg[0:128, jb:jb + 1]),
                          hb[j], hb[j][0:128, 0:HALO], ALU.mult, ALU.add)
        x1a = A.alloc("x1a", 128)
        if "stop=hb" in self.dbg:
            A.pop()
            return
        if "skip=hkv" not in self.dbg:
            self._halo_kv(l, hb, kslot[1], vslot[1], khx, vhx)
        if "stop=hkv" in self.dbg:
            A.pop()
            return
        if "skip=hattn" not in self.dbg:
            self._halo_attn(l, hb, kslot[1], vslot[1], khx, vhx, x1a)
        if "stop=hattn" in self.dbg:
            A.pop()
            return
        for j in range(8):
            p.copy("DVE", self.hbcol, self.hbcol[0:128, j:j + 1], hb[j],
                   hb[j][0:128, HALO - 1:HALO])
        A.pop()
        nblk = 4
        for d in self.dbg:
            if d.startswith("nblk="):
                nblk = int(d.split("=")[1])
        for w in range(nblk):
            self._ffn_block(l, w, xin_d, kslot, vslot)
        A.pop()

    def _qkv_chunks(self, l, src_getter, n, which, dst, rope_c0):
        p, A = self.p, self.A
        wq = self.wqkv[l]
        scv = self.qsc[l] if which == 0 else self.ksc[l]
        tc = self.ropeqc if which == 0 else self.ropekc
        tsn = self.ropeqs if which == 0 else self.ropeks
        A.push()
        rc = A.alloc("rc", n)
        rs = A.alloc("rs", n)
        p.dma("qACT", rc, rc[0:128, 0:n], tc, tc.t[:, rope_c0:rope_c0 + n])
        p.dma("qACT", rs, rs[0:128, 0:n], tsn, tsn.t[:, rope_c0:rope_c0 + n])
        qs_lim = 99
        for d in self.dbg:
            if d.startswith("qkstop="):
                qs_lim = int(d.split("=")[1])
        wb = [A.alloc(f"wqk{i}", 1024) for i in range(4)]
        q2b = [A.alloc(f"q2_{i}", n) for i in range(2)]
        qrawb = [A.alloc(f"qraw{i}", n) for i in range(2)]
        s2b = [A.alloc(f"s2_{i}", n) for i in range(2)]
        t2b = [A.alloc(f"t2_{i}", n) for i in range(2)]
        for m4 in range(4):
            q2, qraw, s2, t2 = (q2b[m4 % 2], qrawb[m4 % 2], s2b[m4 % 2],
                                t2b[m4 % 2])
            m = m4 + (0 if which == 0 else 4)
            w = wb[m4 % 4]
            p.dma("qSP", w, w[0:128, 0:1024], wq,
                  wq.t[m * 128:(m + 1) * 128, :])
            for n0 in range(0, n, 512):
                nn = min(512, n - n0)
                bank = self.pb[0 + (m4 % 2)]
                for kk in range(8):
                    sb_, sap = src_getter(kk)
                    sap = sap.slice_free(n0, nn) if hasattr(sap, "slice_free")                         else sap
                    p.mm(bank, bank[0:128, 0:nn], w,
                         w[0:128, 128 * kk:128 * (kk + 1)], sb_,
                         src_getter(kk, n0, nn)[1]
                         if src_getter.__code__.co_argcount > 1 else sap,
                         start=(kk == 0), stop=(kk == 7))
                if qs_lim < 2:
                    continue
                p.act(q2, q2[0:128, n0:n0 + nn], bank, bank[0:128, 0:nn],
                      AF.Square)
                if qs_lim < 3:
                    continue
                p.copy("DVE", qraw, qraw[0:128, n0:n0 + nn], bank,
                       bank[0:128, 0:nn])
                if qs_lim < 4:
                    continue
                bss = self.pb[6 if m4 % 2 == 0 else 7]
                p.mm(bss, bss[0:2, 0:nn], self.hsel, self.hsel[0:128, 0:2],
                     q2, q2[0:128, n0:n0 + nn], start=True, stop=True)
                if qs_lim < 5:
                    continue
                p.act(s2, s2[0:2, n0:n0 + nn], bss, bss[0:2, 0:nn], AF.Sqrt)
                p.ts("DVE", s2, s2[0:2, n0:n0 + nn], s2, s2[0:2, n0:n0 + nn],
                     1e-12, None, ALU.max)
                p.recip(t2, t2[0:2, n0:n0 + nn], s2, s2[0:2, n0:n0 + nn])
                if qs_lim < 6:
                    continue
                brb = self.pb[4 if m4 % 2 == 0 else 5]
                p.mm(brb, brb[0:128, 0:nn], self.hsel2,
                     self.hsel2[0:128, 0:128], t2,
                     self.A.t[0:128, t2.pcol + n0:t2.pcol + n0 + nn],
                     start=True, stop=True)
                if qs_lim < 7:
                    continue
                p.stt("DVE", qraw, qraw[0:128, n0:n0 + nn], qraw,
                      qraw[0:128, n0:n0 + nn],
                      (scv, scv[0:128, m4:m4 + 1]), brb, brb[0:128, 0:nn],
                      ALU.mult, ALU.mult)
                if qs_lim < 8:
                    continue
                brt = self.pb[2 if m4 % 2 == 0 else 3]
                p.mm(brt, brt[0:128, 0:nn], self.rot, self.rot[0:128, 0:128],
                     qraw, qraw[0:128, n0:n0 + nn], start=True, stop=True)
                d = dst[m4]
                p.tt("DVE", ALU.mult, d, d[0:128, n0:n0 + nn], qraw,
                     qraw[0:128, n0:n0 + nn], rc, rc[0:128, n0:n0 + nn])
                p.tt("DVE", ALU.mult, t2, t2[0:128, n0:n0 + nn], brt,
                     brt[0:128, 0:nn], rs, rs[0:128, n0:n0 + nn])
                p.tt("DVE", ALU.add, d, d[0:128, n0:n0 + nn], d,
                     d[0:128, n0:n0 + nn], t2, t2[0:128, n0:n0 + nn])
        A.pop()

    def _halo_kv(self, l, hb, kslot1, vslot1, khx, vhx):
        p, A = self.p, self.A
        A.push()
        def src(kk, n0=0, nn=KHALO):
            off = 4 if kk < 4 else 3
            return (hb[kk], hb[kk][0:128, off + n0:off + n0 + nn])
        kd = [A.alloc(f"hkd{pr}", KHALO) for pr in range(4)]
        if "hkv=v" not in self.dbg:
            self._qkv_chunks(l, src, KHALO, 1, kd, 0)
            for pr in range(4):
                p.copy("DVE", khx[pr], khx[pr][0:128, 0:128], kd[pr],
                       kd[pr][0:128, 0:128])
                p.copy("DVE", kslot1[pr], kslot1[pr][0:128, 0:512], kd[pr],
                       kd[pr][0:128, 128:640])
        if "hkv=qk" in self.dbg:
            A.pop()
            return
        wq = self.wqkv[l]
        wb = [A.alloc(f"wv{i}", 1024) for i in range(2)]
        wvt = [A.alloc(f"wvt{i}", 512) for i in range(2)]
        for tc5 in range(5):
            bank = self.pb[0 + (tc5 % 2)]
            for kk in range(8):
                w = wb[kk % 2]
                if tc5 == 0:
                    for mi, m in enumerate(range(8, 12)):
                        p.dma("qSP", self.wv[kk], 
                              self.wv[kk][0:128, 128 * mi:128 * (mi + 1)],
                              wq, wq.t[m * 128:(m + 1) * 128,
                                       128 * kk:128 * (kk + 1)])
                off = 4 if kk < 4 else 3
                p.mm(bank, bank[0:128, 0:512], hb[kk],
                     hb[kk][0:128, off + 128 * tc5:off + 128 * tc5 + 128],
                     self.wv[kk], self.wv[kk][0:128, 0:512],
                     start=(kk == 0), stop=(kk == 7))
            dst = vhx if tc5 == 0 else vslot1[tc5 - 1]
            p.copy("DVE", dst, dst[0:128, 0:512], bank, bank[0:128, 0:512])
        A.pop()

    def _halo_attn(self, l, hb, kslot1, vslot1, khx, vhx, x1a):
        p, A = self.p, self.A
        A.push()
        def src(kk, n0=0, nn=128):
            off = 516 if kk < 4 else 515
            return (hb[kk], hb[kk][0:128, off + n0:off + n0 + nn])
        qd = [A.alloc(f"hq{pr}", 128) for pr in range(4)]
        self._qkv_chunks(l, src, 128, 0, qd, 0)
        hm = A.alloc("hm", KHALO)
        p.dma("qACT", hm, hm[0:128, 0:KHALO], self.hmask, self.hmask.t[:, :])
        ot = [A.alloc(f"ho{j}", 128) for j in range(4)]
        PT = [A.alloc(f"hPT{c}", 128) for c in range(5)]
        Sm = A.alloc("hSm", KHALO)
        rl = A.alloc("hrl", 2)
        for hh in range(NH):
            pr, par = hh // 2, hh % 2
            qp = self.qpadO if par else self.qpadE
            p.copy("DVE", qp, qp[64 * par:64 * par + 64, 0:128], qd[pr],
                   qd[pr][64 * par:64 * par + 64, 0:128])
            b0, b1 = self.pb[0], self.pb[1]
            p.mm(b0, b0[0:128, 0:128], qp, qp[0:128, 0:128],
                 khx[pr], khx[pr][0:128, 0:128], start=True, stop=True)
            p.mm(b1, b1[0:128, 0:512], qp, qp[0:128, 0:128],
                 kslot1[pr], kslot1[pr][0:128, 0:512], start=True, stop=True)
            p.tt("DVE", ALU.add, Sm, Sm[0:128, 0:128], b0, b0[0:128, 0:128],
                 hm, hm[0:128, 0:128])
            p.tt("DVE", ALU.add, Sm, Sm[0:128, 128:640], b1,
                 b1[0:128, 0:512], hm, hm[0:128, 128:640])
            p.reduce("DVE", rl, rl[0:128, 0:1], Sm, Sm[0:128, 0:KHALO],
                     op=ALU.max, negate=True)
            p.act(Sm, Sm[0:128, 0:KHALO], Sm, Sm[0:128, 0:KHALO], AF.Exp,
                  bias=(rl, rl[0:128, 0:1]), accum=(rl, rl[0:128, 1:2]))
            p.recip(rl, rl[0:128, 0:1], rl, rl[0:128, 1:2])
            p.act(Sm, Sm[0:128, 0:KHALO], Sm, Sm[0:128, 0:KHALO], AF.Copy,
                  scale=(rl, rl[0:128, 0:1]))
            for c in range(5):
                bt = self.pb[2 + (c % 2)]
                p.transpose(bt, bt[0:128, 0:128], Sm,
                            Sm[0:128, 128 * c:128 * (c + 1)], self.ident)
                p.copy("ACT", PT[c], PT[c][0:128, 0:128], bt,
                       bt[0:128, 0:128])
            bat = self.pb[4]
            for c in range(5):
                vsrc = vhx if c == 0 else vslot1[c - 1]
                p.mm(bat, bat[64 * par:64 * par + 64, 0:128], vsrc,
                     vsrc[0:128, 64 * hh:64 * hh + 64], PT[c],
                     PT[c][0:128, 0:128], start=(c == 0), stop=(c == 4))
            if par == 1:
                p.copy("DVE", ot[pr], ot[pr][0:128, 0:128], bat,
                       bat[0:128, 0:128])
        wb = [A.alloc(f"hwo{i}", 512) for i in range(2)]
        for m in range(8):
            w = wb[m % 2]
            p.dma("qSP", w, w[0:128, 0:512], self.wo[l],
                  self.wo[l].t[m * 128:(m + 1) * 128, :])
            bank = self.pb[5 + (m % 2)]
            for kk in range(4):
                p.mm(bank, bank[0:128, 0:128], w,
                     w[0:128, 128 * kk:128 * (kk + 1)],
                     ot[kk], ot[kk][0:128, 0:128], start=(kk == 0),
                     stop=(kk == 3))
            p.tt("DVE", ALU.add, x1a, x1a[0:128, 0:128], bank,
                 bank[0:128, 0:128], hb[m], hb[m][0:128, 516:644])
            p.ts("DVE", x1a, x1a[0:128, 0:128], x1a, x1a[0:128, 0:128],
                 (self.hgate, self.hgate[0:128, 0:1]), None, ALU.mult)
            p.copy("DVE", self.x1acol, self.x1acol[0:128, m:m + 1], x1a,
                   x1a[0:128, 127:128])
        A.pop()

    def _ffn_block(self, l, w, xin_d, kslot, vslot):
        p, A = self.p, self.A
        qb = 512 * w
        A.push()
        xt = [A.alloc(f"fx{j}", 513) for j in range(8)]
        for j in range(8):
            if w == 0:
                p.dma("qACT", xt[j], xt[j][0:128, 1:513], xin_d,
                      xin_d.t[128 * j:128 * (j + 1), 0:512])
                p.copy("DVE", xt[j], xt[j][0:128, 0:1], self.hbcol,
                       self.hbcol[0:128, j:j + 1])
            else:
                p.dma("qACT", xt[j], xt[j][0:128, 0:513], xin_d,
                      xin_d.t[128 * j:128 * (j + 1), qb - 1:qb + 512])
        def src(kk, n0=0, nn=512):
            off = 1 if kk < 4 else 0
            return (xt[kk], xt[kk][0:128, off + n0:off + n0 + nn])
        sl = w % 2
        A.push()
        qd = [A.alloc(f"fq{pr}", 512) for pr in range(4)]
        self._qkv_chunks(l, src, 512, 0, qd, 128 + qb)
        self._qkv_chunks(l, src, 512, 1, kslot[sl], KHALO + qb)
        for tc4 in range(4):
            bank = self.pb[0 + (tc4 % 2)]
            for kk in range(8):
                off = 1 if kk < 4 else 0
                p.mm(bank, bank[0:128, 0:512], xt[kk],
                     xt[kk][0:128, off + 128 * tc4:off + 128 * tc4 + 128],
                     self.wv[kk], self.wv[kk][0:128, 0:512],
                     start=(kk == 0), stop=(kk == 7))
            d = vslot[sl][tc4]
            p.copy("DVE", d, d[0:128, 0:512], bank, bank[0:128, 0:512])
        prv = 1 - sl
        ot = [A.alloc(f"fo{j}", 512) for j in range(4)]
        A.push()
        if w == 0:
            mkb = [A.alloc(f"mkb{st}", 1024) for st in range(4)]
            for st in range(4):
                p.dma("qACT", mkb[st], mkb[st][0:128, 0:1024], self.blk0,
                      self.blk0.t[128 * st:128 * (st + 1), :])
        else:
            mkb = None
        PT = [A.alloc(f"PT{c}", 512) for c in range(8)]
        Sm2 = [A.alloc(f"Sm{i}", 1024) for i in range(2)]
        rl2 = [A.alloc(f"rl{i}", 2) for i in range(2)]
        bandb = self.bandb
        for hh in range(NH):
            pr, par = hh // 2, hh % 2
            qp = self.qpadO if par else self.qpadE
            p.copy("DVE", qp, qp[64 * par:64 * par + 64, 0:512], qd[pr],
                   qd[pr][64 * par:64 * par + 64, 0:512])
            for st in range(4):
                Sm, rl = Sm2[st % 2], rl2[st % 2]
                b0, b1 = (self.pb[0], self.pb[1]) if st % 2 == 0 else \
                    (self.pb[5], self.pb[6])
                p.mm(b0, b0[0:128, 0:512], qp,
                     qp[0:128, 128 * st:128 * st + 128],
                     kslot[prv][pr], kslot[prv][pr][0:128, 0:512],
                     start=True, stop=True)
                p.mm(b1, b1[0:128, 0:512], qp,
                     qp[0:128, 128 * st:128 * st + 128],
                     kslot[sl][pr], kslot[sl][pr][0:128, 0:512],
                     start=True, stop=True)
                if w == 0:
                    p.tt("DVE", ALU.add, Sm, Sm[0:128, 0:512], b0,
                         b0[0:128, 0:512], mkb[st], mkb[st][0:128, 0:512])
                    p.tt("DVE", ALU.add, Sm, Sm[0:128, 512:1024], b1,
                         b1[0:128, 0:512], mkb[st], mkb[st][0:128, 512:1024])
                else:
                    c0b = 384 - st * 128
                    p.tt("DVE", ALU.add, Sm, Sm[0:128, 0:512], b0,
                         b0[0:128, 0:512], bandb, bandb[0:128, c0b:c0b + 512])
                    p.tt("DVE", ALU.add, Sm, Sm[0:128, 512:1024], b1,
                         b1[0:128, 0:512], bandb,
                         bandb[0:128, c0b + 512:c0b + 1024])
                p.reduce("DVE", rl, rl[0:128, 0:1], Sm, Sm[0:128, 0:1024],
                         op=ALU.max, negate=True)
                p.act(Sm, Sm[0:128, 0:1024], Sm, Sm[0:128, 0:1024], AF.Exp,
                      bias=(rl, rl[0:128, 0:1]), accum=(rl, rl[0:128, 1:2]))
                p.recip(rl, rl[0:128, 0:1], rl, rl[0:128, 1:2])
                p.act(Sm, Sm[0:128, 0:1024], Sm, Sm[0:128, 0:1024], AF.Copy,
                      scale=(rl, rl[0:128, 0:1]))
                for c in range(8):
                    bt = self.pb[2 + (c % 2)]
                    p.transpose(bt, bt[0:128, 0:128], Sm,
                                Sm[0:128, 128 * c:128 * (c + 1)], self.ident)
                    p.copy("ACT", PT[c], PT[c][0:128, 128 * st:128 * st + 128],
                           bt, bt[0:128, 0:128])
            bat = self.pb[4]
            for c in range(8):
                vt = vslot[prv][c] if c < 4 else vslot[sl][c - 4]
                p.mm(bat, bat[64 * par:64 * par + 64, 0:512], vt,
                     vt[0:128, 64 * hh:64 * hh + 64], PT[c],
                     PT[c][0:128, 0:512], start=(c == 0), stop=(c == 7))
            if par == 1:
                p.copy("DVE", ot[pr], ot[pr][0:128, 0:512], bat,
                       bat[0:128, 0:512])
        A.pop()
        wb = [A.alloc(f"fwo{i}", 512) for i in range(2)]
        for m in range(8):
            wv_ = wb[m % 2]
            p.dma("qSP", wv_, wv_[0:128, 0:512], self.wo[l],
                  self.wo[l].t[m * 128:(m + 1) * 128, :])
            bank = self.pb[5 + (m % 2)]
            for kk in range(4):
                p.mm(bank, bank[0:128, 0:512], wv_,
                     wv_[0:128, 128 * kk:128 * (kk + 1)],
                     ot[kk], ot[kk][0:128, 0:512],
                     start=(kk == 0), stop=(kk == 3))
            p.tt("DVE", ALU.add, xt[m], xt[m][0:128, 1:513], bank,
                 bank[0:128, 0:512], xt[m], xt[m][0:128, 1:513])
        A.pop()
        for m in range(8):
            if w == 0:
                p.copy("DVE", xt[m], xt[m][0:128, 0:1], self.x1acol,
                       self.x1acol[0:128, m:m + 1])
            else:
                p.copy("DVE", xt[m], xt[m][0:128, 0:1], self.x1carry,
                       self.x1carry[0:128, m:m + 1])
        for m in range(8):
            p.copy("DVE", self.x1carry, self.x1carry[0:128, m:m + 1],
                   xt[m], xt[m][0:128, 512:513])
        A.push()
        x2t = A.alloc("fx2", 512)
        hmid = [A.alloc(f"hm{j}", 512) for j in range(22)]
        A.push()
        bs, bq = self.pb[0], self.pb[1]
        caps = []
        for j in range(8):
            off = 1 if j < 4 else 0
            caps.append(xt[j][0:128, off:off + 512])
        x2s2 = A.alloc("fx2b", 512)
        self.ln_stats(xt, caps, 512, bs, bq, [x2t, x2s2])
        self.ln_rows(bs, bq, 512, self.rinv, self.mur, eps_idx=0, dinv_idx=2)
        bmu, br = self.pb[2], self.pb[3]
        self.bcast(bmu, self.mur, 0, 512)
        self.bcast(br, self.rinv, 0, 512)
        hn = [A.alloc(f"hn{j}", 512) for j in range(8)]
        for j in range(8):
            p.tt("DVE", ALU.mult, hn[j], hn[j][0:128, 0:512], xt[j], caps[j],
                 br, br[0:128, 0:512])
            p.tt("DVE", ALU.subtract, hn[j], hn[j][0:128, 0:512], hn[j],
                 hn[j][0:128, 0:512], bmu, bmu[0:128, 0:512])
            p.ts("DVE", hn[j], hn[j][0:128, 0:512], hn[j],
                 hn[j][0:128, 0:512],
                 (self.lg[l], self.lg[l][0:128, j:j + 1]),
                 (self.lbv[l], self.lbv[l][0:128, j:j + 1]),
                 ALU.mult, ALU.add)
        wb1 = [A.alloc(f"fw1_{i}", 1024) for i in range(4)]
        gel = A.alloc("gel", 512)
        for j in range(22):
            for half in range(2):
                m = 2 * j + half
                wv_ = wb1[m % 4]
                p.dma("qSP", wv_, wv_[0:128, 0:1024], self.w1[l],
                      self.w1[l].t[m * 128:(m + 1) * 128, :])
                bank = self.pb[4 + half]
                for kk in range(8):
                    p.mm(bank, bank[0:128, 0:512], wv_,
                         wv_[0:128, 128 * kk:128 * (kk + 1)], hn[kk],
                         hn[kk][0:128, 0:512], start=(kk == 0), stop=(kk == 7))
            nrow = 128 if j < 21 else HID - 128 * 21
            bA, bG = self.pb[4], self.pb[5]
            p.act(gel, gel[0:nrow, 0:512], bG, bG[0:nrow, 0:512], AF.Gelu,
                  bias=(self.b1[l], self.b1[l][0:nrow, 2 * j + 1:2 * j + 2]))
            p.ts("DVE", hmid[j], hmid[j][0:nrow, 0:512], bA, bA[0:nrow, 0:512],
                 (self.b1[l], self.b1[l][0:nrow, 2 * j:2 * j + 1]), None,
                 ALU.add)
            p.tt("DVE", ALU.mult, hmid[j], hmid[j][0:nrow, 0:512], hmid[j],
                 hmid[j][0:nrow, 0:512], gel, gel[0:nrow, 0:512])
            if nrow < 128:
                p.memset("DVE", hmid[j], hmid[j][64:128, 0:512], 0.0)
        A.pop()
        wb2 = [A.alloc(f"fw2_{i}", 2816) for i in range(2)]
        for m in range(8):
            wv_ = wb2[m % 2]
            p.dma("qSP", wv_, wv_[0:128, 0:2816], self.w2[l],
                  self.w2[l].t[m * 128:(m + 1) * 128, :])
            bank = self.pb[6 + (m % 2)]
            for kk in range(22):
                p.mm(bank, bank[0:128, 0:512], wv_,
                     wv_[0:128, 128 * kk:128 * (kk + 1)], hmid[kk],
                     hmid[kk][0:128, 0:512], start=(kk == 0), stop=(kk == 21))
            p.stt("DVE", x2t, x2t[0:128, 0:512], bank, bank[0:128, 0:512],
                  (self.b2[l], self.b2[l][0:128, m:m + 1]),
                  xt[m], xt[m][0:128, 1:513], ALU.add, ALU.add)
            if l == 0:
                p.dma("qPOOL", self.xffn_b,
                      self.xffn_b.t[128 * m:128 * (m + 1), qb:qb + 512],
                      x2t, x2t[0:128, 0:512])
                g0 = T - HALO
                lo = max(g0, qb)
                if lo < qb + 512:
                    o = lo - qb
                    p.dma("qPOOL", self.e3_in,
                          self.e3_in.t[128 * m:128 * (m + 1),
                                       lo - g0:qb + 512 - g0],
                          x2t, x2t[0:128, o:512])
            else:
                xm = A.alloc("xmt", 512) if m == 0 else xm
                p.dma("qACT", xm, xm[0:128, 0:512], self.xmid,
                      self.xmid.t[128 * m:128 * (m + 1), qb:qb + 512])
                p.stt("DVE", x2t, x2t[0:128, 0:512], x2t, x2t[0:128, 0:512],
                      (self.mod, self.mod[0:128, 40 + m:41 + m]),
                      xm, xm[0:128, 0:512], ALU.mult, ALU.add)
                p.dma("qPOOL", self.out_fm,
                      self.out_fm.t[128 * m:128 * (m + 1), qb:qb + 512],
                      x2t, x2t[0:128, 0:512])
        A.pop()
        A.pop()

    def finish(self):
        pass


def build(dbg=(), upto="mamba_a"):
    nc = bass.Bass("TRN2", target_bir_lowering=False, debug=False)
    p = Prog(nc)
    k = Kern(p, dbg)
    k.setup()
    k.mod_vector()
    stages = ["mamba_a", "e1", "mamba_b", "ffn1", "ffn2", "finish"]
    iu = stages.index(upto)
    k.mamba_a()
    if iu >= 1:
        k.e1_exchange()
    if iu >= 2:
        k.mamba_b()
    if iu >= 3:
        k.ffn_layer(0)
    if iu >= 4:
        k.ffn_layer(1)
    if iu >= 5:
        k.finish()
    p.wait_all("POOL", [s for s in p.cnt if s.startswith("q")] + ["cc"])
    p.finalize()
    return nc, p, k


def kernel(**inputs):
    maps = stage_inputs(inputs)
    nc, p, k = build(upto="finish")
    res = run_bass_kernel_spmd(nc, maps, core_ids=list(range(8)))
    out = np.empty((NB, 8192, D), np.float32)
    for c in range(8):
        b, q = c // 4, c % 4
        out[b, T * q:T * (q + 1), :] = np.asarray(res.results[c]["out_fm"]).T
    return out
